# version 07
# baseline (speedup 1.0000x reference)
import numpy as np
import ml_dtypes
import concourse.bass as bass
import concourse.mybir as mybir
from concourse.bass_utils import run_bass_kernel_spmd

F32 = mybir.dt.float32
BF16 = mybir.dt.bfloat16
I32 = mybir.dt.int32
AF = mybir.ActivationFunctionType
ALU = mybir.AluOpType
AXX = mybir.AxisListType.X

D = 2048
FC = 16
T = 1024
TC = 8
EXT = 16
KALL = EXT + T
NH = 16
NKV = 8
DFF = 5504
OCR = 43
SLC = 6
DS = SLC * 128
RG = [[0, 1, 2, 3, 4, 5, 6, 7]]
SCALE = float(np.float32(1.0 / np.sqrt(128.0)))
NEG = -1e30
BLKROWS = 3 * 2048 * SLC
G_OFF, U_OFF, D_OFF = 0, 12288, 24576


class Sem:
    def __init__(self, nc, name):
        self.h = nc.alloc_semaphore(name)
        self.n = 0

    def inc(self, instr, k=1):
        instr.then_inc(self.h, k)
        self.n += k
        return instr

    def dma(self, instr):
        return self.inc(instr, 16)

    def wait(self, eng, val=None):
        eng.wait_ge(self.h, self.n if val is None else val)


class Rot:
    _pool = {}

    def __new__(cls, nc, name, tiles):
        if name in cls._pool:
            obj = cls._pool[name]
            assert obj.pi == obj.ci, f"rot {name} rebound while in flight"
            obj.tiles = tiles
            return obj
        obj = super().__new__(cls)
        cls._pool[name] = obj
        obj.tiles = tiles
        obj.prod = Sem(nc, name + "_p")
        obj.free = Sem(nc, name + "_f")
        obj.pi = 0
        obj.ci = 0
        return obj

    def __init__(self, nc, name, tiles):
        pass

    def acquire(self, eng):
        if self.pi >= len(self.tiles):
            eng.wait_ge(self.free.h, self.pi - len(self.tiles) + 1)
        return self.tiles[self.pi % len(self.tiles)]

    def commit(self, instr):
        self.prod.inc(instr)
        self.pi += 1
        return instr

    def take(self, eng):
        eng.wait_ge(self.prod.h, self.ci + 1)
        t = self.tiles[self.ci % len(self.tiles)]
        self.ci += 1
        return t

    def release(self, instr):
        self.free.inc(instr)
        return instr



class SafeEng:

    _READS = {
        "tensor_copy": (1,), "tensor_tensor": (1, 2), "tensor_add": (1, 2),
        "tensor_sub": (1, 2), "tensor_mul": (1, 2), "tensor_max": (1, 2),
        "tensor_scalar": (1, 2, 3), "tensor_single_scalar": (1, 2),
        "tensor_scalar_add": (1, 2), "tensor_scalar_sub": (1, 2),
        "tensor_scalar_mul": (1, 2), "tensor_scalar_max": (1, 2),
        "tensor_scalar_min": (1, 2), "tensor_reduce": (1,),
        "reduce_max": (1,), "reduce_sum": (1,), "reciprocal": (1,),
        "scalar_tensor_tensor": (1, 2, 3), "select": (1, 2, 3),
        "activation": (1,), "copy": (1,), "mul": (1,), "add": (1,),
        "sign": (1,), "square": (1,), "sqrt": (1,),
    }

    DEPTH = 1

    def __init__(self, eng, dumfn):
        self._e = eng
        self._dumfn = dumfn
        self._w = [set() for _ in range(self.DEPTH)]
        self.n_dum = 0

    @staticmethod
    def _name(x):
        t = getattr(x, "tensor", None)
        return getattr(t, "name", None) if t is not None else None

    def _pre(self, reads):
        rd = {n for n in reads if n}
        guard = 0
        while rd & set().union(*self._w) and guard < 4:
            self._dumfn(self._e)
            self._w = self._w[1:] + [{"__dum__"}]
            self.n_dum += 1
            guard += 1

    def _post(self, writes):
        self._w = self._w[1:] + [{n for n in writes if n}]

    def __getattr__(self, name):
        eng = self._e
        if name in self._READS:
            ridx = self._READS[name]

            def call(*args, **kwargs):
                reads = [self._name(args[i]) for i in ridx if i < len(args)]
                for kw in ("bias", "scale", "scale2"):
                    if kw in kwargs:
                        reads.append(self._name(kwargs[kw]))
                for a in args[1:]:
                    n = self._name(a)
                    if n:
                        reads.append(n)
                writes = [self._name(args[0])]
                ao = kwargs.get("accum_out")
                if ao is not None:
                    writes.append(self._name(ao))
                self._pre(reads)
                ins = getattr(eng, name)(*args, **kwargs)
                self._post(writes)
                return ins
            return call
        if name == "memset":
            def call(*args, **kwargs):
                ins = eng.memset(*args, **kwargs)
                self._post([self._name(args[0])])
                return ins
            return call
        return getattr(eng, name)


class Plc:
    def __init__(self, nc, budget):
        self.nc = nc
        self.budget = budget
        self.items = []

    def add(self, name, shape, dtype, e0, e1):
        per = int(np.prod(shape[1:])) * mybir.dt.size(dtype)
        per = (per + 31) // 32 * 32
        self.items.append([name, shape, dtype, per, e0, e1])

    def _try(self, order):
        placed, offs = [], {}
        for it in order:
            name, shape, dtype, sz, e0, e1 = it
            for off in sorted({0} | {o + s for o, s, *_ in placed}):
                if off + sz > self.budget:
                    continue
                if all(not (off < o + s and o < off + sz and e0 < f1 and f0 < e1)
                       for o, s, f0, f1 in placed):
                    placed.append((off, sz, e0, e1))
                    offs[name] = off
                    break
            else:
                return None, None
        return placed, offs

    def solve(self):
        keys = [lambda x: (-(x[5] - x[4]), -x[3]),
                lambda x: (-x[3], -(x[5] - x[4])),
                lambda x: (-(x[5] - x[4]) * x[3],),
                lambda x: (x[4], -x[3]),
                lambda x: (-x[3] * (x[5] - x[4]), x[4])]
        placed = None
        for k in keys:
            placed, offs = self._try(sorted(self.items, key=k))
            if placed is not None:
                break
        if placed is None:
            raise RuntimeError("SBUF packing failed under all orderings")
        self.span = max(o + s for o, s, *_ in placed)
        base = (self.nc.sbuf_base + 31) // 32 * 32
        self.nc.sbuf_base = base + self.span
        assert self.nc.sbuf_base <= self.nc.sbuf_top - 256, "SBUF overflow"
        return {name: self.nc.alloc_sbuf_tensor_at(name, shape, dtype, offset=base + offs[name])
                for name, shape, dtype, sz, e0, e1 in self.items}


class _Cut(Exception):
    pass


def build_program(taps=()):
    import os as _os
    KSTOP = int(_os.environ.get("KSTOP", "99"))
    Rot._pool = {}
    nc = bass.Bass("TRN2", target_bir_lowering=False, detect_race_conditions=False)

    din = {}
    for name, shape in [
        ("x_main", [T, D]), ("x_ext", [EXT, D]), ("mask", [128, 8 * 144]),
        ("c_all", [4, D]), ("wmod_sl", [D, 1536]),
        ("wq", [D, D]), ("wk", [1024, D]), ("wv", [1024, D]), ("wo", [D, D]),
        ("wg_sl", [DS, D]), ("wu_sl", [DS, D]), ("wd_sl", [D, DS]),
        ("idf", [128, 128]),
    ]:
        din[name] = nc.dram_tensor(name, shape, F32, kind="ExternalInput")
    din["idb"] = nc.dram_tensor("idb", [128, 128], BF16, kind="ExternalInput")
    out_h = nc.dram_tensor("out", [T, D], F32, kind="ExternalOutput")
    tap_h = {tn: nc.dram_tensor("tap_" + tn, ts_, td_, kind="ExternalOutput")
             for tn, ts_, td_ in taps}

    wq_bf = nc.dram_tensor("wq_bf", [D, D], BF16, kind="Internal")
    wk_bf = nc.dram_tensor("wk_bf", [1024, D], BF16, kind="Internal")
    wv_bf = nc.dram_tensor("wv_bf", [1024, D], BF16, kind="Internal")
    wo_bf = nc.dram_tensor("wo_bf", [D, D], BF16, kind="Internal")
    wg_nat = nc.dram_tensor("wg_nat", [DS, D], BF16, kind="Internal")
    wu_nat = nc.dram_tensor("wu_nat", [DS, D], BF16, kind="Internal")
    wd_nat = nc.dram_tensor("wd_nat", [D, DS], BF16, kind="Internal")
    ag_in = nc.dram_tensor("ag_in", [BLKROWS, 128], BF16, kind="Internal")
    ag_out = nc.dram_tensor("ag_out", [8 * BLKROWS, 128], BF16, kind="Internal",
                            addr_space="Shared")
    mod_in = nc.dram_tensor("mod_in", [4, 1536], F32, kind="Internal")
    mod_out = nc.dram_tensor("mod_out", [32, 1536], F32, kind="Internal",
                             addr_space="Shared")
    hid_sp = nc.dram_tensor("hid_sp", [OCR * 128, 1024], F32, kind="Internal")

    P = Plc(nc, budget=229376 - 1024 - 16896)
    P.add("IDF", [128, 128], F32, 0, 10)
    P.add("IDB", [128, 128], BF16, 0, 10)
    P.add("ONES1", [1, 128], F32, 0, 10)
    P.add("ROWB", [1, 128], F32, 0, 10)
    P.add("MODS6", [16, 6 * 128], F32, 0, 10)
    P.add("MTMP", [16, 256], F32, 0, 10)
    P.add("CAF", [128, 16], F32, 0, 10)
    P.add("SAF", [128, 16], F32, 0, 10)
    P.add("GBF", [128, 16], F32, 0, 10)
    P.add("GAFM", [128, 16], F32, 0, 10)
    P.add("CBFM", [128, 16], F32, 0, 10)
    P.add("SBFM", [128, 16], F32, 0, 10)
    P.add("STAT", [128, 96], F32, 0, 10)
    P.add("X1", [128, TC * D], F32, 0, 2)
    P.add("X2", [128, TC * D], F32, 5, 10)
    P.add("SILC", [4, D], F32, 0, 1)
    P.add("SCF", [128, 64], F32, 0, 1)
    P.add("MODP", [4, 1536], F32, 0, 1)
    P.add("WMOD", [128, 8 * 1536], F32, 0, 1)
    P.add("XEXT", [16, D], F32, 0, 2)
    P.add("DUM", [128, D], F32, 1, 2)
    P.add("U1", [128, 2 * D], BF16, 1, 2)
    P.add("MASK", [128, 8 * 144], F32, 0, 5)
    P.add("H1", [128, FC * KALL], BF16, 1, 4)
    P.add("WST", [128, 16 * 512], BF16, 2, 6)
    P.add("KF", [128, NKV * KALL], BF16, 2, 5)
    P.add("VT", [128, NKV * TC * 128], BF16, 2, 5)
    P.add("VSM", [128, NKV * TC * 128], BF16, 2, 5)
    P.add("VFM", [128, KALL], BF16, 2, 3)
    P.add("VFMX", [128, 7 * 16], BF16, 2, 3)
    P.add("QF", [128, NH * T], BF16, 3, 5)
    P.add("F32Q", [128, D], F32, 1, 4)
    P.add("QBF", [128, 2 * D], BF16, 1, 4)
    P.add("QSC", [128, 64], F32, 1, 4)
    P.add("QS1", [128, D], BF16, 1, 4)
    P.add("QS2", [128, D], BF16, 1, 4)
    P.add("PIECE", [128, 2 * D], BF16, 1, 5)
    P.add("AVF", [128, NH * T], BF16, 4, 6)
    P.add("SSB", [128, 4 * 144], F32, 4, 5)
    P.add("PBF", [128, 2 * 144], BF16, 4, 5)
    P.add("PTAS", [128, 2 * 128], BF16, 4, 5)
    P.add("PTMS", [128, 2 * 128], BF16, 4, 5)
    P.add("TMPW", [128, 1024], F32, 5, 6)
    P.add("GAFULL", [128, D], F32, 5, 6)
    P.add("CBFULL", [128, D], F32, 6, 7)
    P.add("SBFULL", [128, D], F32, 6, 7)
    P.add("U2", [128, D], F32, 6, 7)
    P.add("XQ2I", [128, D], I32, 6, 7)
    P.add("XQ2S", [128, D], BF16, 6, 7)
    P.add("XQ2", [128, FC * T], BF16, 6, 8)
    P.add("BGT", [128, 2 * D], BF16, 7, 8)
    P.add("BUT", [128, 2 * D], BF16, 7, 8)
    P.add("GS", [128, 1024], F32, 7, 8)
    P.add("HIDS", [128, 2 * 1024], F32, 7, 8)
    P.add("RMAX", [128, 1024], F32, 7, 9)
    P.add("RMIN", [128, 1024], F32, 7, 9)
    P.add("S3FULL", [128, 1024], F32, 8, 10)
    P.add("XQ3", [128, OCR * 1024], BF16, 8, 10)
    P.add("XQ3I", [128, 1024], I32, 8, 10)
    P.add("BDT", [128, 6144], BF16, 9, 10)
    P.add("HIDR", [128, 2 * 1024], F32, 8, 10)
    P.add("YD", [128, 2 * 1024], F32, 9, 10)
    P.add("HDUM", [128, 4], F32, 0, 10)
    H = P.solve()
    HDUM = H["HDUM"]

    def _vdum(e):
        e.tensor_scalar_mul(HDUM[:, 0:1], HDUM[:, 0:1], 1.0)

    def _sdum(e):
        e.activation(HDUM[:, 2:3], HDUM[:, 2:3], AF.Copy)

    _WV = lambda e: SafeEng(e, _vdum)
    _WS = lambda e: SafeEng(e, _sdum)

    IDF, IDB, ONES1, ROWB, MODS6, MTMP = (H[k] for k in
        ["IDF", "IDB", "ONES1", "ROWB", "MODS6", "MTMP"])
    CAF, SAF, GBF, STAT = H["CAF"], H["SAF"], H["GBF"], H["STAT"]
    X1, X2, XEXT, MASK, H1, WST = H["X1"], H["X2"], H["XEXT"], H["MASK"], H["H1"], H["WST"]
    KF, VT, VSM, VFM, QF, AVF = (H[k] for k in
        ["KF", "VT", "VSM", "VFM", "QF", "AVF"])

    def st(base, i, n=128):
        return STAT[0:n, base + i: base + i + 1]
    SB_SUM, SB_SSQ, SB_M, SB_VAR, SB_R, SB_NMR = 0, 9, 18, 27, 36, 45
    SB_NM, SB_DEN, SB_S3, SB_S3I, SB_AMX = 54, 58, 62, 70, 86

    sems = {}

    def S(name):
        if name not in sems:
            sems[name] = Sem(nc, name)
        return sems[name]

    def _cut(level, block, waitsems):
        if KSTOP != level:
            return
        @block.sync
        def _(q):
            for sname in waitsems:
                S(sname).wait(q)
            for tc in range(TC):
                S("stout").dma(q.dma_start(out_h.ap()[tc * 128:(tc + 1) * 128, :],
                                           H["X1" if KSTOP <= 2 else "X1"][:, tc * D:(tc + 1) * D]
                                           if KSTOP <= 5 else H["X2"][:, tc * D:(tc + 1) * D]))
            S("stout").wait(q)
        raise _Cut()

    try:
      with nc.Block() as block:
        @block.sync
        def _(q):
            for tc in range(TC):
                S("x").dma(q.dma_start(X1[:, tc * D:(tc + 1) * D],
                                       din["x_main"].ap()[tc * 128:(tc + 1) * 128, :]))
            S("x").dma(q.dma_start(XEXT[:], din["x_ext"].ap()))
            S("misc").dma(q.dma_start(IDF[:], din["idf"].ap()))
            S("misc").dma(q.dma_start(IDB[:], din["idb"].ap()))
            S("misc").dma(q.dma_start(H["SILC"][:], din["c_all"].ap()))
            S("misc").dma(q.dma_start(MASK[:], din["mask"].ap()))
            for kc in range(8):
                S("wm1").dma(q.dma_start(H["WMOD"][:, kc * 1536:(kc + 1) * 1536],
                                         din["wmod_sl"].ap()[kc * 128:(kc + 1) * 128, :]))

        with (nc.psum_tensor("pmt0", [128, 16], F32) as pmt0,
              nc.psum_tensor("pmt1", [128, 16], F32) as pmt1,
              nc.psum_tensor("pmA", [4, 512], F32) as pmA,
              nc.psum_tensor("pmB", [4, 512], F32) as pmB,
              nc.psum_tensor("pmC", [4, 512], F32) as pmC):
            r_scf = Rot(nc, "rA", [pmt0, pmt1])

            @block.scalar
            def _(s):
                s = _WS(s)
                S("misc").wait(s)
                S("silc").inc(s.activation(H["SILC"][:], H["SILC"][:], AF.Silu))

            @block.tensor
            def _(t):
                S("silc").wait(t)
                for fc in range(FC):
                    pt = r_scf.acquire(t)
                    r_scf.commit(t.transpose(pt[:, 0:4],
                                             H["SILC"][0:4, fc * 128:(fc + 1) * 128],
                                             IDF[0:4, 0:4]))

            @block.vector
            def _(v):
                v = _WV(v)
                v.memset(ONES1[:], 1.0)
                for fc in range(FC):
                    pt = r_scf.take(v)
                    r_scf.release(v.tensor_copy(H["SCF"][:, fc * 4:fc * 4 + 4], pt[:, 0:4]))
                S("scf").inc(v.engine_nop())

            @block.tensor
            def _(t):
                S("scf").wait(t)
                S("wm1").wait(t)
                for ct, pm in enumerate([pmA, pmB, pmC]):
                    for kc in range(8):
                        ins = t.matmul(pm[:], H["SCF"][:, kc * 4:kc * 4 + 4],
                                       H["WMOD"][:, kc * 1536 + ct * 512: kc * 1536 + (ct + 1) * 512],
                                       start=(kc == 0), stop=False)
                S("mmh1").inc(ins)

            @block.sync
            def _(q):
                S("mmh1").wait(q)
                for kc in range(8):
                    S("wm2").dma(q.dma_start(H["WMOD"][:, kc * 1536:(kc + 1) * 1536],
                                             din["wmod_sl"].ap()[(8 + kc) * 128:(9 + kc) * 128, :]))

            @block.tensor
            def _(t):
                S("wm2").wait(t)
                for ct, pm in enumerate([pmA, pmB, pmC]):
                    for kc in range(8):
                        ins = t.matmul(pm[:], H["SCF"][:, (8 + kc) * 4:(8 + kc) * 4 + 4],
                                       H["WMOD"][:, kc * 1536 + ct * 512: kc * 1536 + (ct + 1) * 512],
                                       start=False, stop=(kc == 7))
                S("modmm").inc(ins)

            @block.vector
            def _(v):
                v = _WV(v)
                S("modmm").wait(v)
                for ct, pm in enumerate([pmA, pmB, pmC]):
                    v.tensor_copy(H["MODP"][:, ct * 512:(ct + 1) * 512], pm[:])
                v.drain()
                S("modp").inc(v.engine_nop())

            @block.sync
            def _(q):
                S("modp").wait(q)
                S("modin").dma(q.dma_start(mod_in.ap(), H["MODP"][:]))

            @block.gpsimd
            def _(g):
                S("cast").dma(g.dma_start(wk_bf.ap(), din["wk"].ap()))
                S("cast").dma(g.dma_start(wv_bf.ap(), din["wv"].ap()))
                S("cast").dma(g.dma_start(wq_bf.ap(), din["wq"].ap()))
                S("cast").dma(g.dma_start(wo_bf.ap(), din["wo"].ap()))
                S("modin").wait(g)
                S("agm").inc(g.collective_compute(
                    "AllGather", ALU.bypass, RG, ins=[mod_in.ap()], outs=[mod_out.ap()]))
                S("agm").wait(g)
                pid = g.partition_id()
                bsel = pid // 2
                for k in range(6):
                    col = k * 2048
                    while col < (k + 1) * 2048:
                        sl = col // 1536
                        end = min((sl + 1) * 1536, (k + 1) * 2048)
                        f0 = (col - k * 2048) // 128
                        nf = (end - col) // 128
                        srcp = mod_out.ap()[bass.ds(4 * sl + bsel, 1),
                                            col - 1536 * sl: end - 1536 * sl]
                        srcp = srcp.rearrange("o (f hh) -> (o f) hh", hh=128)
                        S("modsq").dma(g.dma_start(
                            MODS6[f0:f0 + nf, k * 128:(k + 1) * 128], srcp))
                        col = end

            @block.vector
            def _(v):
                v = _WV(v)
                S("modsq").wait(v)
                v.tensor_scalar_add(MODS6[:, 128:256], MODS6[:, 128:256], 1.0)
                ins = v.tensor_scalar_add(MODS6[:, 512:640], MODS6[:, 512:640], 1.0)
                S("modpre").inc(ins)

            @block.scalar
            def _(s):
                s = _WS(s)
                S("modpre").wait(s)
                s.activation(MTMP[:, 0:128], MODS6[:, 256:384], AF.Abs)
                S("modabs").inc(s.activation(MTMP[:, 128:256], MODS6[:, 640:768], AF.Abs))

            @block.vector
            def _(v):
                v = _WV(v)
                S("modabs").wait(v)
                for cc, tcol in [(256, 0), (640, 128)]:
                    blk = MODS6[:, cc:cc + 128]
                    v.tensor_scalar_max(MTMP[0:16, tcol:tcol + 128], MTMP[0:16, tcol:tcol + 128], 0.25)
                    v.tensor_scalar(blk, blk, 0.0, 2.0, ALU.is_ge, ALU.mult)
                    v.tensor_scalar_add(blk, blk, -1.0)
                    ins = v.tensor_mul(blk, blk, MTMP[0:16, tcol:tcol + 128])
                S("modclamp").inc(ins)

            @block.tensor
            def _(t):
                S("modclamp").wait(t)
                for kvec in [0, 1, 5, 2, 3, 4]:
                    pt = r_scf.acquire(t)
                    r_scf.commit(t.transpose(pt[:], MODS6[:, kvec * 128:(kvec + 1) * 128],
                                             IDF[0:16, 0:16]))

            @block.vector
            def _(v):
                v = _WV(v)
                for dst in [SAF, CAF, GBF, H["GAFM"], H["SBFM"], H["CBFM"]]:
                    pt = r_scf.take(v)
                    r_scf.release(v.tensor_copy(dst[:], pt[:, 0:16]))
                S("modfm").inc(v.engine_nop())

        _cut(1, block, ["modfm"])

        def csrc(i):
            return X1[:, i * D:(i + 1) * D] if i < TC else XEXT[:]

        def cn(i):
            return 128 if i < TC else EXT

        with (nc.psum_tensor("ph0", [128, 128], BF16) as ph0,
              nc.psum_tensor("ph1", [128, 128], BF16) as ph1,
              nc.psum_tensor("ph2", [128, 128], BF16) as ph2,
              nc.psum_tensor("ph3", [128, 128], BF16) as ph3):
            r_h1 = Rot(nc, "rB", [ph0, ph1, ph2, ph3])

            @block.scalar
            def _(s):
                s = _WS(s)
                S("x").wait(s)
                S("modin").wait(s)
                for i in range(9):
                    n = cn(i)
                    S("sq").inc(s.activation(H["DUM"][0:n, :], csrc(i)[0:n, :], AF.Square,
                                             accum_out=st(SB_SSQ, i, n)))

            @block.vector
            def _(v):
                v = _WV(v)
                S("x").wait(v)
                for i in range(9):
                    n = cn(i)
                    v.wait_ge(S("sq").h, i + 1)
                    v.reduce_sum(st(SB_SUM, i, n), csrc(i)[0:n, :], axis=AXX)
                    v.tensor_scalar_mul(st(SB_M, i, n), st(SB_SUM, i, n), 1.0 / D)
                    v.tensor_scalar_mul(st(SB_VAR, i, n), st(SB_SSQ, i, n), 1.0 / D)
                    v.tensor_mul(st(SB_SUM, i, n), st(SB_M, i, n), st(SB_M, i, n))
                    v.tensor_sub(st(SB_VAR, i, n), st(SB_VAR, i, n), st(SB_SUM, i, n))
                    S("var").inc(v.tensor_scalar_add(st(SB_VAR, i, n), st(SB_VAR, i, n), 1e-5))

            @block.scalar
            def _(s):
                s = _WS(s)
                for i in range(9):
                    n = cn(i)
                    s.wait_ge(S("var").h, i + 1)
                    S("sdev").inc(s.activation(st(SB_VAR, i, n), st(SB_VAR, i, n), AF.Sqrt))

            h1t0 = r_h1.prod.n

            @block.vector
            def _(v):
                v = _WV(v)
                for i in range(9):
                    n = cn(i)
                    v.wait_ge(S("sdev").h, i + 1)
                    v.reciprocal(st(SB_R, i, n), st(SB_VAR, i, n))
                    v.tensor_mul(st(SB_NMR, i, n), st(SB_M, i, n), st(SB_R, i, n))
                    v.tensor_scalar_mul(st(SB_NMR, i, n), st(SB_NMR, i, n), -1.0)
                    if i >= 2:
                        v.wait_ge(r_h1.prod.h, h1t0 + (i - 1) * FC)
                    ub = H["U1"][:, (i % 2) * D:(i % 2 + 1) * D]
                    S("u1").inc(v.tensor_scalar(ub[0:n, :], csrc(i)[0:n, :],
                                                st(SB_R, i, n), st(SB_NMR, i, n),
                                                ALU.mult, ALU.add))

            @block.tensor
            def _(t):
                for i in range(9):
                    n = cn(i)
                    t.wait_ge(S("u1").h, i + 1)
                    ub = H["U1"][:, (i % 2) * D:(i % 2 + 1) * D]
                    for fc in range(FC):
                        pt = r_h1.acquire(t)
                        ins = t.transpose(pt[:, 0:n], ub[0:n, fc * 128:(fc + 1) * 128],
                                          IDB[0:n, 0:n])
                        r_h1.commit(ins)

            @block.scalar
            def _(s):
                s = _WS(s)
                S("modfm").wait(s)
                for i in range(9):
                    n = cn(i)
                    base = 16 + i * 128 if i < TC else 0
                    for fc in range(FC):
                        pt = r_h1.take(s)
                        ins = s.activation(H1[:, fc * KALL + base: fc * KALL + base + n],
                                           pt[:, 0:n], AF.Identity,
                                           bias=SAF[:, fc:fc + 1], scale=CAF[:, fc:fc + 1])
                        r_h1.release(ins)
                S("h1done").inc(s.nop())

        _cut(2, block, ["h1done"])

        qjobs = []
        for hsrc, hnat, nrc, ncol in [(din["wg_sl"], wg_nat, 6, D),
                                      (din["wu_sl"], wu_nat, 6, D),
                                      (din["wd_sl"], wd_nat, 16, DS)]:
            for rc in range(nrc):
                qjobs.append((hsrc, hnat, rc, ncol))
        NQJ = len(qjobs)

        QUANT_ON = KSTOP >= 7

        @block.gpsimd
        def _(g):
            if not QUANT_ON:
                return
            hsrc0, hnat0, rc0, ncol0 = qjobs[0]
            S("qload").dma(g.dma_start(H["F32Q"][:, 0:ncol0],
                                       hsrc0.ap()[rc0 * 128:(rc0 + 1) * 128, :]))
            for j, (hsrc, hnat, rc, ncol) in enumerate(qjobs):
                par = j % 2
                g.wait_ge(S("vpn").h, j + 1)
                g.tensor_tensor(H["QS1"][:, 0:ncol], H["QS1"][:, 0:ncol],
                                H["QS2"][:, 0:ncol], ALU.subtract)
                S("gptern").inc(g.engine_nop())
                if j + 1 < NQJ:
                    hs2, hn2, rc2, nc2 = qjobs[j + 1]
                    S("qload").dma(g.dma_start(H["F32Q"][:, 0:nc2],
                                               hs2.ap()[rc2 * 128:(rc2 + 1) * 128, :]))
                g.wait_ge(S("swq").h, j + 1)
                S("qwr").dma(g.dma_start(hnat.ap()[rc * 128:(rc + 1) * 128, :],
                                         H["QBF"][:, par * D: par * D + ncol]))

        def emit_vq(lo, hi):
            if not QUANT_ON:
                return
            @block.vector
            def _(v):
                v = _WV(v)
                for j in range(lo, hi):
                    hsrc, hnat, rc, ncol = qjobs[j]
                    ng = ncol // 128
                    v.wait_ge(S("qload").h, (j + 1) * 16)
                    if j >= 1:
                        v.wait_ge(S("swq").h, j)
                    scb = (j % 2) * 16
                    for gc in range(ng):
                        wsl = H["F32Q"][:, gc * 128:(gc + 1) * 128]
                        sc = H["QSC"][:, scb + gc:scb + gc + 1]
                        hs = H["QSC"][:, 32:33]
                        nhs = H["QSC"][:, 33:34]
                        v.reduce_sum(sc, wsl, axis=AXX, apply_absolute_value=True)
                        v.tensor_scalar(sc, sc, 1.0 / 128.0, 1e-5, ALU.mult, ALU.add)
                        v.tensor_scalar_mul(hs, sc, 0.5)
                        v.tensor_scalar_mul(nhs, sc, -0.5)
                        v.tensor_scalar(H["QS1"][:, gc * 128:(gc + 1) * 128], wsl, hs,
                                        None, ALU.is_gt)
                        v.tensor_scalar(H["QS2"][:, gc * 128:(gc + 1) * 128], wsl, nhs,
                                        None, ALU.is_lt)
                    S("vpn").inc(v.engine_nop())

        def emit_swq(lo, hi):
            if not QUANT_ON:
                return
            @block.scalar
            def _(s):
                s = _WS(s)
                for j in range(lo, hi):
                    hsrc, hnat, rc, ncol = qjobs[j]
                    ng = ncol // 128
                    par = j % 2
                    s.wait_ge(S("gptern").h, j + 1)
                    if j >= 2:
                        s.wait_ge(S("qwr").h, (j - 1) * 16)
                    scb = (j % 2) * 16
                    for gc in range(ng):
                        s.activation(H["QBF"][:, par * D + gc * 128: par * D + (gc + 1) * 128],
                                     H["QS1"][:, gc * 128:(gc + 1) * 128], AF.Copy,
                                     scale=H["QSC"][:, scb + gc:scb + gc + 1])
                    s.drain()
                    S("swq").inc(s.nop())

        emit_vq(0, 9)
        emit_swq(0, 8)

        with (nc.psum_tensor("pkv", [128, KALL], F32) as pkv,
              nc.psum_tensor("ptva", [128, 128], BF16) as ptva,
              nc.psum_tensor("ptvb", [128, 128], BF16) as ptvb,
              nc.psum_tensor("ptvs0", [16, 128], BF16) as ptvs0,
              nc.psum_tensor("ptvs1", [16, 128], BF16) as ptvs1):
            r_vm = Rot(nc, "rC", [ptva, ptvb])
            r_vs = Rot(nc, "rD", [ptvs0, ptvs1])

            @block.sync
            def _(q):
                q.wait_ge(S("u1").h, 9)
                q.wait_ge(S("cast").h, 32)
                for half in range(2):
                    if half > 0:
                        q.wait_ge(S("vmm").h, 4)
                    for fc in range(FC):
                        S("wstk").dma(q.dma_start(
                            WST[:, fc * 512:(fc + 1) * 512],
                            wk_bf.ap()[half * 512:(half + 1) * 512, fc * 128:(fc + 1) * 128],
                            transpose=True))
                    q.wait_ge(S("kmm").h, 4 * (half + 1))
                    for fc in range(FC):
                        S("wstv").dma(q.dma_start(
                            WST[:, fc * 512:(fc + 1) * 512],
                            wv_bf.ap()[half * 512:(half + 1) * 512, fc * 128:(fc + 1) * 128],
                            transpose=True))

            @block.tensor
            def _(t):
                S("h1done").wait(t)
                for half in range(2):
                    t.wait_ge(S("wstk").h, (half + 1) * 16 * 16)
                    for gl in range(4):
                        g8 = half * 4 + gl
                        if g8 > 0:
                            t.wait_ge(S("kev").h, g8)
                        for fc in range(FC):
                            lhs = WST[:, fc * 512 + gl * 128: fc * 512 + (gl + 1) * 128]
                            st_ = (fc == 0)
                            t.matmul(pkv[:, 0:512], lhs, H1[:, fc * KALL: fc * KALL + 512],
                                     start=st_, stop=(fc == FC - 1))
                            t.matmul(pkv[:, 512:1024], lhs,
                                     H1[:, fc * KALL + 512: fc * KALL + 1024],
                                     start=st_, stop=(fc == FC - 1))
                            ins = t.matmul(pkv[:, 1024:1040], lhs,
                                           H1[:, fc * KALL + 1024: fc * KALL + 1040],
                                           start=st_, stop=(fc == FC - 1))
                        S("kmm").inc(ins)
                    t.wait_ge(S("wstv").h, (half + 1) * 16 * 16)
                    for gl in range(4):
                        g8 = half * 4 + gl
                        t.wait_ge(S("kev").h, 4 * (half + 1))
                        if g8 > 0:
                            t.wait_ge(S("vfm").h, g8)
                        for fc in range(FC):
                            lhs = WST[:, fc * 512 + gl * 128: fc * 512 + (gl + 1) * 128]
                            st_ = (fc == 0)
                            t.matmul(pkv[:, 0:512], lhs, H1[:, fc * KALL: fc * KALL + 512],
                                     start=st_, stop=(fc == FC - 1))
                            t.matmul(pkv[:, 512:1024], lhs,
                                     H1[:, fc * KALL + 512: fc * KALL + 1024],
                                     start=st_, stop=(fc == FC - 1))
                            ins = t.matmul(pkv[:, 1024:1040], lhs,
                                           H1[:, fc * KALL + 1024: fc * KALL + 1040],
                                           start=st_, stop=(fc == FC - 1))
                        S("vmm").inc(ins)
                        t.wait_ge(S("vfm").h, g8 + 1)
                        pa = r_vs.acquire(t)
                        r_vs.commit(t.transpose(pa[:], VFM[:, 0:16], IDB[:]))
                        t.wait_ge(S("vfmx").h, g8 + 1)
                        for k in range(TC):
                            pm = r_vm.acquire(t)
                            r_vm.commit(t.transpose(pm[:], VFM[:, 16 + k * 128:16 + (k + 1) * 128], IDB[:]))
                            if k < TC - 1:
                                ps = r_vs.acquire(t)
                                r_vs.commit(t.transpose(
                                    ps[:], H["VFMX"][:, k * 16:(k + 1) * 16], IDB[:]))

            @block.scalar
            def _(s):
                s = _WS(s)
                for half in range(2):
                    for gl in range(4):
                        g8 = half * 4 + gl
                        s.wait_ge(S("kmm").h, g8 + 1)
                        S("kev").inc(s.activation(KF[:, g8 * KALL:(g8 + 1) * KALL], pkv[:],
                                                  AF.Copy, scale=SCALE))
                    for gl in range(4):
                        g8 = half * 4 + gl
                        s.wait_ge(S("vmm").h, g8 + 1)
                        S("vfm").inc(s.activation(VFM[:], pkv[:], AF.Copy))

            emit_swq(8, 18)

            @block.vector
            def _(v):
                v = _WV(v)
                S("h1done").wait(v)
                v.memset(VSM[:], 0.0)
                for g8 in range(NKV):
                    v.wait_ge(S("vfm").h, g8 + 1)
                    for k in range(TC - 1):
                        v.tensor_copy(H["VFMX"][:, k * 16:k * 16 + 8], VFM[:, 0:8])
                        v.tensor_copy(H["VFMX"][:, k * 16 + 8:(k + 1) * 16],
                                      VFM[:, 16 + k * 128 + 120:16 + (k + 1) * 128])
                    S("vfmx").inc(v.engine_nop())
                    pa = r_vs.take(v)
                    r_vs.release(v.tensor_copy(VSM[0:16, g8 * TC * 128:(g8 * TC + 1) * 128], pa[:]))
                    for k in range(TC):
                        pm = r_vm.take(v)
                        r_vm.release(v.tensor_copy(
                            VT[:, (g8 * TC + k) * 128:(g8 * TC + k + 1) * 128], pm[:]))
                        if k < TC - 1:
                            ps = r_vs.take(v)
                            r_vs.release(v.tensor_copy(
                                VSM[0:16, (g8 * TC + k + 1) * 128:(g8 * TC + k + 2) * 128], ps[:]))
                    S("vev").inc(v.engine_nop())

            emit_vq(9, NQJ)

        _cut(3, block, ["vev"])

        with (nc.psum_tensor("pq0", [128, 1024], F32) as pq0,
              nc.psum_tensor("pq1", [128, 1024], F32) as pq1):
            r_q = Rot(nc, "rE", [pq0, pq1])
            qmm0 = r_q.prod.n

            @block.sync
            def _(q):
                q.wait_ge(S("cast").h, 48)
                q.wait_ge(S("vmm").h, 8)
                for sw in range(8):
                    if sw > 0:
                        q.wait_ge(r_q.prod.h, qmm0 + 2 * sw)
                    for fc in range(FC):
                        S("wstq").dma(q.dma_start(
                            WST[:, fc * 256:(fc + 1) * 256],
                            wq_bf.ap()[sw * 256:(sw + 1) * 256, fc * 128:(fc + 1) * 128],
                            transpose=True))

            @block.tensor
            def _(t):
                t.wait_ge(S("vev").h, NKV)
                for sw in range(8):
                    t.wait_ge(S("wstq").h, (sw + 1) * 16 * 16)
                    for h2 in range(2):
                        h = sw * 2 + h2
                        pq = r_q.acquire(t)
                        for fc in range(FC):
                            lhs = WST[:, fc * 256 + h2 * 128: fc * 256 + (h2 + 1) * 128]
                            st_ = (fc == 0)
                            t.matmul(pq[:, 0:512], lhs, H1[:, fc * KALL + 16: fc * KALL + 528],
                                     start=st_, stop=(fc == FC - 1))
                            ins = t.matmul(pq[:, 512:1024], lhs,
                                           H1[:, fc * KALL + 528: fc * KALL + 1040],
                                           start=st_, stop=(fc == FC - 1))
                        r_q.commit(ins)

            @block.scalar
            def _(s):
                s = _WS(s)
                for h in range(NH):
                    pq = r_q.take(s)
                    ins = s.activation(QF[:, h * T:(h + 1) * T], pq[:], AF.Copy)
                    r_q.release(ins)
                S("qall").inc(s.nop())

            emit_swq(18, NQJ)

        _cut(4, block, ["qall"])

        rjobs = [("gu", wg_nat, ocl, 0) for ocl in range(SLC)] + \
                [("gu", wu_nat, ocl, 1) for ocl in range(SLC)] + \
                [("d", wd_nat, oc16, 2) for oc16 in range(16)]
        qwr_need = [6 * 16, 12 * 16, 28 * 16]

        @block.sync
        def _(q):
            if not QUANT_ON:
                return
            cum = 0
            for j, (kind, hnat, idx, wi) in enumerate(rjobs):
                par = j % 2
                if j >= 2:
                    q.wait_ge(S("rord").h, (j - 1) * 16)
                q.wait_ge(S("qwr").h, qwr_need[wi])
                npc = 16 if kind == "gu" else SLC
                for pc in range(npc):
                    S("tpiece").dma(q.dma_start(
                        H["PIECE"][:, par * D + pc * 128: par * D + (pc + 1) * 128],
                        hnat.ap()[idx * 128:(idx + 1) * 128, pc * 128:(pc + 1) * 128],
                        transpose=True))
                cum += npc

        @block.scalar
        def _(s):
            s = _WS(s)
            if not QUANT_ON:
                return
            cum = 0
            for j, (kind, hnat, idx, wi) in enumerate(rjobs):
                par = j % 2
                npc = 16 if kind == "gu" else SLC
                cum += npc
                s.wait_ge(S("tpiece").h, cum * 16)
                if kind == "gu":
                    base = (G_OFF if wi == 0 else U_OFF) + idx * 2048
                    dst = ag_in.ap()[base:base + 2048, :].rearrange("(p a) j -> p (a j)", p=128)
                    src = H["PIECE"][:, par * D: par * D + 2048]
                else:
                    base = D_OFF + idx * DS
                    dst = ag_in.ap()[base:base + DS, :].rearrange("(p a) j -> p (a j)", p=128)
                    src = H["PIECE"][:, par * D: par * D + DS]
                S("rord").dma(s.dma_start(dst, src))

        @block.gpsimd
        def _(g):
            if not QUANT_ON:
                return
            S("rord").wait(g)
            S("agf").inc(g.collective_compute(
                "AllGather", ALU.bypass, RG, ins=[ag_in.ap()], outs=[ag_out.ap()]))

        tiles = [(h, tt) for h in range(NH) for tt in range(TC)]
        NTI = len(tiles)

        with (nc.psum_tensor("psc0", [128, 144], F32) as psc0,
              nc.psum_tensor("psc1", [128, 144], F32) as psc1,
              nc.psum_tensor("ptab0", [16, 128], BF16) as ptab0,
              nc.psum_tensor("ptab1", [16, 128], BF16) as ptab1,
              nc.psum_tensor("ptm0", [128, 128], BF16) as ptm0,
              nc.psum_tensor("ptm1", [128, 128], BF16) as ptm1,
              nc.psum_tensor("pav0", [128, 128], F32) as pav0,
              nc.psum_tensor("pav1", [128, 128], F32) as pav1):
            r_s = Rot(nc, "rF", [psc0, psc1])
            r_ab = Rot(nc, "rA", [ptab0, ptab1])
            r_m = Rot(nc, "rB", [ptm0, ptm1])
            r_av = Rot(nc, "rC", [pav0, pav1])
            av0 = r_av.prod.n

            def emit_pt(t, i):
                par = i % 2
                t.wait_ge(S("pbf").h, i + 1)
                pab = r_ab.acquire(t)
                r_ab.commit(t.transpose(pab[:], H["PBF"][:, par * 144: par * 144 + 16], IDB[:]))
                pm = r_m.acquire(t)
                r_m.commit(t.transpose(pm[:], H["PBF"][:, par * 144 + 16: par * 144 + 144], IDB[:]))

            def emit_av(t, i):
                h, tt = tiles[i]
                g8 = h // 2
                par = i % 2
                t.wait_ge(S("ptev").h, i + 1)
                pav = r_av.acquire(t)
                t.matmul(pav[:], VT[:, (g8 * TC + tt) * 128:(g8 * TC + tt + 1) * 128],
                         H["PTMS"][:, par * 128:(par + 1) * 128], start=True, stop=False)
                ins = t.matmul(pav[:], VSM[:, (g8 * TC + tt) * 128:(g8 * TC + tt + 1) * 128],
                               H["PTAS"][:, par * 128:(par + 1) * 128], start=False, stop=True)
                r_av.commit(ins)

            @block.tensor
            def _(t):
                S("qall").wait(t)
                for i in range(NTI):
                    h, tt = tiles[i]
                    g8 = h // 2
                    ps = r_s.acquire(t)
                    t.matmul(ps[:, 8:144], QF[:, h * T + tt * 128: h * T + (tt + 1) * 128],
                             KF[:, g8 * KALL + 8 + 128 * tt: g8 * KALL + 144 + 128 * tt],
                             start=True, stop=True)
                    ins = t.matmul(ps[:, 0:8], QF[:, h * T + tt * 128: h * T + (tt + 1) * 128],
                                   KF[:, g8 * KALL: g8 * KALL + 8], start=True, stop=True)
                    r_s.commit(ins)
                    if i >= 2:
                        emit_pt(t, i - 2)
                    if i >= 3:
                        emit_av(t, i - 3)
                for i in range(NTI - 2, NTI):
                    emit_pt(t, i)
                for i in range(NTI - 3, NTI):
                    emit_av(t, i)

            def v_pre(v, i):
                h, tt = tiles[i]
                par = i % 2
                if i >= 2:
                    v.wait_ge(S("sexp").h, i - 1)
                ps = r_s.take(v)
                v.tensor_tensor(H["SSB"][:, par * 144:(par + 1) * 144], ps[:],
                                MASK[:, tt * 144:(tt + 1) * 144], ALU.add)
                r_s.release(
                    v.reduce_max(st(SB_NM, par), H["SSB"][:, par * 144:(par + 1) * 144],
                                 axis=AXX, negate=True))
                S("smax").inc(v.engine_nop())

            def v_post(v, i):
                par = i % 2
                v.wait_ge(S("sexp").h, i + 1)
                if i >= 2:
                    v.wait_ge(S("ptev").h, i - 1)
                v.reciprocal(st(SB_DEN, 2 + par), st(SB_DEN, par))
                S("pbf").inc(v.tensor_scalar(H["PBF"][:, par * 144:(par + 1) * 144],
                                             H["SSB"][:, (2 + par) * 144:(3 + par) * 144],
                                             st(SB_DEN, 2 + par), None, ALU.mult))

            def v_ptev(v, i):
                par = i % 2
                if i >= 2:
                    v.wait_ge(r_av.prod.h, av0 + i - 1)
                pab = r_ab.take(v)
                r_ab.release(v.tensor_copy(H["PTAS"][0:16, par * 128:(par + 1) * 128], pab[:]))
                pm = r_m.take(v)
                r_m.release(v.tensor_copy(H["PTMS"][:, par * 128:(par + 1) * 128], pm[:]))
                S("ptev").inc(v.engine_nop())

            def v_avev(v, i):
                h, tt = tiles[i]
                pav = r_av.take(v)
                r_av.release(v.tensor_copy(
                    AVF[:, h * T + tt * 128: h * T + (tt + 1) * 128], pav[:]))
                S("avev").inc(v.engine_nop())

            @block.vector
            def _(v):
                v = _WV(v)
                S("rord").wait(v)
                v.memset(H["PTAS"][:], 0.0)
                for i in range(NTI):
                    v_pre(v, i)
                    if i >= 1:
                        v_post(v, i - 1)
                    if i >= 2:
                        v_ptev(v, i - 2)
                    if i >= 3:
                        v_avev(v, i - 3)
                v_post(v, NTI - 1)
                for i in range(NTI - 2, NTI):
                    v_ptev(v, i)
                for i in range(NTI - 3, NTI):
                    v_avev(v, i)

            @block.scalar
            def _(s):
                s = _WS(s)
                for i in range(NTI):
                    par = i % 2
                    s.wait_ge(S("smax").h, i + 1)
                    if i >= 2:
                        s.wait_ge(S("pbf").h, i - 1)
                    S("sexp").inc(s.activation(H["SSB"][:, (2 + par) * 144:(3 + par) * 144],
                                               H["SSB"][:, par * 144:(par + 1) * 144], AF.Exp,
                                               bias=st(SB_NM, par), accum_out=st(SB_DEN, par)))

        _cut(5, block, ["avev"])

        with (nc.psum_tensor("pxo0", [128, 512], F32) as pxo0,
              nc.psum_tensor("pxo1", [128, 512], F32) as pxo1,
              nc.psum_tensor("pga", [128, 128], F32) as pga,
              nc.psum_tensor("prow", [1, 128], F32) as prow):
            r_xo = Rot(nc, "rD", [pxo0, pxo1])
            womm0 = r_xo.prod.n

            @block.sync
            def _(q):
                S("avev").wait(q)
                q.wait_ge(S("cast").h, 64)
                for tc in range(TC):
                    S("xrl").dma(q.dma_start(X2[:, tc * D:(tc + 1) * D],
                                             din["x_main"].ap()[tc * 128:(tc + 1) * 128, :]))
                for q4 in range(4):
                    if q4 > 0:
                        q.wait_ge(r_xo.prod.h, womm0 + q4 * TC)
                    for fc in range(FC):
                        S("wsto").dma(q.dma_start(
                            WST[:, fc * 512:(fc + 1) * 512],
                            wo_bf.ap()[q4 * 512:(q4 + 1) * 512, fc * 128:(fc + 1) * 128],
                            transpose=True))

            @block.tensor
            def _(t):
                S("avev").wait(t)
                for fc in range(FC):
                    if fc > 0:
                        t.wait_ge(S("garow").h, fc)
                    t.transpose(prow[:], H["GAFM"][:, fc:fc + 1], IDF[:]).then_inc(S("gaT").h, 1)
                    S("gaT").n += 1
                    t.wait_ge(S("garowe").h, fc + 1)
                    ins = t.matmul(pga[:], ONES1[:], ROWB[:], start=True, stop=True)
                    S("gamm").inc(ins)

            @block.vector
            def _(v):
                v = _WV(v)
                for fc in range(FC):
                    v.wait_ge(S("gaT").h, fc + 1)
                    ins = v.tensor_copy(ROWB[:], prow[:])
                    S("garowe").inc(ins)
                    v.wait_ge(S("gamm").h, fc + 1)
                    ins = v.tensor_copy(H["GAFULL"][:, fc * 128:(fc + 1) * 128], pga[:])
                    S("garow").inc(ins)
                S("gafull").inc(v.engine_nop())

            @block.tensor
            def _(t):
                for q4 in range(4):
                    t.wait_ge(S("wsto").h, (q4 + 1) * 16 * 16)
                    for tc in range(TC):
                        pxo = r_xo.acquire(t)
                        for fc in range(FC):
                            ins = t.matmul(pxo[:], AVF[:, fc * T + tc * 128: fc * T + (tc + 1) * 128],
                                           WST[:, fc * 512:(fc + 1) * 512],
                                           start=(fc == 0), stop=(fc == FC - 1))
                        r_xo.commit(ins)

            @block.vector
            def _(v):
                v = _WV(v)
                S("gafull").wait(v)
                S("xrl").wait(v)
                for q4 in range(4):
                    for tc in range(TC):
                        pxo = r_xo.take(v)
                        dst = X2[:, tc * D + q4 * 512: tc * D + (q4 + 1) * 512]
                        v.tensor_tensor(H["TMPW"][:, 0:512], pxo[:],
                                        H["GAFULL"][:, q4 * 512:(q4 + 1) * 512], ALU.mult)
                        r_xo.release(v.tensor_add(dst, dst, H["TMPW"][:, 0:512]))
                S("x1done").inc(v.engine_nop())

        _cut(6, block, ["x1done"])

        with (nc.psum_tensor("pbc", [128, 128], F32) as pbc,
              nc.psum_tensor("prow2", [1, 128], F32) as prow2,
              nc.psum_tensor("px0", [128, 128], BF16) as px0,
              nc.psum_tensor("px1", [128, 128], BF16) as px1,
              nc.psum_tensor("px2", [128, 128], BF16) as px2,
              nc.psum_tensor("px3", [128, 128], BF16) as px3):
            r_x2 = Rot(nc, "rE", [px0, px1, px2, px3])
            xq2t0 = r_x2.prod.n

            @block.tensor
            def _(t):
                S("x1done").wait(t)
                for k2, fm in enumerate([H["SBFM"], H["CBFM"]]):
                    for fc in range(FC):
                        i = k2 * FC + fc
                        if i > 0:
                            t.wait_ge(S("bcrow").h, i)
                        t.transpose(prow2[:], fm[:, fc:fc + 1], IDF[:]).then_inc(S("bcT").h, 1)
                        S("bcT").n += 1
                        t.wait_ge(S("bcrowe").h, i + 1)
                        ins = t.matmul(pbc[:], ONES1[:], ROWB[:], start=True, stop=True)
                        S("bcmm").inc(ins)

            @block.vector
            def _(v):
                v = _WV(v)
                S("x1done").wait(v)
                for k2, dstn in enumerate(["SBFULL", "CBFULL"]):
                    for fc in range(FC):
                        i = k2 * FC + fc
                        v.wait_ge(S("bcT").h, i + 1)
                        ins = v.tensor_copy(ROWB[:], prow2[:])
                        S("bcrowe").inc(ins)
                        v.wait_ge(S("bcmm").h, i + 1)
                        ins = v.tensor_copy(H[dstn][:, fc * 128:(fc + 1) * 128], pbc[:])
                        S("bcrow").inc(ins)
                S("bcfull").inc(v.engine_nop())

            @block.scalar
            def _(s):
                s = _WS(s)
                S("x1done").wait(s)
                for i in range(TC):
                    S("sq2").inc(s.activation(H["U2"][:], X2[:, i * D:(i + 1) * D], AF.Square,
                                              accum_out=st(SB_SSQ, i)))

            @block.tensor
            def _(t):
                for i in range(TC):
                    t.wait_ge(S("xq2s").h, i + 1)
                    for fc in range(FC):
                        px = r_x2.acquire(t)
                        ins = t.transpose(px[:], H["XQ2S"][:, fc * 128:(fc + 1) * 128], IDB[:])
                        r_x2.commit(ins)

            @block.scalar
            def _(s):
                s = _WS(s)
                for i in range(TC):
                    s.wait_ge(S("var2").h, i + 1)
                    S("sdev2").inc(s.activation(st(SB_VAR, i), st(SB_VAR, i), AF.Sqrt))

            @block.vector
            def _(v):
                v = _WV(v)
                S("bcfull").wait(v)

                def evac_chunk(i):
                    for fc in range(FC):
                        px = r_x2.take(v)
                        r_x2.release(v.tensor_copy(
                            H["XQ2"][:, fc * T + i * 128: fc * T + (i + 1) * 128], px[:]))

                for i in range(TC):
                    v.wait_ge(S("sq2").h, i + 1)
                    v.reduce_sum(st(SB_SUM, i), X2[:, i * D:(i + 1) * D], axis=AXX)
                    v.tensor_scalar_mul(st(SB_M, i), st(SB_SUM, i), 1.0 / D)
                    v.tensor_scalar_mul(st(SB_VAR, i), st(SB_SSQ, i), 1.0 / D)
                    v.tensor_mul(st(SB_SUM, i), st(SB_M, i), st(SB_M, i))
                    v.tensor_sub(st(SB_VAR, i), st(SB_VAR, i), st(SB_SUM, i))
                    S("var2").inc(v.tensor_scalar_add(st(SB_VAR, i), st(SB_VAR, i), 1e-5))
                    v.wait_ge(S("sdev2").h, i + 1)
                    v.reciprocal(st(SB_R, i), st(SB_VAR, i))
                    v.tensor_mul(st(SB_NMR, i), st(SB_M, i), st(SB_R, i))
                    v.tensor_scalar_mul(st(SB_NMR, i), st(SB_NMR, i), -1.0)
                    v.tensor_scalar(H["U2"][:], X2[:, i * D:(i + 1) * D],
                                    st(SB_R, i), st(SB_NMR, i), ALU.mult, ALU.add)
                    v.tensor_mul(H["U2"][:], H["U2"][:], H["CBFULL"][:])
                    v.tensor_add(H["U2"][:], H["U2"][:], H["SBFULL"][:])
                    v.reduce_max(st(SB_AMX, 0), H["U2"][:], axis=AXX, apply_absolute_value=True)
                    v.tensor_scalar_max(st(SB_AMX, 0), st(SB_AMX, 0), 1e-5)
                    v.reciprocal(st(SB_AMX, 1), st(SB_AMX, 0))
                    v.tensor_scalar_mul(st(SB_AMX, 1), st(SB_AMX, 1), 127.0)
                    v.tensor_scalar_mul(st(SB_AMX, 2), st(SB_AMX, 0), 1.0 / 127.0)
                    if i >= 1:
                        evac_chunk(i - 1)
                        v.wait_ge(r_x2.prod.h, xq2t0 + i * FC)
                    v.tensor_scalar(H["XQ2I"][:], H["U2"][:], st(SB_AMX, 1), None, ALU.mult)
                    S("xq2s").inc(v.tensor_scalar(H["XQ2S"][:], H["XQ2I"][:], st(SB_AMX, 2),
                                                  None, ALU.mult))
                evac_chunk(TC - 1)
                S("xq2done").inc(v.engine_nop())



        _cut(7, block, ["xq2done"])

        with (nc.psum_tensor("pg0", [128, 1024], F32) as pg0,
              nc.psum_tensor("pg1", [128, 1024], F32) as pg1,
              nc.psum_tensor("pu0", [128, 1024], F32) as pu0,
              nc.psum_tensor("pu1", [128, 1024], F32) as pu1):
            r_g = Rot(nc, "rF", [pg0, pg1])
            r_u = Rot(nc, "rA", [pu0, pu1])
            gsil0 = r_g.free.n

            @block.sync
            def _(q):
                S("agf").wait(q)
                S("xq2done").wait(q)
                for oc in range(OCR):
                    par = oc % 2
                    c, ocl = oc // SLC, oc % SLC
                    if oc >= 2:
                        q.wait_ge(S("guev").h, oc - 1)
                    base = c * BLKROWS + G_OFF + ocl * 2048
                    S("bg").dma(q.dma_start(
                        H["BGT"][:, par * D:(par + 1) * D],
                        ag_out.ap()[base:base + 2048, :].rearrange("(p a) j -> p (a j)", p=128)))
                    base = c * BLKROWS + U_OFF + ocl * 2048
                    S("bu").dma(q.dma_start(
                        H["BUT"][:, par * D:(par + 1) * D],
                        ag_out.ap()[base:base + 2048, :].rearrange("(p a) j -> p (a j)", p=128)))

            @block.vector
            def _(v):
                v = _WV(v)
                S("xq2done").wait(v)
                v.memset(H["RMAX"][:], 0.0)
                v.memset(H["RMIN"][:], 0.0)
                S("rmax0").inc(v.engine_nop())

            @block.tensor
            def _(t):
                S("rmax0").wait(t)
                for oc in range(OCR):
                    par = oc % 2
                    t.wait_ge(S("bg").h, (oc + 1) * 16)
                    t.wait_ge(S("bu").h, (oc + 1) * 16)
                    pg = r_g.acquire(t)
                    for fc in range(FC):
                        lhs = H["BGT"][:, par * D + fc * 128: par * D + (fc + 1) * 128]
                        st_ = (fc == 0)
                        t.matmul(pg[:, 0:512], lhs, H["XQ2"][:, fc * T: fc * T + 512],
                                 start=st_, stop=(fc == FC - 1))
                        ins = t.matmul(pg[:, 512:1024], lhs,
                                       H["XQ2"][:, fc * T + 512: fc * T + 1024],
                                       start=st_, stop=(fc == FC - 1))
                    r_g.commit(ins)
                    pu = r_u.acquire(t)
                    for fc in range(FC):
                        lhs = H["BUT"][:, par * D + fc * 128: par * D + (fc + 1) * 128]
                        st_ = (fc == 0)
                        t.matmul(pu[:, 0:512], lhs, H["XQ2"][:, fc * T: fc * T + 512],
                                 start=st_, stop=(fc == FC - 1))
                        ins = t.matmul(pu[:, 512:1024], lhs,
                                       H["XQ2"][:, fc * T + 512: fc * T + 1024],
                                       start=st_, stop=(fc == FC - 1))
                    r_u.commit(ins)

            @block.scalar
            def _(s):
                s = _WS(s)
                for oc in range(OCR):
                    pg = r_g.take(s)
                    ins = s.activation(H["GS"][:], pg[:], AF.Silu)
                    r_g.release(ins)
                    if oc >= 2:
                        par2 = (oc - 2) % 2
                        s.wait_ge(S("guev").h, oc - 1)
                        S("spill").dma(s.dma_start(
                            hid_sp.ap()[(oc - 2) * 128:(oc - 1) * 128, :],
                            H["HIDS"][:, par2 * 1024:(par2 + 1) * 1024]))
                for oc in range(OCR - 2, OCR):
                    par2 = oc % 2
                    s.wait_ge(S("guev").h, oc + 1)
                    S("spill").dma(s.dma_start(hid_sp.ap()[oc * 128:(oc + 1) * 128, :],
                                               H["HIDS"][:, par2 * 1024:(par2 + 1) * 1024]))

            @block.vector
            def _(v):
                v = _WV(v)
                for oc in range(OCR):
                    par = oc % 2
                    v.wait_ge(r_g.free.h, gsil0 + oc + 1)
                    if oc >= 2:
                        v.wait_ge(S("spill").h, (oc - 1) * 16)
                    pu = r_u.take(v)
                    hb = H["HIDS"][:, par * 1024:(par + 1) * 1024]
                    r_u.release(v.tensor_tensor(hb, H["GS"][:], pu[:], ALU.mult))
                    v.tensor_tensor(H["RMAX"][:], H["RMAX"][:], hb, ALU.max)
                    v.tensor_tensor(H["RMIN"][:], H["RMIN"][:], hb, ALU.min)
                    v.drain()
                    S("guev").inc(v.engine_nop())

        _cut(8, block, ["guev", "spill"])

        with (nc.psum_tensor("pr0", [128, 128], F32) as pr0,
              nc.psum_tensor("pr1", [128, 128], F32) as pr1,
              nc.psum_tensor("prow3", [1, 128], F32) as prow3,
              nc.psum_tensor("ps3b", [128, 128], F32) as ps3b):
            r_r = Rot(nc, "rB", [pr0, pr1])

            @block.vector
            def _(v):
                v = _WV(v)
                S("guev").wait(v)
                v.tensor_scalar_mul(H["RMIN"][:], H["RMIN"][:], -1.0)
                ins = v.tensor_max(H["RMAX"][:], H["RMAX"][:], H["RMIN"][:])
                S("rmfin").inc(ins)

            @block.tensor
            def _(t):
                S("rmfin").wait(t)
                for tc in range(TC):
                    pr = r_r.acquire(t)
                    r_r.commit(t.transpose(pr[:], H["RMAX"][:, tc * 128:(tc + 1) * 128], IDF[:]))

            @block.vector
            def _(v):
                v = _WV(v)
                S("spill").wait(v)
                for tc in range(TC):
                    pr = r_r.take(v)
                    v.reduce_max(st(SB_S3, tc), pr[:], axis=AXX)
                    r_r.release(v.tensor_scalar_max(st(SB_S3, tc), st(SB_S3, tc), 1e-5))
                    v.reciprocal(st(SB_S3, tc), st(SB_S3, tc))
                    v.tensor_scalar_mul(st(SB_S3, tc), st(SB_S3, tc), 127.0)
                    ins = v.reciprocal(st(SB_S3I, tc), st(SB_S3, tc))
                S("s3").inc(ins)

            @block.tensor
            def _(t):
                S("s3").wait(t)
                for tc in range(TC):
                    if tc > 0:
                        t.wait_ge(S("s3row").h, tc)
                    t.transpose(prow3[:], st(SB_S3, tc), IDF[:]).then_inc(S("s3T").h, 1)
                    S("s3T").n += 1
                    t.wait_ge(S("s3rowe").h, tc + 1)
                    ins = t.matmul(ps3b[:], ONES1[:], ROWB[:], start=True, stop=True)
                    S("s3mm").inc(ins)

            @block.vector
            def _(v):
                v = _WV(v)
                for tc in range(TC):
                    v.wait_ge(S("s3T").h, tc + 1)
                    ins = v.tensor_copy(ROWB[:], prow3[:])
                    S("s3rowe").inc(ins)
                    v.wait_ge(S("s3mm").h, tc + 1)
                    ins = v.tensor_copy(H["S3FULL"][:, tc * 128:(tc + 1) * 128], ps3b[:])
                    S("s3row").inc(ins)
                S("s3f").inc(v.engine_nop())

            @block.sync
            def _(q):
                S("spill").wait(q)
                for oc in range(OCR):
                    par = oc % 2
                    if oc >= 2:
                        q.wait_ge(S("xq3").h, oc - 1)
                    S("hrl").dma(q.dma_start(H["HIDR"][:, par * 1024:(par + 1) * 1024],
                                             hid_sp.ap()[oc * 128:(oc + 1) * 128, :]))

            @block.vector
            def _(v):
                v = _WV(v)
                S("s3f").wait(v)
                for oc in range(OCR):
                    par = oc % 2
                    v.wait_ge(S("hrl").h, (oc + 1) * 16)
                    v.tensor_tensor(H["XQ3I"][:], H["HIDR"][:, par * 1024:(par + 1) * 1024],
                                    H["S3FULL"][:], ALU.mult)
                    S("xq3").inc(v.tensor_copy(H["XQ3"][:, oc * 1024:(oc + 1) * 1024], H["XQ3I"][:]))

        _cut(9, block, ["xq3"])

        with (nc.psum_tensor("pd0", [128, 1024], F32) as pd0,
              nc.psum_tensor("pd1", [128, 1024], F32) as pd1,
              nc.psum_tensor("py0", [128, 128], F32) as py0,
              nc.psum_tensor("py1", [128, 128], F32) as py1):
            r_d = Rot(nc, "rC", [pd0, pd1])
            r_y = Rot(nc, "rD", [py0, py1])
            dmm0 = r_d.prod.n
            yd0 = r_d.free.n
            yfree0 = r_y.free.n

            @block.sync
            def _(q):
                q.wait_ge(S("xq3").h, 2)
                for oc16 in range(16):
                    if oc16 >= 1:
                        q.wait_ge(r_d.prod.h, dmm0 + oc16)
                    src = ag_out.ap().rearrange("(c r) j -> c r j", c=8)[
                        :, D_OFF + oc16 * DS: D_OFF + (oc16 + 1) * DS, :]
                    src = src.rearrange("c (p l) j -> p c l j", p=128)
                    dst = H["BDT"][:].rearrange("p (c l j) -> p c l j", c=8, l=SLC)
                    S("bd").dma(q.dma_start(dst, src))

            @block.tensor
            def _(t):
                S("xq3").wait(t)
                for oc16 in range(16):
                    par = oc16 % 2
                    t.wait_ge(S("bd").h, (oc16 + 1) * 16)
                    pd = r_d.acquire(t)
                    for fc in range(OCR):
                        lhs = H["BDT"][:, fc * 128:(fc + 1) * 128]
                        st_ = (fc == 0)
                        t.matmul(pd[:, 0:512], lhs, H["XQ3"][:, fc * 1024: fc * 1024 + 512],
                                 start=st_, stop=(fc == OCR - 1))
                        ins = t.matmul(pd[:, 512:1024], lhs,
                                       H["XQ3"][:, fc * 1024 + 512: (fc + 1) * 1024],
                                       start=st_, stop=(fc == OCR - 1))
                    r_d.commit(ins)
                    t.wait_ge(r_d.free.h, yd0 + oc16 + 1)
                    ypar = oc16 % 2
                    for tc in range(TC):
                        py = r_y.acquire(t)
                        r_y.commit(t.transpose(
                            py[:], H["YD"][:, ypar * 1024 + tc * 128: ypar * 1024 + (tc + 1) * 128],
                            IDF[:]))

            @block.scalar
            def _(s):
                s = _WS(s)
                for oc16 in range(16):
                    ypar = oc16 % 2
                    pd = r_d.take(s)
                    if oc16 >= 2:
                        s.wait_ge(r_y.free.h, yfree0 + (oc16 - 1) * TC)
                    ins = s.activation(H["YD"][:, ypar * 1024:(ypar + 1) * 1024], pd[:],
                                       AF.Identity, scale=GBF[:, oc16:oc16 + 1])
                    r_d.release(ins)

            @block.vector
            def _(v):
                v = _WV(v)
                for oc16 in range(16):
                    for tc in range(TC):
                        py = r_y.take(v)
                        dst = X2[:, tc * D + oc16 * 128: tc * D + (oc16 + 1) * 128]
                        ins = v.scalar_tensor_tensor(dst, py[:], st(SB_S3I, tc), dst,
                                                     ALU.mult, ALU.add)
                        r_y.release(ins)
                v.drain()
                S("xfence").inc(v.engine_nop())

            @block.sync
            def _(q):
                q.wait_ge(r_y.free.h, yfree0 + 16 * TC)
                S("xfence").wait(q)
                for tc in range(TC):
                    S("stout").dma(q.dma_start(out_h.ap()[tc * 128:(tc + 1) * 128, :],
                                               X2[:, tc * D:(tc + 1) * D]))
                name2buf = {"h1": "H1", "kf": "KF", "qf": "QF", "avf": "AVF",
                            "xq2": "XQ2", "xq3": "XQ3", "s3full": "S3FULL",
                            "modsq": "MODSQ", "gafull": "GAFULL",
                            "stat": "STAT", "mods6": "MODS6", "hidr": "HIDR",
                            "yd": "YD", "xq3i": "XQ3I", "gs": "GS",
                            "rmax": "RMAX", "sbfull": "SBFULL",
                            "cbfull": "CBFULL", "xq2i": "XQ2I"}
                dram_map = {"wgnat": wg_nat, "wunat": wu_nat, "wdnat": wd_nat,
                            "agout": ag_out, "modout": mod_out, "hidsp": hid_sp}
                for tn, hh in tap_h.items():
                    if tn in name2buf:
                        S("stout").dma(q.dma_start(hh.ap(), H[name2buf[tn]][:]))
                    elif tn in dram_map:
                        S("stout").dma(q.dma_start(hh.ap(), dram_map[tn].ap()))
                S("stout").wait(q)

    except _Cut:
        pass
    return nc



TAPS = ()
LASTRES = None

def _make_mask(o):
    t = np.arange(8)[:, None, None]
    qi = np.arange(128)[None, :, None]
    col = np.arange(144)[None, None, :]
    i = o + t * 128 + qi
    wstart = o + t * 128 - 8
    jwin = wstart + (col - 8)
    jini = col
    init_ok = (jini < 4) & (jini <= i) & (jini < wstart) & (col < 8)
    win_ok = (jwin >= 0) & (jwin <= i) & ((jwin < 4) | (i - jwin < 8)) & (col >= 8)
    allowed = np.where(col < 8, init_ok, win_ok)
    m = np.where(allowed, np.float32(0.0), np.float32(NEG)).astype(np.float32)
    return np.ascontiguousarray(m.transpose(1, 0, 2).reshape(128, 8 * 144))


def kernel(L0_attn_x, L0_attn_Wq, L0_attn_Wk, L0_attn_Wv, L0_attn_Wo,
           L0_ffn_w_gate, L0_ffn_w_up, L0_ffn_w_down, L0_c, L0_wmod):
    x = np.ascontiguousarray(np.asarray(L0_attn_x, np.float32))
    wq = np.ascontiguousarray(np.asarray(L0_attn_Wq, np.float32))
    wk = np.ascontiguousarray(np.asarray(L0_attn_Wk, np.float32))
    wv = np.ascontiguousarray(np.asarray(L0_attn_Wv, np.float32))
    wo = np.ascontiguousarray(np.asarray(L0_attn_Wo, np.float32))
    wg = np.asarray(L0_ffn_w_gate, np.float32)
    wu = np.asarray(L0_ffn_w_up, np.float32)
    wd = np.asarray(L0_ffn_w_down, np.float32)
    c = np.ascontiguousarray(np.asarray(L0_c, np.float32))
    wmod = np.asarray(L0_wmod, np.float32)

    wg_p = np.zeros((8 * DS, D), np.float32)
    wg_p[:DFF] = wg
    wu_p = np.zeros((8 * DS, D), np.float32)
    wu_p[:DFF] = wu
    wd_p = np.zeros((D, 8 * DS), np.float32)
    wd_p[:, :DFF] = wd

    idf = np.eye(128, dtype=np.float32)
    idb = np.eye(128, dtype=np.float32).astype(ml_dtypes.bfloat16)
    mask0, mask1 = _make_mask(0), _make_mask(T)

    nc = build_program(taps=TAPS)
    in_maps = []
    for core in range(8):
        b, half = core // 2, core % 2
        o = half * T
        if half == 0:
            x_ext = np.ascontiguousarray(np.concatenate([x[b, 0:8], x[b, 8:16]], 0))
        else:
            x_ext = np.ascontiguousarray(np.concatenate([x[b, 0:8], x[b, o - 8:o]], 0))
        in_maps.append({
            "x_main": np.ascontiguousarray(x[b, o:o + T]),
            "x_ext": x_ext,
            "mask": mask0 if half == 0 else mask1,
            "c_all": c,
            "wmod_sl": np.ascontiguousarray(wmod[:, core * 1536:(core + 1) * 1536]),
            "wq": wq, "wk": wk, "wv": wv, "wo": wo,
            "wg_sl": np.ascontiguousarray(wg_p[core * DS:(core + 1) * DS]),
            "wu_sl": np.ascontiguousarray(wu_p[core * DS:(core + 1) * DS]),
            "wd_sl": np.ascontiguousarray(wd_p[:, core * DS:(core + 1) * DS]),
            "idf": idf, "idb": idb,
        })

    res = run_bass_kernel_spmd(nc, in_maps, core_ids=list(range(8)))
    globals()["LASTRES"] = res
    out = np.empty((4, 2048, D), np.float32)
    for core in range(8):
        b, half = core // 2, core % 2
        out[b, half * T:(half + 1) * T] = res.results[core]["out"]
    return out


# version 08
# speedup vs baseline: 1.0284x; 1.0284x over previous
import numpy as np
import ml_dtypes
import concourse.bass as bass
import concourse.mybir as mybir
from concourse.bass_utils import run_bass_kernel_spmd

F32 = mybir.dt.float32
BF16 = mybir.dt.bfloat16
I32 = mybir.dt.int32
AF = mybir.ActivationFunctionType
ALU = mybir.AluOpType
AXX = mybir.AxisListType.X

D = 2048
FC = 16
T = 1024
TC = 8
EXT = 16
KALL = EXT + T
NH = 16
NKV = 8
DFF = 5504
OCR = 43
SLC = 6
DS = SLC * 128
RG = [[0, 1, 2, 3, 4, 5, 6, 7]]
SCALE = float(np.float32(1.0 / np.sqrt(128.0)))
NEG = -1e30
BLKROWS = 3 * 2048 * SLC
G_OFF, U_OFF, D_OFF = 0, 12288, 24576


class Sem:
    def __init__(self, nc, name):
        self.h = nc.alloc_semaphore(name)
        self.n = 0

    def inc(self, instr, k=1):
        instr.then_inc(self.h, k)
        self.n += k
        return instr

    def dma(self, instr):
        return self.inc(instr, 16)

    def wait(self, eng, val=None):
        eng.wait_ge(self.h, self.n if val is None else val)


class Rot:
    _pool = {}

    def __new__(cls, nc, name, tiles):
        if name in cls._pool:
            obj = cls._pool[name]
            assert obj.pi == obj.ci, f"rot {name} rebound while in flight"
            obj.tiles = tiles
            return obj
        obj = super().__new__(cls)
        cls._pool[name] = obj
        obj.tiles = tiles
        obj.prod = Sem(nc, name + "_p")
        obj.free = Sem(nc, name + "_f")
        obj.pi = 0
        obj.ci = 0
        return obj

    def __init__(self, nc, name, tiles):
        pass

    def acquire(self, eng):
        if self.pi >= len(self.tiles):
            eng.wait_ge(self.free.h, self.pi - len(self.tiles) + 1)
        return self.tiles[self.pi % len(self.tiles)]

    def commit(self, instr):
        self.prod.inc(instr)
        self.pi += 1
        return instr

    def take(self, eng):
        eng.wait_ge(self.prod.h, self.ci + 1)
        t = self.tiles[self.ci % len(self.tiles)]
        self.ci += 1
        return t

    def release(self, instr):
        self.free.inc(instr)
        return instr



class SafeEng:

    _READS = {
        "tensor_copy": (1,), "tensor_tensor": (1, 2), "tensor_add": (1, 2),
        "tensor_sub": (1, 2), "tensor_mul": (1, 2), "tensor_max": (1, 2),
        "tensor_scalar": (1, 2, 3), "tensor_single_scalar": (1, 2),
        "tensor_scalar_add": (1, 2), "tensor_scalar_sub": (1, 2),
        "tensor_scalar_mul": (1, 2), "tensor_scalar_max": (1, 2),
        "tensor_scalar_min": (1, 2), "tensor_reduce": (1,),
        "reduce_max": (1,), "reduce_sum": (1,), "reciprocal": (1,),
        "scalar_tensor_tensor": (1, 2, 3), "select": (1, 2, 3),
        "activation": (1,), "copy": (1,), "mul": (1,), "add": (1,),
        "sign": (1,), "square": (1,), "sqrt": (1,),
    }

    DEPTH = 1

    def __init__(self, eng, dumfn):
        self._e = eng
        self._dumfn = dumfn
        self._w = [set() for _ in range(self.DEPTH)]
        self.n_dum = 0

    @staticmethod
    def _name(x):
        t = getattr(x, "tensor", None)
        return getattr(t, "name", None) if t is not None else None

    def _pre(self, reads):
        rd = {n for n in reads if n}
        guard = 0
        while rd & set().union(*self._w) and guard < 4:
            self._dumfn(self._e)
            self._w = self._w[1:] + [{"__dum__"}]
            self.n_dum += 1
            guard += 1

    def _post(self, writes):
        self._w = self._w[1:] + [{n for n in writes if n}]

    def __getattr__(self, name):
        eng = self._e
        if name in self._READS:
            ridx = self._READS[name]

            def call(*args, **kwargs):
                reads = [self._name(args[i]) for i in ridx if i < len(args)]
                for kw in ("bias", "scale", "scale2"):
                    if kw in kwargs:
                        reads.append(self._name(kwargs[kw]))
                for a in args[1:]:
                    n = self._name(a)
                    if n:
                        reads.append(n)
                writes = [self._name(args[0])]
                ao = kwargs.get("accum_out")
                if ao is not None:
                    writes.append(self._name(ao))
                self._pre(reads)
                ins = getattr(eng, name)(*args, **kwargs)
                self._post(writes)
                return ins
            return call
        if name == "memset":
            def call(*args, **kwargs):
                ins = eng.memset(*args, **kwargs)
                self._post([self._name(args[0])])
                return ins
            return call
        return getattr(eng, name)


class Plc:
    def __init__(self, nc, budget):
        self.nc = nc
        self.budget = budget
        self.items = []

    def add(self, name, shape, dtype, e0, e1):
        per = int(np.prod(shape[1:])) * mybir.dt.size(dtype)
        per = (per + 31) // 32 * 32
        self.items.append([name, shape, dtype, per, e0, e1])

    def _try(self, order):
        placed, offs = [], {}
        for it in order:
            name, shape, dtype, sz, e0, e1 = it
            for off in sorted({0} | {o + s for o, s, *_ in placed}):
                if off + sz > self.budget:
                    continue
                if all(not (off < o + s and o < off + sz and e0 < f1 and f0 < e1)
                       for o, s, f0, f1 in placed):
                    placed.append((off, sz, e0, e1))
                    offs[name] = off
                    break
            else:
                return None, None
        return placed, offs

    def solve(self):
        keys = [lambda x: (-(x[5] - x[4]), -x[3]),
                lambda x: (-x[3], -(x[5] - x[4])),
                lambda x: (-(x[5] - x[4]) * x[3],),
                lambda x: (x[4], -x[3]),
                lambda x: (-x[3] * (x[5] - x[4]), x[4])]
        placed = None
        for k in keys:
            placed, offs = self._try(sorted(self.items, key=k))
            if placed is not None:
                break
        if placed is None:
            raise RuntimeError("SBUF packing failed under all orderings")
        self.span = max(o + s for o, s, *_ in placed)
        base = (self.nc.sbuf_base + 31) // 32 * 32
        self.nc.sbuf_base = base + self.span
        assert self.nc.sbuf_base <= self.nc.sbuf_top - 256, "SBUF overflow"
        return {name: self.nc.alloc_sbuf_tensor_at(name, shape, dtype, offset=base + offs[name])
                for name, shape, dtype, sz, e0, e1 in self.items}


class _Cut(Exception):
    pass


def build_program(taps=()):
    import os as _os
    KSTOP = int(_os.environ.get("KSTOP", "99"))
    Rot._pool = {}
    nc = bass.Bass("TRN2", target_bir_lowering=False, detect_race_conditions=False)

    din = {}
    for name, shape in [
        ("x_main", [T, D]), ("x_ext", [EXT, D]), ("mask", [128, 8 * 144]),
        ("c_all", [4, D]), ("wmod_sl", [D, 1536]),
        ("wq", [D, D]), ("wk", [1024, D]), ("wv", [1024, D]), ("wo", [D, D]),
        ("wg_sl", [DS, D]), ("wu_sl", [DS, D]), ("wd_sl", [D, DS]),
        ("idf", [128, 128]),
    ]:
        din[name] = nc.dram_tensor(name, shape, F32, kind="ExternalInput")
    din["idb"] = nc.dram_tensor("idb", [128, 128], BF16, kind="ExternalInput")
    out_h = nc.dram_tensor("out", [T, D], F32, kind="ExternalOutput")
    tap_h = {tn: nc.dram_tensor("tap_" + tn, ts_, td_, kind="ExternalOutput")
             for tn, ts_, td_ in taps}

    wq_bf = nc.dram_tensor("wq_bf", [D, D], BF16, kind="Internal")
    wk_bf = nc.dram_tensor("wk_bf", [1024, D], BF16, kind="Internal")
    wv_bf = nc.dram_tensor("wv_bf", [1024, D], BF16, kind="Internal")
    wo_bf = nc.dram_tensor("wo_bf", [D, D], BF16, kind="Internal")
    wg_nat = nc.dram_tensor("wg_nat", [DS, D], BF16, kind="Internal")
    wu_nat = nc.dram_tensor("wu_nat", [DS, D], BF16, kind="Internal")
    wd_nat = nc.dram_tensor("wd_nat", [D, DS], BF16, kind="Internal")
    ag_in = nc.dram_tensor("ag_in", [BLKROWS, 128], BF16, kind="Internal")
    ag_out = nc.dram_tensor("ag_out", [8 * BLKROWS, 128], BF16, kind="Internal",
                            addr_space="Shared")
    mod_in = nc.dram_tensor("mod_in", [4, 1536], F32, kind="Internal")
    mod_out = nc.dram_tensor("mod_out", [32, 1536], F32, kind="Internal",
                             addr_space="Shared")
    hid_sp = nc.dram_tensor("hid_sp", [OCR * 128, 1024], F32, kind="Internal")

    P = Plc(nc, budget=229376 - 1024 - 16896)
    P.add("IDF", [128, 128], F32, 0, 10)
    P.add("IDB", [128, 128], BF16, 0, 10)
    P.add("ONES1", [1, 128], F32, 0, 10)
    P.add("ROWB", [1, 128], F32, 0, 10)
    P.add("MODS6", [16, 6 * 128], F32, 0, 10)
    P.add("MTMP", [16, 256], F32, 0, 10)
    P.add("CAF", [128, 16], F32, 0, 10)
    P.add("SAF", [128, 16], F32, 0, 10)
    P.add("GBF", [128, 16], F32, 0, 10)
    P.add("GAFM", [128, 16], F32, 0, 10)
    P.add("CBFM", [128, 16], F32, 0, 10)
    P.add("SBFM", [128, 16], F32, 0, 10)
    P.add("STAT", [128, 96], F32, 0, 10)
    P.add("X1", [128, TC * D], F32, 0, 2)
    P.add("X2", [128, TC * D], F32, 5, 10)
    P.add("SILC", [4, D], F32, 0, 1)
    P.add("SCF", [128, 64], F32, 0, 1)
    P.add("MODP", [4, 1536], F32, 0, 1)
    P.add("WMOD", [128, 8 * 1536], F32, 0, 1)
    P.add("XEXT", [16, D], F32, 0, 2)
    P.add("DUM", [128, D], F32, 1, 2)
    P.add("U1", [128, 2 * D], BF16, 1, 2)
    P.add("MASK", [128, 8 * 144], F32, 0, 5)
    P.add("H1", [128, FC * KALL], BF16, 1, 4)
    P.add("WST", [128, 16 * 512], BF16, 2, 6)
    P.add("KF", [128, NKV * KALL], BF16, 2, 5)
    P.add("VT", [128, NKV * TC * 128], BF16, 2, 5)
    P.add("VSM", [128, NKV * TC * 128], BF16, 2, 5)
    P.add("VFM", [128, KALL], BF16, 2, 3)
    P.add("VFMX", [128, 7 * 16], BF16, 2, 3)
    P.add("QF", [128, NH * T], BF16, 3, 5)
    P.add("F32Q", [128, 2 * D], F32, 1, 4)
    P.add("QBF", [128, 2 * D], BF16, 1, 4)
    P.add("QSC", [128, 64], F32, 1, 4)
    P.add("QS1", [128, 2 * D], BF16, 1, 4)
    P.add("QS2", [128, 2 * D], BF16, 1, 4)
    P.add("PIECE", [128, 2 * D], BF16, 1, 5)
    P.add("AVF", [128, NH * T], BF16, 4, 6)
    P.add("SSB", [128, 4 * 144], F32, 4, 5)
    P.add("PBF", [128, 2 * 144], BF16, 4, 5)
    P.add("PTAS", [128, 2 * 128], BF16, 4, 5)
    P.add("PTMS", [128, 2 * 128], BF16, 4, 5)
    P.add("TMPW", [128, 1024], F32, 5, 6)
    P.add("GAFULL", [128, D], F32, 5, 6)
    P.add("CBFULL", [128, D], F32, 6, 7)
    P.add("SBFULL", [128, D], F32, 6, 7)
    P.add("U2", [128, D], F32, 6, 7)
    P.add("XQ2I", [128, D], I32, 6, 7)
    P.add("XQ2S", [128, D], BF16, 6, 7)
    P.add("XQ2", [128, FC * T], BF16, 6, 8)
    P.add("BGT", [128, 3 * D], BF16, 7, 8)
    P.add("BUT", [128, 3 * D], BF16, 7, 8)
    P.add("GS", [128, 1024], F32, 7, 8)
    P.add("HIDS", [128, 2 * 1024], F32, 7, 8)
    P.add("RMAX", [128, 1024], F32, 7, 9)
    P.add("RMIN", [128, 1024], F32, 7, 9)
    P.add("S3FULL", [128, 1024], F32, 8, 10)
    P.add("XQ3", [128, OCR * 1024], BF16, 8, 10)
    P.add("XQ3I", [128, 1024], I32, 8, 10)
    P.add("BDT", [128, 6144], BF16, 9, 10)
    P.add("HIDR", [128, 2 * 1024], F32, 8, 10)
    P.add("YD", [128, 2 * 1024], F32, 9, 10)
    P.add("HDUM", [128, 4], F32, 0, 10)
    H = P.solve()
    HDUM = H["HDUM"]

    def _vdum(e):
        e.tensor_scalar_mul(HDUM[:, 0:1], HDUM[:, 0:1], 1.0)

    def _sdum(e):
        e.activation(HDUM[:, 2:3], HDUM[:, 2:3], AF.Copy)

    _WV = lambda e: SafeEng(e, _vdum)
    _WS = lambda e: SafeEng(e, _sdum)

    IDF, IDB, ONES1, ROWB, MODS6, MTMP = (H[k] for k in
        ["IDF", "IDB", "ONES1", "ROWB", "MODS6", "MTMP"])
    CAF, SAF, GBF, STAT = H["CAF"], H["SAF"], H["GBF"], H["STAT"]
    X1, X2, XEXT, MASK, H1, WST = H["X1"], H["X2"], H["XEXT"], H["MASK"], H["H1"], H["WST"]
    KF, VT, VSM, VFM, QF, AVF = (H[k] for k in
        ["KF", "VT", "VSM", "VFM", "QF", "AVF"])

    def st(base, i, n=128):
        return STAT[0:n, base + i: base + i + 1]
    SB_SUM, SB_SSQ, SB_M, SB_VAR, SB_R, SB_NMR = 0, 9, 18, 27, 36, 45
    SB_NM, SB_DEN, SB_S3, SB_S3I, SB_AMX = 54, 58, 62, 70, 86

    sems = {}

    def S(name):
        if name not in sems:
            sems[name] = Sem(nc, name)
        return sems[name]

    def _cut(level, block, waitsems):
        if KSTOP != level:
            return
        @block.sync
        def _(q):
            for sname in waitsems:
                S(sname).wait(q)
            for tc in range(TC):
                S("stout").dma(q.dma_start(out_h.ap()[tc * 128:(tc + 1) * 128, :],
                                           H["X1" if KSTOP <= 2 else "X1"][:, tc * D:(tc + 1) * D]
                                           if KSTOP <= 5 else H["X2"][:, tc * D:(tc + 1) * D]))
            S("stout").wait(q)
        raise _Cut()

    try:
      with nc.Block() as block:
        _sid_P0, _ = nc.enter_named_scope("P0", False)
        @block.sync
        def _(q):
            for tc in range(TC):
                S("x").dma(q.dma_start(X1[:, tc * D:(tc + 1) * D],
                                       din["x_main"].ap()[tc * 128:(tc + 1) * 128, :]))
            S("x").dma(q.dma_start(XEXT[:], din["x_ext"].ap()))
            S("misc").dma(q.dma_start(IDF[:], din["idf"].ap()))
            S("misc").dma(q.dma_start(IDB[:], din["idb"].ap()))
            S("misc").dma(q.dma_start(H["SILC"][:], din["c_all"].ap()))
            S("misc").dma(q.dma_start(MASK[:], din["mask"].ap()))
            for kc in range(8):
                S("wm1").dma(q.dma_start(H["WMOD"][:, kc * 1536:(kc + 1) * 1536],
                                         din["wmod_sl"].ap()[kc * 128:(kc + 1) * 128, :]))

        nc.leave_named_scope("P0", _sid_P0, False)
        _sid_P1, _ = nc.enter_named_scope("P1", False)
        with (nc.psum_tensor("pmt0", [128, 16], F32) as pmt0,
              nc.psum_tensor("pmt1", [128, 16], F32) as pmt1,
              nc.psum_tensor("pmA", [4, 512], F32) as pmA,
              nc.psum_tensor("pmB", [4, 512], F32) as pmB,
              nc.psum_tensor("pmC", [4, 512], F32) as pmC):
            r_scf = Rot(nc, "rA", [pmt0, pmt1])

            @block.scalar
            def _(s):
                s = _WS(s)
                S("misc").wait(s)
                S("silc").inc(s.activation(H["SILC"][:], H["SILC"][:], AF.Silu))

            @block.tensor
            def _(t):
                S("silc").wait(t)
                for fc in range(FC):
                    pt = r_scf.acquire(t)
                    r_scf.commit(t.transpose(pt[:, 0:4],
                                             H["SILC"][0:4, fc * 128:(fc + 1) * 128],
                                             IDF[0:4, 0:4]))

            @block.vector
            def _(v):
                v = _WV(v)
                v.memset(ONES1[:], 1.0)
                for fc in range(FC):
                    pt = r_scf.take(v)
                    r_scf.release(v.tensor_copy(H["SCF"][:, fc * 4:fc * 4 + 4], pt[:, 0:4]))
                S("scf").inc(v.engine_nop())

            @block.tensor
            def _(t):
                S("scf").wait(t)
                S("wm1").wait(t)
                for ct, pm in enumerate([pmA, pmB, pmC]):
                    for kc in range(8):
                        ins = t.matmul(pm[:], H["SCF"][:, kc * 4:kc * 4 + 4],
                                       H["WMOD"][:, kc * 1536 + ct * 512: kc * 1536 + (ct + 1) * 512],
                                       start=(kc == 0), stop=False)
                S("mmh1").inc(ins)

            @block.sync
            def _(q):
                S("mmh1").wait(q)
                for kc in range(8):
                    S("wm2").dma(q.dma_start(H["WMOD"][:, kc * 1536:(kc + 1) * 1536],
                                             din["wmod_sl"].ap()[(8 + kc) * 128:(9 + kc) * 128, :]))

            @block.tensor
            def _(t):
                S("wm2").wait(t)
                for ct, pm in enumerate([pmA, pmB, pmC]):
                    for kc in range(8):
                        ins = t.matmul(pm[:], H["SCF"][:, (8 + kc) * 4:(8 + kc) * 4 + 4],
                                       H["WMOD"][:, kc * 1536 + ct * 512: kc * 1536 + (ct + 1) * 512],
                                       start=False, stop=(kc == 7))
                S("modmm").inc(ins)

            @block.vector
            def _(v):
                v = _WV(v)
                S("modmm").wait(v)
                for ct, pm in enumerate([pmA, pmB, pmC]):
                    v.tensor_copy(H["MODP"][:, ct * 512:(ct + 1) * 512], pm[:])
                v.drain()
                S("modp").inc(v.engine_nop())

            @block.sync
            def _(q):
                S("modp").wait(q)
                S("modin").dma(q.dma_start(mod_in.ap(), H["MODP"][:]))

            @block.gpsimd
            def _(g):
                S("cast").dma(g.dma_start(wk_bf.ap(), din["wk"].ap()))
                S("cast").dma(g.dma_start(wv_bf.ap(), din["wv"].ap()))
                S("cast").dma(g.dma_start(wq_bf.ap(), din["wq"].ap()))
                S("cast").dma(g.dma_start(wo_bf.ap(), din["wo"].ap()))
                S("modin").wait(g)
                S("agm").inc(g.collective_compute(
                    "AllGather", ALU.bypass, RG, ins=[mod_in.ap()], outs=[mod_out.ap()]))
                S("agm").wait(g)
                pid = g.partition_id()
                bsel = pid // 2
                for k in range(6):
                    col = k * 2048
                    while col < (k + 1) * 2048:
                        sl = col // 1536
                        end = min((sl + 1) * 1536, (k + 1) * 2048)
                        f0 = (col - k * 2048) // 128
                        nf = (end - col) // 128
                        srcp = mod_out.ap()[bass.ds(4 * sl + bsel, 1),
                                            col - 1536 * sl: end - 1536 * sl]
                        srcp = srcp.rearrange("o (f hh) -> (o f) hh", hh=128)
                        S("modsq").dma(g.dma_start(
                            MODS6[f0:f0 + nf, k * 128:(k + 1) * 128], srcp))
                        col = end

            @block.vector
            def _(v):
                v = _WV(v)
                S("modsq").wait(v)
                v.tensor_scalar_add(MODS6[:, 128:256], MODS6[:, 128:256], 1.0)
                ins = v.tensor_scalar_add(MODS6[:, 512:640], MODS6[:, 512:640], 1.0)
                S("modpre").inc(ins)

            @block.scalar
            def _(s):
                s = _WS(s)
                S("modpre").wait(s)
                s.activation(MTMP[:, 0:128], MODS6[:, 256:384], AF.Abs)
                S("modabs").inc(s.activation(MTMP[:, 128:256], MODS6[:, 640:768], AF.Abs))

            @block.vector
            def _(v):
                v = _WV(v)
                S("modabs").wait(v)
                for cc, tcol in [(256, 0), (640, 128)]:
                    blk = MODS6[:, cc:cc + 128]
                    v.tensor_scalar_max(MTMP[0:16, tcol:tcol + 128], MTMP[0:16, tcol:tcol + 128], 0.25)
                    v.tensor_scalar(blk, blk, 0.0, 2.0, ALU.is_ge, ALU.mult)
                    v.tensor_scalar_add(blk, blk, -1.0)
                    ins = v.tensor_mul(blk, blk, MTMP[0:16, tcol:tcol + 128])
                S("modclamp").inc(ins)

            @block.tensor
            def _(t):
                S("modclamp").wait(t)
                for kvec in [0, 1, 5, 2, 3, 4]:
                    pt = r_scf.acquire(t)
                    r_scf.commit(t.transpose(pt[:], MODS6[:, kvec * 128:(kvec + 1) * 128],
                                             IDF[0:16, 0:16]))

            @block.vector
            def _(v):
                v = _WV(v)
                for dst in [SAF, CAF, GBF, H["GAFM"], H["SBFM"], H["CBFM"]]:
                    pt = r_scf.take(v)
                    r_scf.release(v.tensor_copy(dst[:], pt[:, 0:16]))
                S("modfm").inc(v.engine_nop())

        _cut(1, block, ["modfm"])

        nc.leave_named_scope("P1", _sid_P1, False)
        _sid_P2, _ = nc.enter_named_scope("P2", False)
        def csrc(i):
            return X1[:, i * D:(i + 1) * D] if i < TC else XEXT[:]

        def cn(i):
            return 128 if i < TC else EXT

        with (nc.psum_tensor("ph0", [128, 128], BF16) as ph0,
              nc.psum_tensor("ph1", [128, 128], BF16) as ph1,
              nc.psum_tensor("ph2", [128, 128], BF16) as ph2,
              nc.psum_tensor("ph3", [128, 128], BF16) as ph3):
            r_h1 = Rot(nc, "rB", [ph0, ph1, ph2, ph3])

            @block.scalar
            def _(s):
                s = _WS(s)
                S("x").wait(s)
                S("modin").wait(s)
                for i in range(9):
                    n = cn(i)
                    S("sq").inc(s.activation(H["DUM"][0:n, :], csrc(i)[0:n, :], AF.Square,
                                             accum_out=st(SB_SSQ, i, n)))

            @block.vector
            def _(v):
                v = _WV(v)
                S("x").wait(v)
                for i in range(9):
                    n = cn(i)
                    v.wait_ge(S("sq").h, i + 1)
                    v.reduce_sum(st(SB_SUM, i, n), csrc(i)[0:n, :], axis=AXX)
                    v.tensor_scalar_mul(st(SB_M, i, n), st(SB_SUM, i, n), 1.0 / D)
                    v.tensor_scalar_mul(st(SB_VAR, i, n), st(SB_SSQ, i, n), 1.0 / D)
                    v.tensor_mul(st(SB_SUM, i, n), st(SB_M, i, n), st(SB_M, i, n))
                    v.tensor_sub(st(SB_VAR, i, n), st(SB_VAR, i, n), st(SB_SUM, i, n))
                    S("var").inc(v.tensor_scalar_add(st(SB_VAR, i, n), st(SB_VAR, i, n), 1e-5))

            @block.scalar
            def _(s):
                s = _WS(s)
                for i in range(9):
                    n = cn(i)
                    s.wait_ge(S("var").h, i + 1)
                    S("sdev").inc(s.activation(st(SB_VAR, i, n), st(SB_VAR, i, n), AF.Sqrt))

            h1t0 = r_h1.prod.n

            @block.vector
            def _(v):
                v = _WV(v)
                for i in range(9):
                    n = cn(i)
                    v.wait_ge(S("sdev").h, i + 1)
                    v.reciprocal(st(SB_R, i, n), st(SB_VAR, i, n))
                    v.tensor_mul(st(SB_NMR, i, n), st(SB_M, i, n), st(SB_R, i, n))
                    v.tensor_scalar_mul(st(SB_NMR, i, n), st(SB_NMR, i, n), -1.0)
                    if i >= 2:
                        v.wait_ge(r_h1.prod.h, h1t0 + (i - 1) * FC)
                    ub = H["U1"][:, (i % 2) * D:(i % 2 + 1) * D]
                    S("u1").inc(v.tensor_scalar(ub[0:n, :], csrc(i)[0:n, :],
                                                st(SB_R, i, n), st(SB_NMR, i, n),
                                                ALU.mult, ALU.add))

            @block.tensor
            def _(t):
                for i in range(9):
                    n = cn(i)
                    t.wait_ge(S("u1").h, i + 1)
                    ub = H["U1"][:, (i % 2) * D:(i % 2 + 1) * D]
                    for fc in range(FC):
                        pt = r_h1.acquire(t)
                        ins = t.transpose(pt[:, 0:n], ub[0:n, fc * 128:(fc + 1) * 128],
                                          IDB[0:n, 0:n])
                        r_h1.commit(ins)

            @block.scalar
            def _(s):
                s = _WS(s)
                S("modfm").wait(s)
                for i in range(9):
                    n = cn(i)
                    base = 16 + i * 128 if i < TC else 0
                    for fc in range(FC):
                        pt = r_h1.take(s)
                        ins = s.activation(H1[:, fc * KALL + base: fc * KALL + base + n],
                                           pt[:, 0:n], AF.Identity,
                                           bias=SAF[:, fc:fc + 1], scale=CAF[:, fc:fc + 1])
                        r_h1.release(ins)
                S("h1done").inc(s.nop())

        _cut(2, block, ["h1done"])

        nc.leave_named_scope("P2", _sid_P2, False)
        _sid_P3a, _ = nc.enter_named_scope("P3a", False)
        qjobs = []
        for hsrc, hnat, nrc, ncol in [(din["wg_sl"], wg_nat, 6, D),
                                      (din["wu_sl"], wu_nat, 6, D),
                                      (din["wd_sl"], wd_nat, 16, DS)]:
            for rc in range(nrc):
                qjobs.append((hsrc, hnat, rc, ncol))
        NQJ = len(qjobs)

        QUANT_ON = KSTOP >= 7

        @block.gpsimd
        def _(g):
            if not QUANT_ON:
                return
            hsrc0, hnat0, rc0, ncol0 = qjobs[0]
            S("qload").dma(g.dma_start(H["F32Q"][:, 0:ncol0],
                                       hsrc0.ap()[rc0 * 128:(rc0 + 1) * 128, :]))
            for j, (hsrc, hnat, rc, ncol) in enumerate(qjobs):
                par = j % 2
                if j + 1 < NQJ:
                    hs2, hn2, rc2, nc2 = qjobs[j + 1]
                    if j >= 1:
                        g.wait_ge(S("vpn").h, j)
                    S("qload").dma(g.dma_start(
                        H["F32Q"][:, ((j + 1) % 2) * D: ((j + 1) % 2) * D + nc2],
                        hs2.ap()[rc2 * 128:(rc2 + 1) * 128, :]))
                g.wait_ge(S("vpn").h, j + 1)
                g.tensor_tensor(H["QS1"][:, par * D: par * D + ncol],
                                H["QS1"][:, par * D: par * D + ncol],
                                H["QS2"][:, par * D: par * D + ncol], ALU.subtract)
                S("gptern").inc(g.engine_nop())
                g.wait_ge(S("swq").h, j + 1)
                S("qwr").dma(g.dma_start(hnat.ap()[rc * 128:(rc + 1) * 128, :],
                                         H["QBF"][:, par * D: par * D + ncol]))

        def emit_vq(lo, hi):
            if not QUANT_ON:
                return
            @block.vector
            def _(v):
                v = _WV(v)
                for j in range(lo, hi):
                    hsrc, hnat, rc, ncol = qjobs[j]
                    ng = ncol // 128
                    par = j % 2
                    v.wait_ge(S("qload").h, (j + 1) * 16)
                    if j >= 2:
                        v.wait_ge(S("swq").h, j - 1)
                    scb = par * 16
                    for gc in range(ng):
                        wsl = H["F32Q"][:, par * D + gc * 128: par * D + (gc + 1) * 128]
                        sc = H["QSC"][:, scb + gc:scb + gc + 1]
                        hs = H["QSC"][:, 32:33]
                        nhs = H["QSC"][:, 33:34]
                        v.reduce_sum(sc, wsl, axis=AXX, apply_absolute_value=True)
                        v.tensor_scalar(sc, sc, 1.0 / 128.0, 1e-5, ALU.mult, ALU.add)
                        v.tensor_scalar_mul(hs, sc, 0.5)
                        v.tensor_scalar_mul(nhs, sc, -0.5)
                        v.tensor_scalar(H["QS1"][:, par * D + gc * 128: par * D + (gc + 1) * 128],
                                        wsl, hs, None, ALU.is_gt)
                        v.tensor_scalar(H["QS2"][:, par * D + gc * 128: par * D + (gc + 1) * 128],
                                        wsl, nhs, None, ALU.is_lt)
                    S("vpn").inc(v.engine_nop())

        def emit_swq(lo, hi):
            if not QUANT_ON:
                return
            @block.scalar
            def _(s):
                s = _WS(s)
                for j in range(lo, hi):
                    hsrc, hnat, rc, ncol = qjobs[j]
                    ng = ncol // 128
                    par = j % 2
                    s.wait_ge(S("gptern").h, j + 1)
                    if j >= 2:
                        s.wait_ge(S("qwr").h, (j - 1) * 16)
                    scb = (j % 2) * 16
                    for gc in range(ng):
                        s.activation(H["QBF"][:, par * D + gc * 128: par * D + (gc + 1) * 128],
                                     H["QS1"][:, par * D + gc * 128: par * D + (gc + 1) * 128],
                                     AF.Copy, scale=H["QSC"][:, scb + gc:scb + gc + 1])
                    s.drain()
                    S("swq").inc(s.nop())

        emit_vq(0, 9)
        emit_swq(0, 8)

        nc.leave_named_scope("P3a", _sid_P3a, False)
        _sid_P4, _ = nc.enter_named_scope("P4", False)
        with (nc.psum_tensor("pkv", [128, KALL], F32) as pkv,
              nc.psum_tensor("ptva", [128, 128], BF16) as ptva,
              nc.psum_tensor("ptvb", [128, 128], BF16) as ptvb,
              nc.psum_tensor("ptvs0", [16, 128], BF16) as ptvs0,
              nc.psum_tensor("ptvs1", [16, 128], BF16) as ptvs1):
            r_vm = Rot(nc, "rC", [ptva, ptvb])
            r_vs = Rot(nc, "rD", [ptvs0, ptvs1])

            @block.sync
            def _(q):
                q.wait_ge(S("u1").h, 9)
                q.wait_ge(S("cast").h, 32)
                for half in range(2):
                    if half > 0:
                        q.wait_ge(S("vmm").h, 4)
                    for fc in range(FC):
                        S("wstk").dma(q.dma_start(
                            WST[:, fc * 512:(fc + 1) * 512],
                            wk_bf.ap()[half * 512:(half + 1) * 512, fc * 128:(fc + 1) * 128],
                            transpose=True))
                    q.wait_ge(S("kmm").h, 4 * (half + 1))
                    for fc in range(FC):
                        S("wstv").dma(q.dma_start(
                            WST[:, fc * 512:(fc + 1) * 512],
                            wv_bf.ap()[half * 512:(half + 1) * 512, fc * 128:(fc + 1) * 128],
                            transpose=True))

            @block.tensor
            def _(t):
                S("h1done").wait(t)
                for half in range(2):
                    t.wait_ge(S("wstk").h, (half + 1) * 16 * 16)
                    for gl in range(4):
                        g8 = half * 4 + gl
                        if g8 > 0:
                            t.wait_ge(S("kev").h, g8)
                        for fc in range(FC):
                            lhs = WST[:, fc * 512 + gl * 128: fc * 512 + (gl + 1) * 128]
                            st_ = (fc == 0)
                            t.matmul(pkv[:, 0:512], lhs, H1[:, fc * KALL: fc * KALL + 512],
                                     start=st_, stop=(fc == FC - 1))
                            t.matmul(pkv[:, 512:1024], lhs,
                                     H1[:, fc * KALL + 512: fc * KALL + 1024],
                                     start=st_, stop=(fc == FC - 1))
                            ins = t.matmul(pkv[:, 1024:1040], lhs,
                                           H1[:, fc * KALL + 1024: fc * KALL + 1040],
                                           start=st_, stop=(fc == FC - 1))
                        S("kmm").inc(ins)
                    t.wait_ge(S("wstv").h, (half + 1) * 16 * 16)
                    for gl in range(4):
                        g8 = half * 4 + gl
                        t.wait_ge(S("kev").h, 4 * (half + 1))
                        if g8 > 0:
                            t.wait_ge(S("vfm").h, g8)
                        for fc in range(FC):
                            lhs = WST[:, fc * 512 + gl * 128: fc * 512 + (gl + 1) * 128]
                            st_ = (fc == 0)
                            t.matmul(pkv[:, 0:512], lhs, H1[:, fc * KALL: fc * KALL + 512],
                                     start=st_, stop=(fc == FC - 1))
                            t.matmul(pkv[:, 512:1024], lhs,
                                     H1[:, fc * KALL + 512: fc * KALL + 1024],
                                     start=st_, stop=(fc == FC - 1))
                            ins = t.matmul(pkv[:, 1024:1040], lhs,
                                           H1[:, fc * KALL + 1024: fc * KALL + 1040],
                                           start=st_, stop=(fc == FC - 1))
                        S("vmm").inc(ins)
                        t.wait_ge(S("vfm").h, g8 + 1)
                        pa = r_vs.acquire(t)
                        r_vs.commit(t.transpose(pa[:], VFM[:, 0:16], IDB[:]))
                        t.wait_ge(S("vfmx").h, g8 + 1)
                        for k in range(TC):
                            pm = r_vm.acquire(t)
                            r_vm.commit(t.transpose(pm[:], VFM[:, 16 + k * 128:16 + (k + 1) * 128], IDB[:]))
                            if k < TC - 1:
                                ps = r_vs.acquire(t)
                                r_vs.commit(t.transpose(
                                    ps[:], H["VFMX"][:, k * 16:(k + 1) * 16], IDB[:]))

            @block.scalar
            def _(s):
                s = _WS(s)
                for half in range(2):
                    for gl in range(4):
                        g8 = half * 4 + gl
                        s.wait_ge(S("kmm").h, g8 + 1)
                        S("kev").inc(s.activation(KF[:, g8 * KALL:(g8 + 1) * KALL], pkv[:],
                                                  AF.Copy, scale=SCALE))
                    for gl in range(4):
                        g8 = half * 4 + gl
                        s.wait_ge(S("vmm").h, g8 + 1)
                        S("vfm").inc(s.activation(VFM[:], pkv[:], AF.Copy))

            emit_swq(8, 18)

            @block.vector
            def _(v):
                v = _WV(v)
                S("h1done").wait(v)
                v.memset(VSM[:], 0.0)
                for g8 in range(NKV):
                    v.wait_ge(S("vfm").h, g8 + 1)
                    for k in range(TC - 1):
                        v.tensor_copy(H["VFMX"][:, k * 16:k * 16 + 8], VFM[:, 0:8])
                        v.tensor_copy(H["VFMX"][:, k * 16 + 8:(k + 1) * 16],
                                      VFM[:, 16 + k * 128 + 120:16 + (k + 1) * 128])
                    S("vfmx").inc(v.engine_nop())
                    pa = r_vs.take(v)
                    r_vs.release(v.tensor_copy(VSM[0:16, g8 * TC * 128:(g8 * TC + 1) * 128], pa[:]))
                    for k in range(TC):
                        pm = r_vm.take(v)
                        r_vm.release(v.tensor_copy(
                            VT[:, (g8 * TC + k) * 128:(g8 * TC + k + 1) * 128], pm[:]))
                        if k < TC - 1:
                            ps = r_vs.take(v)
                            r_vs.release(v.tensor_copy(
                                VSM[0:16, (g8 * TC + k + 1) * 128:(g8 * TC + k + 2) * 128], ps[:]))
                    S("vev").inc(v.engine_nop())

            emit_vq(9, NQJ)

        _cut(3, block, ["vev"])

        nc.leave_named_scope("P4", _sid_P4, False)
        _sid_P5, _ = nc.enter_named_scope("P5", False)
        with (nc.psum_tensor("pq0", [128, 1024], F32) as pq0,
              nc.psum_tensor("pq1", [128, 1024], F32) as pq1):
            r_q = Rot(nc, "rE", [pq0, pq1])
            qmm0 = r_q.prod.n

            @block.sync
            def _(q):
                q.wait_ge(S("cast").h, 48)
                q.wait_ge(S("vmm").h, 8)
                for sw in range(8):
                    if sw > 0:
                        q.wait_ge(r_q.prod.h, qmm0 + 2 * sw)
                    for fc in range(FC):
                        S("wstq").dma(q.dma_start(
                            WST[:, fc * 256:(fc + 1) * 256],
                            wq_bf.ap()[sw * 256:(sw + 1) * 256, fc * 128:(fc + 1) * 128],
                            transpose=True))

            @block.tensor
            def _(t):
                t.wait_ge(S("vev").h, NKV)
                for sw in range(8):
                    t.wait_ge(S("wstq").h, (sw + 1) * 16 * 16)
                    for h2 in range(2):
                        h = sw * 2 + h2
                        pq = r_q.acquire(t)
                        for fc in range(FC):
                            lhs = WST[:, fc * 256 + h2 * 128: fc * 256 + (h2 + 1) * 128]
                            st_ = (fc == 0)
                            t.matmul(pq[:, 0:512], lhs, H1[:, fc * KALL + 16: fc * KALL + 528],
                                     start=st_, stop=(fc == FC - 1))
                            ins = t.matmul(pq[:, 512:1024], lhs,
                                           H1[:, fc * KALL + 528: fc * KALL + 1040],
                                           start=st_, stop=(fc == FC - 1))
                        r_q.commit(ins)

            @block.scalar
            def _(s):
                s = _WS(s)
                for h in range(NH):
                    pq = r_q.take(s)
                    ins = s.activation(QF[:, h * T:(h + 1) * T], pq[:], AF.Copy)
                    r_q.release(ins)
                S("qall").inc(s.nop())

            emit_swq(18, NQJ)

        _cut(4, block, ["qall"])

        nc.leave_named_scope("P5", _sid_P5, False)
        _sid_P3b, _ = nc.enter_named_scope("P3b", False)
        rjobs = [("gu", wg_nat, ocl, 0) for ocl in range(SLC)] + \
                [("gu", wu_nat, ocl, 1) for ocl in range(SLC)] + \
                [("d", wd_nat, oc16, 2) for oc16 in range(16)]
        qwr_need = [6 * 16, 12 * 16, 28 * 16]

        @block.sync
        def _(q):
            if not QUANT_ON:
                return
            cum = 0
            for j, (kind, hnat, idx, wi) in enumerate(rjobs):
                par = j % 2
                if j >= 2:
                    q.wait_ge(S("rord").h, (j - 1) * 16)
                q.wait_ge(S("qwr").h, qwr_need[wi])
                npc = 16 if kind == "gu" else SLC
                for pc in range(npc):
                    S("tpiece").dma(q.dma_start(
                        H["PIECE"][:, par * D + pc * 128: par * D + (pc + 1) * 128],
                        hnat.ap()[idx * 128:(idx + 1) * 128, pc * 128:(pc + 1) * 128],
                        transpose=True))
                cum += npc

        @block.scalar
        def _(s):
            s = _WS(s)
            if not QUANT_ON:
                return
            cum = 0
            for j, (kind, hnat, idx, wi) in enumerate(rjobs):
                par = j % 2
                npc = 16 if kind == "gu" else SLC
                cum += npc
                s.wait_ge(S("tpiece").h, cum * 16)
                if kind == "gu":
                    base = (G_OFF if wi == 0 else U_OFF) + idx * 2048
                    dst = ag_in.ap()[base:base + 2048, :].rearrange("(p a) j -> p (a j)", p=128)
                    src = H["PIECE"][:, par * D: par * D + 2048]
                else:
                    base = D_OFF + idx * DS
                    dst = ag_in.ap()[base:base + DS, :].rearrange("(p a) j -> p (a j)", p=128)
                    src = H["PIECE"][:, par * D: par * D + DS]
                S("rord").dma(s.dma_start(dst, src))

        @block.gpsimd
        def _(g):
            if not QUANT_ON:
                return
            S("rord").wait(g)
            S("agf").inc(g.collective_compute(
                "AllGather", ALU.bypass, RG, ins=[ag_in.ap()], outs=[ag_out.ap()]))

        nc.leave_named_scope("P3b", _sid_P3b, False)
        _sid_P6, _ = nc.enter_named_scope("P6", False)
        tiles = [(h, tt) for h in range(NH) for tt in range(TC)]
        NTI = len(tiles)

        with (nc.psum_tensor("psc0", [128, 144], F32) as psc0,
              nc.psum_tensor("psc1", [128, 144], F32) as psc1,
              nc.psum_tensor("ptab0", [16, 128], BF16) as ptab0,
              nc.psum_tensor("ptab1", [16, 128], BF16) as ptab1,
              nc.psum_tensor("ptm0", [128, 128], BF16) as ptm0,
              nc.psum_tensor("ptm1", [128, 128], BF16) as ptm1,
              nc.psum_tensor("pav0", [128, 128], F32) as pav0,
              nc.psum_tensor("pav1", [128, 128], F32) as pav1):
            r_s = Rot(nc, "rF", [psc0, psc1])
            r_ab = Rot(nc, "rA", [ptab0, ptab1])
            r_m = Rot(nc, "rB", [ptm0, ptm1])
            r_av = Rot(nc, "rC", [pav0, pav1])
            av0 = r_av.prod.n

            def emit_pt(t, i):
                par = i % 2
                t.wait_ge(S("pbf").h, i + 1)
                pab = r_ab.acquire(t)
                r_ab.commit(t.transpose(pab[:], H["PBF"][:, par * 144: par * 144 + 16], IDB[:]))
                pm = r_m.acquire(t)
                r_m.commit(t.transpose(pm[:], H["PBF"][:, par * 144 + 16: par * 144 + 144], IDB[:]))

            def emit_av(t, i):
                h, tt = tiles[i]
                g8 = h // 2
                par = i % 2
                t.wait_ge(S("ptev").h, i + 1)
                pav = r_av.acquire(t)
                t.matmul(pav[:], VT[:, (g8 * TC + tt) * 128:(g8 * TC + tt + 1) * 128],
                         H["PTMS"][:, par * 128:(par + 1) * 128], start=True, stop=False)
                ins = t.matmul(pav[:], VSM[:, (g8 * TC + tt) * 128:(g8 * TC + tt + 1) * 128],
                               H["PTAS"][:, par * 128:(par + 1) * 128], start=False, stop=True)
                r_av.commit(ins)

            @block.tensor
            def _(t):
                S("qall").wait(t)
                for i in range(NTI):
                    h, tt = tiles[i]
                    g8 = h // 2
                    ps = r_s.acquire(t)
                    t.matmul(ps[:, 8:144], QF[:, h * T + tt * 128: h * T + (tt + 1) * 128],
                             KF[:, g8 * KALL + 8 + 128 * tt: g8 * KALL + 144 + 128 * tt],
                             start=True, stop=True)
                    ins = t.matmul(ps[:, 0:8], QF[:, h * T + tt * 128: h * T + (tt + 1) * 128],
                                   KF[:, g8 * KALL: g8 * KALL + 8], start=True, stop=True)
                    r_s.commit(ins)
                    if i >= 2:
                        emit_pt(t, i - 2)
                    if i >= 3:
                        emit_av(t, i - 3)
                for i in range(NTI - 2, NTI):
                    emit_pt(t, i)
                for i in range(NTI - 3, NTI):
                    emit_av(t, i)

            def v_pre(v, i):
                h, tt = tiles[i]
                par = i % 2
                if i >= 2:
                    v.wait_ge(S("sexp").h, i - 1)
                ps = r_s.take(v)
                v.tensor_tensor(H["SSB"][:, par * 144:(par + 1) * 144], ps[:],
                                MASK[:, tt * 144:(tt + 1) * 144], ALU.add)
                r_s.release(
                    v.reduce_max(st(SB_NM, par), H["SSB"][:, par * 144:(par + 1) * 144],
                                 axis=AXX, negate=True))
                S("smax").inc(v.engine_nop())

            def v_post(v, i):
                par = i % 2
                v.wait_ge(S("sexp").h, i + 1)
                if i >= 2:
                    v.wait_ge(S("ptev").h, i - 1)
                v.reciprocal(st(SB_DEN, 2 + par), st(SB_DEN, par))
                S("pbf").inc(v.tensor_scalar(H["PBF"][:, par * 144:(par + 1) * 144],
                                             H["SSB"][:, (2 + par) * 144:(3 + par) * 144],
                                             st(SB_DEN, 2 + par), None, ALU.mult))

            def v_ptev(v, i):
                par = i % 2
                if i >= 2:
                    v.wait_ge(r_av.prod.h, av0 + i - 1)
                pab = r_ab.take(v)
                r_ab.release(v.tensor_copy(H["PTAS"][0:16, par * 128:(par + 1) * 128], pab[:]))
                pm = r_m.take(v)
                r_m.release(v.tensor_copy(H["PTMS"][:, par * 128:(par + 1) * 128], pm[:]))
                S("ptev").inc(v.engine_nop())

            def v_avev(v, i):
                h, tt = tiles[i]
                pav = r_av.take(v)
                r_av.release(v.tensor_copy(
                    AVF[:, h * T + tt * 128: h * T + (tt + 1) * 128], pav[:]))
                S("avev").inc(v.engine_nop())

            @block.vector
            def _(v):
                v = _WV(v)
                S("rord").wait(v)
                v.memset(H["PTAS"][:], 0.0)
                for i in range(NTI):
                    v_pre(v, i)
                    if i >= 1:
                        v_post(v, i - 1)
                    if i >= 2:
                        v_ptev(v, i - 2)
                    if i >= 3:
                        v_avev(v, i - 3)
                v_post(v, NTI - 1)
                for i in range(NTI - 2, NTI):
                    v_ptev(v, i)
                for i in range(NTI - 3, NTI):
                    v_avev(v, i)

            @block.scalar
            def _(s):
                s = _WS(s)
                for i in range(NTI):
                    par = i % 2
                    s.wait_ge(S("smax").h, i + 1)
                    if i >= 2:
                        s.wait_ge(S("pbf").h, i - 1)
                    S("sexp").inc(s.activation(H["SSB"][:, (2 + par) * 144:(3 + par) * 144],
                                               H["SSB"][:, par * 144:(par + 1) * 144], AF.Exp,
                                               bias=st(SB_NM, par), accum_out=st(SB_DEN, par)))

        _cut(5, block, ["avev"])

        nc.leave_named_scope("P6", _sid_P6, False)
        _sid_P7, _ = nc.enter_named_scope("P7", False)
        with (nc.psum_tensor("pxo0", [128, 512], F32) as pxo0,
              nc.psum_tensor("pxo1", [128, 512], F32) as pxo1,
              nc.psum_tensor("pga", [128, 128], F32) as pga,
              nc.psum_tensor("prow", [1, 128], F32) as prow):
            r_xo = Rot(nc, "rD", [pxo0, pxo1])
            womm0 = r_xo.prod.n

            @block.sync
            def _(q):
                S("avev").wait(q)
                q.wait_ge(S("cast").h, 64)
                for tc in range(TC):
                    S("xrl").dma(q.dma_start(X2[:, tc * D:(tc + 1) * D],
                                             din["x_main"].ap()[tc * 128:(tc + 1) * 128, :]))
                for q4 in range(4):
                    if q4 > 0:
                        q.wait_ge(r_xo.prod.h, womm0 + q4 * TC)
                    for fc in range(FC):
                        S("wsto").dma(q.dma_start(
                            WST[:, fc * 512:(fc + 1) * 512],
                            wo_bf.ap()[q4 * 512:(q4 + 1) * 512, fc * 128:(fc + 1) * 128],
                            transpose=True))

            @block.tensor
            def _(t):
                S("avev").wait(t)
                for fc in range(FC):
                    if fc > 0:
                        t.wait_ge(S("garow").h, fc)
                    t.transpose(prow[:], H["GAFM"][:, fc:fc + 1], IDF[:]).then_inc(S("gaT").h, 1)
                    S("gaT").n += 1
                    t.wait_ge(S("garowe").h, fc + 1)
                    ins = t.matmul(pga[:], ONES1[:], ROWB[:], start=True, stop=True)
                    S("gamm").inc(ins)

            @block.vector
            def _(v):
                v = _WV(v)
                for fc in range(FC):
                    v.wait_ge(S("gaT").h, fc + 1)
                    ins = v.tensor_copy(ROWB[:], prow[:])
                    S("garowe").inc(ins)
                    v.wait_ge(S("gamm").h, fc + 1)
                    ins = v.tensor_copy(H["GAFULL"][:, fc * 128:(fc + 1) * 128], pga[:])
                    S("garow").inc(ins)
                S("gafull").inc(v.engine_nop())

            @block.tensor
            def _(t):
                for q4 in range(4):
                    t.wait_ge(S("wsto").h, (q4 + 1) * 16 * 16)
                    for tc in range(TC):
                        pxo = r_xo.acquire(t)
                        for fc in range(FC):
                            ins = t.matmul(pxo[:], AVF[:, fc * T + tc * 128: fc * T + (tc + 1) * 128],
                                           WST[:, fc * 512:(fc + 1) * 512],
                                           start=(fc == 0), stop=(fc == FC - 1))
                        r_xo.commit(ins)

            @block.vector
            def _(v):
                v = _WV(v)
                S("gafull").wait(v)
                S("xrl").wait(v)
                for q4 in range(4):
                    for tc in range(TC):
                        pxo = r_xo.take(v)
                        dst = X2[:, tc * D + q4 * 512: tc * D + (q4 + 1) * 512]
                        v.tensor_tensor(H["TMPW"][:, 0:512], pxo[:],
                                        H["GAFULL"][:, q4 * 512:(q4 + 1) * 512], ALU.mult)
                        r_xo.release(v.tensor_add(dst, dst, H["TMPW"][:, 0:512]))
                S("x1done").inc(v.engine_nop())

        _cut(6, block, ["x1done"])

        nc.leave_named_scope("P7", _sid_P7, False)
        _sid_P8, _ = nc.enter_named_scope("P8", False)
        with (nc.psum_tensor("pbc", [128, 128], F32) as pbc,
              nc.psum_tensor("prow2", [1, 128], F32) as prow2,
              nc.psum_tensor("px0", [128, 128], BF16) as px0,
              nc.psum_tensor("px1", [128, 128], BF16) as px1,
              nc.psum_tensor("px2", [128, 128], BF16) as px2,
              nc.psum_tensor("px3", [128, 128], BF16) as px3):
            r_x2 = Rot(nc, "rE", [px0, px1, px2, px3])
            xq2t0 = r_x2.prod.n

            @block.tensor
            def _(t):
                S("x1done").wait(t)
                for k2, fm in enumerate([H["SBFM"], H["CBFM"]]):
                    for fc in range(FC):
                        i = k2 * FC + fc
                        if i > 0:
                            t.wait_ge(S("bcrow").h, i)
                        t.transpose(prow2[:], fm[:, fc:fc + 1], IDF[:]).then_inc(S("bcT").h, 1)
                        S("bcT").n += 1
                        t.wait_ge(S("bcrowe").h, i + 1)
                        ins = t.matmul(pbc[:], ONES1[:], ROWB[:], start=True, stop=True)
                        S("bcmm").inc(ins)

            @block.vector
            def _(v):
                v = _WV(v)
                S("x1done").wait(v)
                for k2, dstn in enumerate(["SBFULL", "CBFULL"]):
                    for fc in range(FC):
                        i = k2 * FC + fc
                        v.wait_ge(S("bcT").h, i + 1)
                        ins = v.tensor_copy(ROWB[:], prow2[:])
                        S("bcrowe").inc(ins)
                        v.wait_ge(S("bcmm").h, i + 1)
                        ins = v.tensor_copy(H[dstn][:, fc * 128:(fc + 1) * 128], pbc[:])
                        S("bcrow").inc(ins)
                S("bcfull").inc(v.engine_nop())

            @block.scalar
            def _(s):
                s = _WS(s)
                S("x1done").wait(s)
                for i in range(TC):
                    S("sq2").inc(s.activation(H["U2"][:], X2[:, i * D:(i + 1) * D], AF.Square,
                                              accum_out=st(SB_SSQ, i)))

            @block.tensor
            def _(t):
                for i in range(TC):
                    t.wait_ge(S("xq2s").h, i + 1)
                    for fc in range(FC):
                        px = r_x2.acquire(t)
                        ins = t.transpose(px[:], H["XQ2S"][:, fc * 128:(fc + 1) * 128], IDB[:])
                        r_x2.commit(ins)

            @block.scalar
            def _(s):
                s = _WS(s)
                for i in range(TC):
                    s.wait_ge(S("var2").h, i + 1)
                    S("sdev2").inc(s.activation(st(SB_VAR, i), st(SB_VAR, i), AF.Sqrt))

            @block.vector
            def _(v):
                v = _WV(v)
                S("bcfull").wait(v)

                def evac_chunk(i):
                    for fc in range(FC):
                        px = r_x2.take(v)
                        r_x2.release(v.tensor_copy(
                            H["XQ2"][:, fc * T + i * 128: fc * T + (i + 1) * 128], px[:]))

                for i in range(TC):
                    v.wait_ge(S("sq2").h, i + 1)
                    v.reduce_sum(st(SB_SUM, i), X2[:, i * D:(i + 1) * D], axis=AXX)
                    v.tensor_scalar_mul(st(SB_M, i), st(SB_SUM, i), 1.0 / D)
                    v.tensor_scalar_mul(st(SB_VAR, i), st(SB_SSQ, i), 1.0 / D)
                    v.tensor_mul(st(SB_SUM, i), st(SB_M, i), st(SB_M, i))
                    v.tensor_sub(st(SB_VAR, i), st(SB_VAR, i), st(SB_SUM, i))
                    S("var2").inc(v.tensor_scalar_add(st(SB_VAR, i), st(SB_VAR, i), 1e-5))
                    v.wait_ge(S("sdev2").h, i + 1)
                    v.reciprocal(st(SB_R, i), st(SB_VAR, i))
                    v.tensor_mul(st(SB_NMR, i), st(SB_M, i), st(SB_R, i))
                    v.tensor_scalar_mul(st(SB_NMR, i), st(SB_NMR, i), -1.0)
                    v.tensor_scalar(H["U2"][:], X2[:, i * D:(i + 1) * D],
                                    st(SB_R, i), st(SB_NMR, i), ALU.mult, ALU.add)
                    v.tensor_mul(H["U2"][:], H["U2"][:], H["CBFULL"][:])
                    v.tensor_add(H["U2"][:], H["U2"][:], H["SBFULL"][:])
                    v.reduce_max(st(SB_AMX, 0), H["U2"][:], axis=AXX, apply_absolute_value=True)
                    v.tensor_scalar_max(st(SB_AMX, 0), st(SB_AMX, 0), 1e-5)
                    v.reciprocal(st(SB_AMX, 1), st(SB_AMX, 0))
                    v.tensor_scalar_mul(st(SB_AMX, 1), st(SB_AMX, 1), 127.0)
                    v.tensor_scalar_mul(st(SB_AMX, 2), st(SB_AMX, 0), 1.0 / 127.0)
                    if i >= 1:
                        evac_chunk(i - 1)
                        v.wait_ge(r_x2.prod.h, xq2t0 + i * FC)
                    v.tensor_scalar(H["XQ2I"][:], H["U2"][:], st(SB_AMX, 1), None, ALU.mult)
                    S("xq2s").inc(v.tensor_scalar(H["XQ2S"][:], H["XQ2I"][:], st(SB_AMX, 2),
                                                  None, ALU.mult))
                evac_chunk(TC - 1)
                S("xq2done").inc(v.engine_nop())



        _cut(7, block, ["xq2done"])

        nc.leave_named_scope("P8", _sid_P8, False)
        _sid_P9, _ = nc.enter_named_scope("P9", False)
        with (nc.psum_tensor("pg0", [128, 1024], F32) as pg0,
              nc.psum_tensor("pg1", [128, 1024], F32) as pg1,
              nc.psum_tensor("pu0", [128, 1024], F32) as pu0,
              nc.psum_tensor("pu1", [128, 1024], F32) as pu1):
            r_g = Rot(nc, "rF", [pg0, pg1])
            r_u = Rot(nc, "rA", [pu0, pu1])
            gsil0 = r_g.free.n

            @block.sync
            def _(q):
                S("agf").wait(q)
                S("xq2done").wait(q)
                for oc in range(OCR):
                    par = oc % 3
                    c, ocl = oc // SLC, oc % SLC
                    if oc >= 3:
                        q.wait_ge(S("guev").h, oc - 2)
                    base = c * BLKROWS + G_OFF + ocl * 2048
                    S("bg").dma(q.dma_start(
                        H["BGT"][:, par * D:(par + 1) * D],
                        ag_out.ap()[base:base + 2048, :].rearrange("(p a) j -> p (a j)", p=128)))

            @block.gpsimd
            def _(g):
                S("agf").wait(g)
                S("xq2done").wait(g)
                for oc in range(OCR):
                    par = oc % 3
                    c, ocl = oc // SLC, oc % SLC
                    if oc >= 3:
                        g.wait_ge(S("guev").h, oc - 2)
                    base = c * BLKROWS + U_OFF + ocl * 2048
                    S("bu").dma(g.dma_start(
                        H["BUT"][:, par * D:(par + 1) * D],
                        ag_out.ap()[base:base + 2048, :].rearrange("(p a) j -> p (a j)", p=128)))

            @block.vector
            def _(v):
                v = _WV(v)
                S("xq2done").wait(v)
                v.memset(H["RMAX"][:], 0.0)
                v.memset(H["RMIN"][:], 0.0)
                S("rmax0").inc(v.engine_nop())

            @block.tensor
            def _(t):
                S("rmax0").wait(t)
                for oc in range(OCR):
                    par = oc % 3
                    t.wait_ge(S("bg").h, (oc + 1) * 16)
                    t.wait_ge(S("bu").h, (oc + 1) * 16)
                    pg = r_g.acquire(t)
                    for fc in range(FC):
                        lhs = H["BGT"][:, par * D + fc * 128: par * D + (fc + 1) * 128]
                        st_ = (fc == 0)
                        t.matmul(pg[:, 0:512], lhs, H["XQ2"][:, fc * T: fc * T + 512],
                                 start=st_, stop=(fc == FC - 1))
                        ins = t.matmul(pg[:, 512:1024], lhs,
                                       H["XQ2"][:, fc * T + 512: fc * T + 1024],
                                       start=st_, stop=(fc == FC - 1))
                    r_g.commit(ins)
                    pu = r_u.acquire(t)
                    for fc in range(FC):
                        lhs = H["BUT"][:, par * D + fc * 128: par * D + (fc + 1) * 128]
                        st_ = (fc == 0)
                        t.matmul(pu[:, 0:512], lhs, H["XQ2"][:, fc * T: fc * T + 512],
                                 start=st_, stop=(fc == FC - 1))
                        ins = t.matmul(pu[:, 512:1024], lhs,
                                       H["XQ2"][:, fc * T + 512: fc * T + 1024],
                                       start=st_, stop=(fc == FC - 1))
                    r_u.commit(ins)

            @block.scalar
            def _(s):
                s = _WS(s)
                for oc in range(OCR):
                    pg = r_g.take(s)
                    ins = s.activation(H["GS"][:], pg[:], AF.Silu)
                    r_g.release(ins)
                    if oc >= 2:
                        par2 = (oc - 2) % 2
                        s.wait_ge(S("guev").h, oc - 1)
                        S("spill").dma(s.dma_start(
                            hid_sp.ap()[(oc - 2) * 128:(oc - 1) * 128, :],
                            H["HIDS"][:, par2 * 1024:(par2 + 1) * 1024]))
                for oc in range(OCR - 2, OCR):
                    par2 = oc % 2
                    s.wait_ge(S("guev").h, oc + 1)
                    S("spill").dma(s.dma_start(hid_sp.ap()[oc * 128:(oc + 1) * 128, :],
                                               H["HIDS"][:, par2 * 1024:(par2 + 1) * 1024]))

            @block.vector
            def _(v):
                v = _WV(v)
                for oc in range(OCR):
                    par = oc % 2
                    v.wait_ge(r_g.free.h, gsil0 + oc + 1)
                    if oc >= 2:
                        v.wait_ge(S("spill").h, (oc - 1) * 16)
                    pu = r_u.take(v)
                    hb = H["HIDS"][:, par * 1024:(par + 1) * 1024]
                    r_u.release(v.tensor_tensor(hb, H["GS"][:], pu[:], ALU.mult))
                    v.tensor_tensor(H["RMAX"][:], H["RMAX"][:], hb, ALU.max)
                    v.tensor_tensor(H["RMIN"][:], H["RMIN"][:], hb, ALU.min)
                    v.drain()
                    S("guev").inc(v.engine_nop())

        _cut(8, block, ["guev", "spill"])

        nc.leave_named_scope("P9", _sid_P9, False)
        _sid_P10, _ = nc.enter_named_scope("P10", False)
        with (nc.psum_tensor("pr0", [128, 128], F32) as pr0,
              nc.psum_tensor("pr1", [128, 128], F32) as pr1,
              nc.psum_tensor("prow3", [1, 128], F32) as prow3,
              nc.psum_tensor("ps3b", [128, 128], F32) as ps3b):
            r_r = Rot(nc, "rB", [pr0, pr1])

            @block.vector
            def _(v):
                v = _WV(v)
                S("guev").wait(v)
                v.tensor_scalar_mul(H["RMIN"][:], H["RMIN"][:], -1.0)
                ins = v.tensor_max(H["RMAX"][:], H["RMAX"][:], H["RMIN"][:])
                S("rmfin").inc(ins)

            @block.tensor
            def _(t):
                S("rmfin").wait(t)
                for tc in range(TC):
                    pr = r_r.acquire(t)
                    r_r.commit(t.transpose(pr[:], H["RMAX"][:, tc * 128:(tc + 1) * 128], IDF[:]))

            @block.vector
            def _(v):
                v = _WV(v)
                S("spill").wait(v)
                for tc in range(TC):
                    pr = r_r.take(v)
                    v.reduce_max(st(SB_S3, tc), pr[:], axis=AXX)
                    r_r.release(v.tensor_scalar_max(st(SB_S3, tc), st(SB_S3, tc), 1e-5))
                    v.reciprocal(st(SB_S3, tc), st(SB_S3, tc))
                    v.tensor_scalar_mul(st(SB_S3, tc), st(SB_S3, tc), 127.0)
                    ins = v.reciprocal(st(SB_S3I, tc), st(SB_S3, tc))
                S("s3").inc(ins)

            @block.tensor
            def _(t):
                S("s3").wait(t)
                for tc in range(TC):
                    if tc > 0:
                        t.wait_ge(S("s3row").h, tc)
                    t.transpose(prow3[:], st(SB_S3, tc), IDF[:]).then_inc(S("s3T").h, 1)
                    S("s3T").n += 1
                    t.wait_ge(S("s3rowe").h, tc + 1)
                    ins = t.matmul(ps3b[:], ONES1[:], ROWB[:], start=True, stop=True)
                    S("s3mm").inc(ins)

            @block.vector
            def _(v):
                v = _WV(v)
                for tc in range(TC):
                    v.wait_ge(S("s3T").h, tc + 1)
                    ins = v.tensor_copy(ROWB[:], prow3[:])
                    S("s3rowe").inc(ins)
                    v.wait_ge(S("s3mm").h, tc + 1)
                    ins = v.tensor_copy(H["S3FULL"][:, tc * 128:(tc + 1) * 128], ps3b[:])
                    S("s3row").inc(ins)
                S("s3f").inc(v.engine_nop())

            @block.sync
            def _(q):
                S("spill").wait(q)
                for oc in range(OCR):
                    par = oc % 2
                    if oc >= 2:
                        q.wait_ge(S("xq3").h, oc - 1)
                    S("hrl").dma(q.dma_start(H["HIDR"][:, par * 1024:(par + 1) * 1024],
                                             hid_sp.ap()[oc * 128:(oc + 1) * 128, :]))

            @block.vector
            def _(v):
                v = _WV(v)
                S("s3f").wait(v)
                for oc in range(OCR):
                    par = oc % 2
                    v.wait_ge(S("hrl").h, (oc + 1) * 16)
                    v.tensor_tensor(H["XQ3I"][:], H["HIDR"][:, par * 1024:(par + 1) * 1024],
                                    H["S3FULL"][:], ALU.mult)
                    S("xq3").inc(v.tensor_copy(H["XQ3"][:, oc * 1024:(oc + 1) * 1024], H["XQ3I"][:]))

        _cut(9, block, ["xq3"])

        nc.leave_named_scope("P10", _sid_P10, False)
        _sid_P11, _ = nc.enter_named_scope("P11", False)
        with (nc.psum_tensor("pd0", [128, 1024], F32) as pd0,
              nc.psum_tensor("pd1", [128, 1024], F32) as pd1,
              nc.psum_tensor("py0", [128, 128], F32) as py0,
              nc.psum_tensor("py1", [128, 128], F32) as py1):
            r_d = Rot(nc, "rC", [pd0, pd1])
            r_y = Rot(nc, "rD", [py0, py1])
            dmm0 = r_d.prod.n
            yd0 = r_d.free.n
            yfree0 = r_y.free.n

            @block.sync
            def _(q):
                q.wait_ge(S("xq3").h, 2)
                for oc16 in range(16):
                    if oc16 >= 1:
                        q.wait_ge(r_d.prod.h, dmm0 + oc16)
                    src = ag_out.ap().rearrange("(c r) j -> c r j", c=8)[
                        0:4, D_OFF + oc16 * DS: D_OFF + (oc16 + 1) * DS, :]
                    src = src.rearrange("c (p l) j -> p c l j", p=128)
                    dst = H["BDT"][:, 0:4 * SLC * 128].rearrange(
                        "p (c l j) -> p c l j", c=4, l=SLC)
                    S("bd").dma(q.dma_start(dst, src))

            @block.gpsimd
            def _(g):
                g.wait_ge(S("xq3").h, 2)
                for oc16 in range(16):
                    if oc16 >= 1:
                        g.wait_ge(r_d.prod.h, dmm0 + oc16)
                    src = ag_out.ap().rearrange("(c r) j -> c r j", c=8)[
                        4:8, D_OFF + oc16 * DS: D_OFF + (oc16 + 1) * DS, :]
                    src = src.rearrange("c (p l) j -> p c l j", p=128)
                    dst = H["BDT"][:, 4 * SLC * 128:8 * SLC * 128].rearrange(
                        "p (c l j) -> p c l j", c=4, l=SLC)
                    S("bd").dma(g.dma_start(dst, src))

            @block.tensor
            def _(t):
                S("xq3").wait(t)
                for oc16 in range(16):
                    par = oc16 % 2
                    t.wait_ge(S("bd").h, (oc16 + 1) * 32)
                    pd = r_d.acquire(t)
                    for fc in range(OCR):
                        lhs = H["BDT"][:, fc * 128:(fc + 1) * 128]
                        st_ = (fc == 0)
                        t.matmul(pd[:, 0:512], lhs, H["XQ3"][:, fc * 1024: fc * 1024 + 512],
                                 start=st_, stop=(fc == OCR - 1))
                        ins = t.matmul(pd[:, 512:1024], lhs,
                                       H["XQ3"][:, fc * 1024 + 512: (fc + 1) * 1024],
                                       start=st_, stop=(fc == OCR - 1))
                    r_d.commit(ins)
                    t.wait_ge(r_d.free.h, yd0 + oc16 + 1)
                    ypar = oc16 % 2
                    for tc in range(TC):
                        py = r_y.acquire(t)
                        r_y.commit(t.transpose(
                            py[:], H["YD"][:, ypar * 1024 + tc * 128: ypar * 1024 + (tc + 1) * 128],
                            IDF[:]))

            @block.scalar
            def _(s):
                s = _WS(s)
                for oc16 in range(16):
                    ypar = oc16 % 2
                    pd = r_d.take(s)
                    if oc16 >= 2:
                        s.wait_ge(r_y.free.h, yfree0 + (oc16 - 1) * TC)
                    ins = s.activation(H["YD"][:, ypar * 1024:(ypar + 1) * 1024], pd[:],
                                       AF.Identity, scale=GBF[:, oc16:oc16 + 1])
                    r_d.release(ins)

            @block.vector
            def _(v):
                v = _WV(v)
                for oc16 in range(16):
                    for tc in range(TC):
                        py = r_y.take(v)
                        dst = X2[:, tc * D + oc16 * 128: tc * D + (oc16 + 1) * 128]
                        ins = v.scalar_tensor_tensor(dst, py[:], st(SB_S3I, tc), dst,
                                                     ALU.mult, ALU.add)
                        r_y.release(ins)
                v.drain()
                S("xfence").inc(v.engine_nop())

            @block.sync
            def _(q):
                q.wait_ge(r_y.free.h, yfree0 + 16 * TC)
                S("xfence").wait(q)
                for tc in range(TC):
                    S("stout").dma(q.dma_start(out_h.ap()[tc * 128:(tc + 1) * 128, :],
                                               X2[:, tc * D:(tc + 1) * D]))
                name2buf = {"h1": "H1", "kf": "KF", "qf": "QF", "avf": "AVF",
                            "xq2": "XQ2", "xq3": "XQ3", "s3full": "S3FULL",
                            "modsq": "MODSQ", "gafull": "GAFULL",
                            "stat": "STAT", "mods6": "MODS6", "hidr": "HIDR",
                            "yd": "YD", "xq3i": "XQ3I", "gs": "GS",
                            "rmax": "RMAX", "sbfull": "SBFULL",
                            "cbfull": "CBFULL", "xq2i": "XQ2I"}
                dram_map = {"wgnat": wg_nat, "wunat": wu_nat, "wdnat": wd_nat,
                            "agout": ag_out, "modout": mod_out, "hidsp": hid_sp}
                for tn, hh in tap_h.items():
                    if tn in name2buf:
                        S("stout").dma(q.dma_start(hh.ap(), H[name2buf[tn]][:]))
                    elif tn in dram_map:
                        S("stout").dma(q.dma_start(hh.ap(), dram_map[tn].ap()))
                S("stout").wait(q)

        nc.leave_named_scope("P11", _sid_P11, False)
    except _Cut:
        pass
    return nc



TAPS = ()
LASTRES = None
RUNKW = {}

def _make_mask(o):
    t = np.arange(8)[:, None, None]
    qi = np.arange(128)[None, :, None]
    col = np.arange(144)[None, None, :]
    i = o + t * 128 + qi
    wstart = o + t * 128 - 8
    jwin = wstart + (col - 8)
    jini = col
    init_ok = (jini < 4) & (jini <= i) & (jini < wstart) & (col < 8)
    win_ok = (jwin >= 0) & (jwin <= i) & ((jwin < 4) | (i - jwin < 8)) & (col >= 8)
    allowed = np.where(col < 8, init_ok, win_ok)
    m = np.where(allowed, np.float32(0.0), np.float32(NEG)).astype(np.float32)
    return np.ascontiguousarray(m.transpose(1, 0, 2).reshape(128, 8 * 144))


def kernel(L0_attn_x, L0_attn_Wq, L0_attn_Wk, L0_attn_Wv, L0_attn_Wo,
           L0_ffn_w_gate, L0_ffn_w_up, L0_ffn_w_down, L0_c, L0_wmod):
    x = np.ascontiguousarray(np.asarray(L0_attn_x, np.float32))
    wq = np.ascontiguousarray(np.asarray(L0_attn_Wq, np.float32))
    wk = np.ascontiguousarray(np.asarray(L0_attn_Wk, np.float32))
    wv = np.ascontiguousarray(np.asarray(L0_attn_Wv, np.float32))
    wo = np.ascontiguousarray(np.asarray(L0_attn_Wo, np.float32))
    wg = np.asarray(L0_ffn_w_gate, np.float32)
    wu = np.asarray(L0_ffn_w_up, np.float32)
    wd = np.asarray(L0_ffn_w_down, np.float32)
    c = np.ascontiguousarray(np.asarray(L0_c, np.float32))
    wmod = np.asarray(L0_wmod, np.float32)

    wg_p = np.zeros((8 * DS, D), np.float32)
    wg_p[:DFF] = wg
    wu_p = np.zeros((8 * DS, D), np.float32)
    wu_p[:DFF] = wu
    wd_p = np.zeros((D, 8 * DS), np.float32)
    wd_p[:, :DFF] = wd

    idf = np.eye(128, dtype=np.float32)
    idb = np.eye(128, dtype=np.float32).astype(ml_dtypes.bfloat16)
    mask0, mask1 = _make_mask(0), _make_mask(T)

    nc = build_program(taps=TAPS)
    in_maps = []
    for core in range(8):
        b, half = core // 2, core % 2
        o = half * T
        if half == 0:
            x_ext = np.ascontiguousarray(np.concatenate([x[b, 0:8], x[b, 8:16]], 0))
        else:
            x_ext = np.ascontiguousarray(np.concatenate([x[b, 0:8], x[b, o - 8:o]], 0))
        in_maps.append({
            "x_main": np.ascontiguousarray(x[b, o:o + T]),
            "x_ext": x_ext,
            "mask": mask0 if half == 0 else mask1,
            "c_all": c,
            "wmod_sl": np.ascontiguousarray(wmod[:, core * 1536:(core + 1) * 1536]),
            "wq": wq, "wk": wk, "wv": wv, "wo": wo,
            "wg_sl": np.ascontiguousarray(wg_p[core * DS:(core + 1) * DS]),
            "wu_sl": np.ascontiguousarray(wu_p[core * DS:(core + 1) * DS]),
            "wd_sl": np.ascontiguousarray(wd_p[:, core * DS:(core + 1) * DS]),
            "idf": idf, "idb": idb,
        })

    res = run_bass_kernel_spmd(nc, in_maps, core_ids=list(range(8)), **RUNKW)
    globals()["LASTRES"] = res
    out = np.empty((4, 2048, D), np.float32)
    for core in range(8):
        b, half = core // 2, core % 2
        out[b, half * T:(half + 1) * T] = res.results[core]["out"]
    return out


# version 09 (final)
# speedup vs baseline: 1.0335x; 1.0049x over previous
import numpy as np
import ml_dtypes
import concourse.bass as bass
import concourse.mybir as mybir
from concourse.bass_utils import run_bass_kernel_spmd

F32 = mybir.dt.float32
BF16 = mybir.dt.bfloat16
I32 = mybir.dt.int32
AF = mybir.ActivationFunctionType
ALU = mybir.AluOpType
AXX = mybir.AxisListType.X

D = 2048
FC = 16
T = 1024
TC = 8
EXT = 16
KALL = EXT + T
NH = 16
NKV = 8
DFF = 5504
OCR = 43
SLC = 6
DS = SLC * 128
RG = [[0, 1, 2, 3, 4, 5, 6, 7]]
SCALE = float(np.float32(1.0 / np.sqrt(128.0)))
NEG = -1e30
BLKROWS = 3 * 2048 * SLC
G_OFF, U_OFF, D_OFF = 0, 12288, 24576


class Sem:
    def __init__(self, nc, name):
        self.h = nc.alloc_semaphore(name)
        self.n = 0

    def inc(self, instr, k=1):
        instr.then_inc(self.h, k)
        self.n += k
        return instr

    def dma(self, instr):
        return self.inc(instr, 16)

    def wait(self, eng, val=None):
        eng.wait_ge(self.h, self.n if val is None else val)


class Rot:
    _pool = {}

    def __new__(cls, nc, name, tiles):
        if name in cls._pool:
            obj = cls._pool[name]
            assert obj.pi == obj.ci, f"rot {name} rebound while in flight"
            obj.tiles = tiles
            return obj
        obj = super().__new__(cls)
        cls._pool[name] = obj
        obj.tiles = tiles
        obj.prod = Sem(nc, name + "_p")
        obj.free = Sem(nc, name + "_f")
        obj.pi = 0
        obj.ci = 0
        return obj

    def __init__(self, nc, name, tiles):
        pass

    def acquire(self, eng):
        if self.pi >= len(self.tiles):
            eng.wait_ge(self.free.h, self.pi - len(self.tiles) + 1)
        return self.tiles[self.pi % len(self.tiles)]

    def commit(self, instr):
        self.prod.inc(instr)
        self.pi += 1
        return instr

    def take(self, eng):
        eng.wait_ge(self.prod.h, self.ci + 1)
        t = self.tiles[self.ci % len(self.tiles)]
        self.ci += 1
        return t

    def release(self, instr):
        self.free.inc(instr)
        return instr



class SafeEng:

    _READS = {
        "tensor_copy": (1,), "tensor_tensor": (1, 2), "tensor_add": (1, 2),
        "tensor_sub": (1, 2), "tensor_mul": (1, 2), "tensor_max": (1, 2),
        "tensor_scalar": (1, 2, 3), "tensor_single_scalar": (1, 2),
        "tensor_scalar_add": (1, 2), "tensor_scalar_sub": (1, 2),
        "tensor_scalar_mul": (1, 2), "tensor_scalar_max": (1, 2),
        "tensor_scalar_min": (1, 2), "tensor_reduce": (1,),
        "reduce_max": (1,), "reduce_sum": (1,), "reciprocal": (1,),
        "scalar_tensor_tensor": (1, 2, 3), "select": (1, 2, 3),
        "activation": (1,), "copy": (1,), "mul": (1,), "add": (1,),
        "sign": (1,), "square": (1,), "sqrt": (1,),
    }

    DEPTH = 1

    def __init__(self, eng, dumfn):
        self._e = eng
        self._dumfn = dumfn
        self._w = [set() for _ in range(self.DEPTH)]
        self.n_dum = 0

    @staticmethod
    def _name(x):
        t = getattr(x, "tensor", None)
        return getattr(t, "name", None) if t is not None else None

    def _pre(self, reads):
        rd = {n for n in reads if n}
        guard = 0
        while rd & set().union(*self._w) and guard < 4:
            self._dumfn(self._e)
            self._w = self._w[1:] + [{"__dum__"}]
            self.n_dum += 1
            guard += 1

    def _post(self, writes):
        self._w = self._w[1:] + [{n for n in writes if n}]

    def __getattr__(self, name):
        eng = self._e
        if name in self._READS:
            ridx = self._READS[name]

            def call(*args, **kwargs):
                reads = [self._name(args[i]) for i in ridx if i < len(args)]
                for kw in ("bias", "scale", "scale2"):
                    if kw in kwargs:
                        reads.append(self._name(kwargs[kw]))
                for a in args[1:]:
                    n = self._name(a)
                    if n:
                        reads.append(n)
                writes = [self._name(args[0])]
                ao = kwargs.get("accum_out")
                if ao is not None:
                    writes.append(self._name(ao))
                self._pre(reads)
                ins = getattr(eng, name)(*args, **kwargs)
                self._post(writes)
                return ins
            return call
        if name == "memset":
            def call(*args, **kwargs):
                ins = eng.memset(*args, **kwargs)
                self._post([self._name(args[0])])
                return ins
            return call
        return getattr(eng, name)


class Plc:
    def __init__(self, nc, budget):
        self.nc = nc
        self.budget = budget
        self.items = []

    def add(self, name, shape, dtype, e0, e1):
        per = int(np.prod(shape[1:])) * mybir.dt.size(dtype)
        per = (per + 31) // 32 * 32
        self.items.append([name, shape, dtype, per, e0, e1])

    def _try(self, order):
        placed, offs = [], {}
        for it in order:
            name, shape, dtype, sz, e0, e1 = it
            for off in sorted({0} | {o + s for o, s, *_ in placed}):
                if off + sz > self.budget:
                    continue
                if all(not (off < o + s and o < off + sz and e0 < f1 and f0 < e1)
                       for o, s, f0, f1 in placed):
                    placed.append((off, sz, e0, e1))
                    offs[name] = off
                    break
            else:
                return None, None
        return placed, offs

    def solve(self):
        keys = [lambda x: (-(x[5] - x[4]), -x[3]),
                lambda x: (-x[3], -(x[5] - x[4])),
                lambda x: (-(x[5] - x[4]) * x[3],),
                lambda x: (x[4], -x[3]),
                lambda x: (-x[3] * (x[5] - x[4]), x[4])]
        placed = None
        for k in keys:
            placed, offs = self._try(sorted(self.items, key=k))
            if placed is not None:
                break
        if placed is None:
            raise RuntimeError("SBUF packing failed under all orderings")
        self.span = max(o + s for o, s, *_ in placed)
        base = (self.nc.sbuf_base + 31) // 32 * 32
        self.nc.sbuf_base = base + self.span
        assert self.nc.sbuf_base <= self.nc.sbuf_top - 256, "SBUF overflow"
        return {name: self.nc.alloc_sbuf_tensor_at(name, shape, dtype, offset=base + offs[name])
                for name, shape, dtype, sz, e0, e1 in self.items}


class _Cut(Exception):
    pass


def build_program(taps=()):
    import os as _os
    KSTOP = int(_os.environ.get("KSTOP", "99"))
    Rot._pool = {}
    nc = bass.Bass("TRN2", target_bir_lowering=False, detect_race_conditions=False)

    din = {}
    for name, shape in [
        ("x_main", [T, D]), ("x_ext", [EXT, D]), ("mask", [128, 8 * 144]),
        ("c_all", [4, D]), ("wmod_sl", [D, 1536]),
        ("wq", [D, D]), ("wk", [1024, D]), ("wv", [1024, D]), ("wo", [D, D]),
        ("wg_sl", [DS, D]), ("wu_sl", [DS, D]), ("wd_sl", [D, DS]),
        ("idf", [128, 128]),
    ]:
        din[name] = nc.dram_tensor(name, shape, F32, kind="ExternalInput")
    din["idb"] = nc.dram_tensor("idb", [128, 128], BF16, kind="ExternalInput")
    out_h = nc.dram_tensor("out", [T, D], F32, kind="ExternalOutput")
    tap_h = {tn: nc.dram_tensor("tap_" + tn, ts_, td_, kind="ExternalOutput")
             for tn, ts_, td_ in taps}

    wq_bf = nc.dram_tensor("wq_bf", [D, D], BF16, kind="Internal")
    wk_bf = nc.dram_tensor("wk_bf", [1024, D], BF16, kind="Internal")
    wv_bf = nc.dram_tensor("wv_bf", [1024, D], BF16, kind="Internal")
    wo_bf = nc.dram_tensor("wo_bf", [D, D], BF16, kind="Internal")
    wg_nat = nc.dram_tensor("wg_nat", [DS, D], BF16, kind="Internal")
    wu_nat = nc.dram_tensor("wu_nat", [DS, D], BF16, kind="Internal")
    wd_nat = nc.dram_tensor("wd_nat", [D, DS], BF16, kind="Internal")
    ag_in = nc.dram_tensor("ag_in", [BLKROWS, 128], BF16, kind="Internal")
    ag_out = nc.dram_tensor("ag_out", [8 * BLKROWS, 128], BF16, kind="Internal",
                            addr_space="Shared")
    mod_in = nc.dram_tensor("mod_in", [4, 1536], F32, kind="Internal")
    mod_out = nc.dram_tensor("mod_out", [32, 1536], F32, kind="Internal",
                             addr_space="Shared")
    hid_sp = nc.dram_tensor("hid_sp", [OCR * 128, 1024], BF16, kind="Internal")

    P = Plc(nc, budget=229376 - 1024 - 16896)
    P.add("IDF", [128, 128], F32, 0, 10)
    P.add("IDB", [128, 128], BF16, 0, 10)
    P.add("ONES1", [1, 128], F32, 0, 10)
    P.add("ROWB", [1, 128], F32, 0, 10)
    P.add("MODS6", [16, 6 * 128], F32, 0, 10)
    P.add("MTMP", [16, 256], F32, 0, 10)
    P.add("CAF", [128, 16], F32, 0, 10)
    P.add("SAF", [128, 16], F32, 0, 10)
    P.add("GBF", [128, 16], F32, 0, 10)
    P.add("GAFM", [128, 16], F32, 0, 10)
    P.add("CBFM", [128, 16], F32, 0, 10)
    P.add("SBFM", [128, 16], F32, 0, 10)
    P.add("STAT", [128, 96], F32, 0, 10)
    P.add("X1", [128, TC * D], F32, 0, 2)
    P.add("X2", [128, TC * D], F32, 5, 10)
    P.add("SILC", [4, D], F32, 0, 1)
    P.add("SCF", [128, 64], F32, 0, 1)
    P.add("MODP", [4, 1536], F32, 0, 1)
    P.add("WMOD", [128, 8 * 1536], F32, 0, 1)
    P.add("XEXT", [16, D], F32, 0, 2)
    P.add("DUM", [128, D], F32, 1, 2)
    P.add("U1", [128, 2 * D], BF16, 1, 2)
    P.add("MASK", [128, 8 * 144], F32, 0, 5)
    P.add("H1", [128, FC * KALL], BF16, 1, 4)
    P.add("WST", [128, 16 * 512], BF16, 2, 6)
    P.add("KF", [128, NKV * KALL], BF16, 2, 5)
    P.add("VT", [128, NKV * TC * 128], BF16, 2, 5)
    P.add("VSM", [128, NKV * TC * 128], BF16, 2, 5)
    P.add("VFM", [128, KALL], BF16, 2, 3)
    P.add("VFMX", [128, 7 * 16], BF16, 2, 3)
    P.add("QF", [128, NH * T], BF16, 3, 5)
    P.add("F32Q", [128, 2 * D], F32, 1, 4)
    P.add("QBF", [128, 2 * D], BF16, 1, 4)
    P.add("QSC", [128, 64], F32, 1, 4)
    P.add("QS1", [128, 2 * D], BF16, 1, 4)
    P.add("QS2", [128, 2 * D], BF16, 1, 4)
    P.add("PIECE", [128, 2 * D], BF16, 1, 5)
    P.add("AVF", [128, NH * T], BF16, 4, 6)
    P.add("SSB", [128, 4 * 144], F32, 4, 5)
    P.add("PBF", [128, 2 * 144], BF16, 4, 5)
    P.add("PTAS", [128, 2 * 128], BF16, 4, 5)
    P.add("PTMS", [128, 2 * 128], BF16, 4, 5)
    P.add("TMPW", [128, 1024], F32, 5, 6)
    P.add("GAFULL", [128, D], F32, 5, 6)
    P.add("CBFULL", [128, D], F32, 6, 7)
    P.add("SBFULL", [128, D], F32, 6, 7)
    P.add("U2", [128, D], F32, 6, 7)
    P.add("XQ2I", [128, D], I32, 6, 7)
    P.add("XQ2S", [128, D], BF16, 6, 7)
    P.add("XQ2", [128, FC * T], BF16, 6, 8)
    P.add("BGT", [128, 3 * D], BF16, 7, 8)
    P.add("BUT", [128, 3 * D], BF16, 7, 8)
    P.add("GS", [128, 1024], F32, 7, 8)
    P.add("HIDS", [128, 2 * 1024], F32, 7, 8)
    P.add("RMAX", [128, 1024], F32, 7, 9)
    P.add("RMIN", [128, 1024], F32, 7, 9)
    P.add("S3FULL", [128, 1024], F32, 8, 10)
    P.add("XQ3", [128, OCR * 1024], BF16, 8, 10)
    P.add("XQ3I", [128, 1024], I32, 8, 10)
    P.add("BDT", [128, 6144], BF16, 9, 10)
    P.add("HIDR", [128, 2 * 1024], BF16, 8, 10)
    P.add("YD", [128, 2 * 1024], F32, 9, 10)
    P.add("HBF", [128, 2 * 1024], BF16, 7, 9)
    P.add("HDUM", [128, 4], F32, 0, 10)
    H = P.solve()
    HDUM = H["HDUM"]

    def _vdum(e):
        e.tensor_scalar_mul(HDUM[:, 0:1], HDUM[:, 0:1], 1.0)

    def _sdum(e):
        e.activation(HDUM[:, 2:3], HDUM[:, 2:3], AF.Copy)

    _WV = lambda e: SafeEng(e, _vdum)
    _WS = lambda e: SafeEng(e, _sdum)

    IDF, IDB, ONES1, ROWB, MODS6, MTMP = (H[k] for k in
        ["IDF", "IDB", "ONES1", "ROWB", "MODS6", "MTMP"])
    CAF, SAF, GBF, STAT = H["CAF"], H["SAF"], H["GBF"], H["STAT"]
    X1, X2, XEXT, MASK, H1, WST = H["X1"], H["X2"], H["XEXT"], H["MASK"], H["H1"], H["WST"]
    KF, VT, VSM, VFM, QF, AVF = (H[k] for k in
        ["KF", "VT", "VSM", "VFM", "QF", "AVF"])

    def st(base, i, n=128):
        return STAT[0:n, base + i: base + i + 1]
    SB_SUM, SB_SSQ, SB_M, SB_VAR, SB_R, SB_NMR = 0, 9, 18, 27, 36, 45
    SB_NM, SB_DEN, SB_S3, SB_S3I, SB_AMX = 54, 58, 62, 70, 86

    sems = {}

    def S(name):
        if name not in sems:
            sems[name] = Sem(nc, name)
        return sems[name]

    def _cut(level, block, waitsems):
        if KSTOP != level:
            return
        @block.sync
        def _(q):
            for sname in waitsems:
                S(sname).wait(q)
            for tc in range(TC):
                S("stout").dma(q.dma_start(out_h.ap()[tc * 128:(tc + 1) * 128, :],
                                           H["X1" if KSTOP <= 2 else "X1"][:, tc * D:(tc + 1) * D]
                                           if KSTOP <= 5 else H["X2"][:, tc * D:(tc + 1) * D]))
            S("stout").wait(q)
        raise _Cut()

    try:
      with nc.Block() as block:
        _sid_P0, _ = nc.enter_named_scope("P0", False)
        @block.sync
        def _(q):
            for tc in range(TC):
                S("x").dma(q.dma_start(X1[:, tc * D:(tc + 1) * D],
                                       din["x_main"].ap()[tc * 128:(tc + 1) * 128, :]))
            S("x").dma(q.dma_start(XEXT[:], din["x_ext"].ap()))
            S("misc").dma(q.dma_start(IDF[:], din["idf"].ap()))
            S("misc").dma(q.dma_start(IDB[:], din["idb"].ap()))
            S("misc").dma(q.dma_start(H["SILC"][:], din["c_all"].ap()))
            S("misc").dma(q.dma_start(MASK[:], din["mask"].ap()))
            for kc in range(8):
                S("wm1").dma(q.dma_start(H["WMOD"][:, kc * 1536:(kc + 1) * 1536],
                                         din["wmod_sl"].ap()[kc * 128:(kc + 1) * 128, :]))

        nc.leave_named_scope("P0", _sid_P0, False)
        _sid_P1, _ = nc.enter_named_scope("P1", False)
        with (nc.psum_tensor("pmt0", [128, 16], F32) as pmt0,
              nc.psum_tensor("pmt1", [128, 16], F32) as pmt1,
              nc.psum_tensor("pmA", [4, 512], F32) as pmA,
              nc.psum_tensor("pmB", [4, 512], F32) as pmB,
              nc.psum_tensor("pmC", [4, 512], F32) as pmC):
            r_scf = Rot(nc, "rA", [pmt0, pmt1])

            @block.scalar
            def _(s):
                s = _WS(s)
                S("misc").wait(s)
                S("silc").inc(s.activation(H["SILC"][:], H["SILC"][:], AF.Silu))

            @block.tensor
            def _(t):
                S("silc").wait(t)
                for fc in range(FC):
                    pt = r_scf.acquire(t)
                    r_scf.commit(t.transpose(pt[:, 0:4],
                                             H["SILC"][0:4, fc * 128:(fc + 1) * 128],
                                             IDF[0:4, 0:4]))

            @block.vector
            def _(v):
                v = _WV(v)
                v.memset(ONES1[:], 1.0)
                for fc in range(FC):
                    pt = r_scf.take(v)
                    r_scf.release(v.tensor_copy(H["SCF"][:, fc * 4:fc * 4 + 4], pt[:, 0:4]))
                S("scf").inc(v.engine_nop())

            @block.tensor
            def _(t):
                S("scf").wait(t)
                S("wm1").wait(t)
                for ct, pm in enumerate([pmA, pmB, pmC]):
                    for kc in range(8):
                        ins = t.matmul(pm[:], H["SCF"][:, kc * 4:kc * 4 + 4],
                                       H["WMOD"][:, kc * 1536 + ct * 512: kc * 1536 + (ct + 1) * 512],
                                       start=(kc == 0), stop=False)
                S("mmh1").inc(ins)

            @block.sync
            def _(q):
                S("mmh1").wait(q)
                for kc in range(8):
                    S("wm2").dma(q.dma_start(H["WMOD"][:, kc * 1536:(kc + 1) * 1536],
                                             din["wmod_sl"].ap()[(8 + kc) * 128:(9 + kc) * 128, :]))

            @block.tensor
            def _(t):
                S("wm2").wait(t)
                for ct, pm in enumerate([pmA, pmB, pmC]):
                    for kc in range(8):
                        ins = t.matmul(pm[:], H["SCF"][:, (8 + kc) * 4:(8 + kc) * 4 + 4],
                                       H["WMOD"][:, kc * 1536 + ct * 512: kc * 1536 + (ct + 1) * 512],
                                       start=False, stop=(kc == 7))
                S("modmm").inc(ins)

            @block.vector
            def _(v):
                v = _WV(v)
                S("modmm").wait(v)
                for ct, pm in enumerate([pmA, pmB, pmC]):
                    v.tensor_copy(H["MODP"][:, ct * 512:(ct + 1) * 512], pm[:])
                v.drain()
                S("modp").inc(v.engine_nop())

            @block.sync
            def _(q):
                S("modp").wait(q)
                S("modin").dma(q.dma_start(mod_in.ap(), H["MODP"][:]))

            @block.gpsimd
            def _(g):
                S("cast").dma(g.dma_start(wk_bf.ap(), din["wk"].ap()))
                S("cast").dma(g.dma_start(wv_bf.ap(), din["wv"].ap()))
                S("cast").dma(g.dma_start(wq_bf.ap(), din["wq"].ap()))
                S("cast").dma(g.dma_start(wo_bf.ap(), din["wo"].ap()))
                S("modin").wait(g)
                S("agm").inc(g.collective_compute(
                    "AllGather", ALU.bypass, RG, ins=[mod_in.ap()], outs=[mod_out.ap()]))
                S("agm").wait(g)
                pid = g.partition_id()
                bsel = pid // 2
                for k in range(6):
                    col = k * 2048
                    while col < (k + 1) * 2048:
                        sl = col // 1536
                        end = min((sl + 1) * 1536, (k + 1) * 2048)
                        f0 = (col - k * 2048) // 128
                        nf = (end - col) // 128
                        srcp = mod_out.ap()[bass.ds(4 * sl + bsel, 1),
                                            col - 1536 * sl: end - 1536 * sl]
                        srcp = srcp.rearrange("o (f hh) -> (o f) hh", hh=128)
                        S("modsq").dma(g.dma_start(
                            MODS6[f0:f0 + nf, k * 128:(k + 1) * 128], srcp))
                        col = end

            @block.vector
            def _(v):
                v = _WV(v)
                S("modsq").wait(v)
                v.tensor_scalar_add(MODS6[:, 128:256], MODS6[:, 128:256], 1.0)
                ins = v.tensor_scalar_add(MODS6[:, 512:640], MODS6[:, 512:640], 1.0)
                S("modpre").inc(ins)

            @block.scalar
            def _(s):
                s = _WS(s)
                S("modpre").wait(s)
                s.activation(MTMP[:, 0:128], MODS6[:, 256:384], AF.Abs)
                S("modabs").inc(s.activation(MTMP[:, 128:256], MODS6[:, 640:768], AF.Abs))

            @block.vector
            def _(v):
                v = _WV(v)
                S("modabs").wait(v)
                for cc, tcol in [(256, 0), (640, 128)]:
                    blk = MODS6[:, cc:cc + 128]
                    v.tensor_scalar_max(MTMP[0:16, tcol:tcol + 128], MTMP[0:16, tcol:tcol + 128], 0.25)
                    v.tensor_scalar(blk, blk, 0.0, 2.0, ALU.is_ge, ALU.mult)
                    v.tensor_scalar_add(blk, blk, -1.0)
                    ins = v.tensor_mul(blk, blk, MTMP[0:16, tcol:tcol + 128])
                S("modclamp").inc(ins)

            @block.tensor
            def _(t):
                S("modclamp").wait(t)
                for kvec in [0, 1, 5, 2, 3, 4]:
                    pt = r_scf.acquire(t)
                    r_scf.commit(t.transpose(pt[:], MODS6[:, kvec * 128:(kvec + 1) * 128],
                                             IDF[0:16, 0:16]))

            @block.vector
            def _(v):
                v = _WV(v)
                for dst in [SAF, CAF, GBF, H["GAFM"], H["SBFM"], H["CBFM"]]:
                    pt = r_scf.take(v)
                    r_scf.release(v.tensor_copy(dst[:], pt[:, 0:16]))
                S("modfm").inc(v.engine_nop())

        _cut(1, block, ["modfm"])

        nc.leave_named_scope("P1", _sid_P1, False)
        _sid_P2, _ = nc.enter_named_scope("P2", False)
        def csrc(i):
            return X1[:, i * D:(i + 1) * D] if i < TC else XEXT[:]

        def cn(i):
            return 128 if i < TC else EXT

        with (nc.psum_tensor("ph0", [128, 128], BF16) as ph0,
              nc.psum_tensor("ph1", [128, 128], BF16) as ph1,
              nc.psum_tensor("ph2", [128, 128], BF16) as ph2,
              nc.psum_tensor("ph3", [128, 128], BF16) as ph3):
            r_h1 = Rot(nc, "rB", [ph0, ph1, ph2, ph3])

            @block.scalar
            def _(s):
                s = _WS(s)
                S("x").wait(s)
                S("modin").wait(s)
                for i in range(9):
                    n = cn(i)
                    S("sq").inc(s.activation(H["DUM"][0:n, :], csrc(i)[0:n, :], AF.Square,
                                             accum_out=st(SB_SSQ, i, n)))

            @block.vector
            def _(v):
                v = _WV(v)
                S("x").wait(v)
                for i in range(9):
                    n = cn(i)
                    v.wait_ge(S("sq").h, i + 1)
                    v.reduce_sum(st(SB_SUM, i, n), csrc(i)[0:n, :], axis=AXX)
                    v.tensor_scalar_mul(st(SB_M, i, n), st(SB_SUM, i, n), 1.0 / D)
                    v.tensor_scalar_mul(st(SB_VAR, i, n), st(SB_SSQ, i, n), 1.0 / D)
                    v.tensor_mul(st(SB_SUM, i, n), st(SB_M, i, n), st(SB_M, i, n))
                    v.tensor_sub(st(SB_VAR, i, n), st(SB_VAR, i, n), st(SB_SUM, i, n))
                    S("var").inc(v.tensor_scalar_add(st(SB_VAR, i, n), st(SB_VAR, i, n), 1e-5))

            @block.scalar
            def _(s):
                s = _WS(s)
                for i in range(9):
                    n = cn(i)
                    s.wait_ge(S("var").h, i + 1)
                    S("sdev").inc(s.activation(st(SB_VAR, i, n), st(SB_VAR, i, n), AF.Sqrt))

            h1t0 = r_h1.prod.n

            @block.vector
            def _(v):
                v = _WV(v)
                for i in range(9):
                    n = cn(i)
                    v.wait_ge(S("sdev").h, i + 1)
                    v.reciprocal(st(SB_R, i, n), st(SB_VAR, i, n))
                    v.tensor_mul(st(SB_NMR, i, n), st(SB_M, i, n), st(SB_R, i, n))
                    v.tensor_scalar_mul(st(SB_NMR, i, n), st(SB_NMR, i, n), -1.0)
                    if i >= 2:
                        v.wait_ge(r_h1.prod.h, h1t0 + (i - 1) * FC)
                    ub = H["U1"][:, (i % 2) * D:(i % 2 + 1) * D]
                    S("u1").inc(v.tensor_scalar(ub[0:n, :], csrc(i)[0:n, :],
                                                st(SB_R, i, n), st(SB_NMR, i, n),
                                                ALU.mult, ALU.add))

            @block.tensor
            def _(t):
                for i in range(9):
                    n = cn(i)
                    t.wait_ge(S("u1").h, i + 1)
                    ub = H["U1"][:, (i % 2) * D:(i % 2 + 1) * D]
                    for fc in range(FC):
                        pt = r_h1.acquire(t)
                        ins = t.transpose(pt[:, 0:n], ub[0:n, fc * 128:(fc + 1) * 128],
                                          IDB[0:n, 0:n])
                        r_h1.commit(ins)

            @block.scalar
            def _(s):
                s = _WS(s)
                S("modfm").wait(s)
                for i in range(9):
                    n = cn(i)
                    base = 16 + i * 128 if i < TC else 0
                    for fc in range(FC):
                        pt = r_h1.take(s)
                        ins = s.activation(H1[:, fc * KALL + base: fc * KALL + base + n],
                                           pt[:, 0:n], AF.Identity,
                                           bias=SAF[:, fc:fc + 1], scale=CAF[:, fc:fc + 1])
                        r_h1.release(ins)
                S("h1done").inc(s.nop())

        _cut(2, block, ["h1done"])

        nc.leave_named_scope("P2", _sid_P2, False)
        _sid_P3a, _ = nc.enter_named_scope("P3a", False)
        qjobs = []
        for hsrc, hnat, nrc, ncol in [(din["wg_sl"], wg_nat, 6, D),
                                      (din["wu_sl"], wu_nat, 6, D),
                                      (din["wd_sl"], wd_nat, 16, DS)]:
            for rc in range(nrc):
                qjobs.append((hsrc, hnat, rc, ncol))
        NQJ = len(qjobs)

        QUANT_ON = KSTOP >= 7

        @block.gpsimd
        def _(g):
            if not QUANT_ON:
                return
            hsrc0, hnat0, rc0, ncol0 = qjobs[0]
            S("qload").dma(g.dma_start(H["F32Q"][:, 0:ncol0],
                                       hsrc0.ap()[rc0 * 128:(rc0 + 1) * 128, :]))
            for j, (hsrc, hnat, rc, ncol) in enumerate(qjobs):
                par = j % 2
                if j + 1 < NQJ:
                    hs2, hn2, rc2, nc2 = qjobs[j + 1]
                    if j >= 1:
                        g.wait_ge(S("vpn").h, j)
                    S("qload").dma(g.dma_start(
                        H["F32Q"][:, ((j + 1) % 2) * D: ((j + 1) % 2) * D + nc2],
                        hs2.ap()[rc2 * 128:(rc2 + 1) * 128, :]))
                g.wait_ge(S("vpn").h, j + 1)
                g.tensor_tensor(H["QS1"][:, par * D: par * D + ncol],
                                H["QS1"][:, par * D: par * D + ncol],
                                H["QS2"][:, par * D: par * D + ncol], ALU.subtract)
                S("gptern").inc(g.engine_nop())
                g.wait_ge(S("swq").h, j + 1)
                S("qwr").dma(g.dma_start(hnat.ap()[rc * 128:(rc + 1) * 128, :],
                                         H["QBF"][:, par * D: par * D + ncol]))

        def emit_vq(lo, hi):
            if not QUANT_ON:
                return
            @block.vector
            def _(v):
                v = _WV(v)
                for j in range(lo, hi):
                    hsrc, hnat, rc, ncol = qjobs[j]
                    ng = ncol // 128
                    par = j % 2
                    v.wait_ge(S("qload").h, (j + 1) * 16)
                    if j >= 2:
                        v.wait_ge(S("swq").h, j - 1)
                    scb = par * 16
                    for gc in range(ng):
                        wsl = H["F32Q"][:, par * D + gc * 128: par * D + (gc + 1) * 128]
                        sc = H["QSC"][:, scb + gc:scb + gc + 1]
                        hs = H["QSC"][:, 32:33]
                        nhs = H["QSC"][:, 33:34]
                        v.reduce_sum(sc, wsl, axis=AXX, apply_absolute_value=True)
                        v.tensor_scalar(sc, sc, 1.0 / 128.0, 1e-5, ALU.mult, ALU.add)
                        v.tensor_scalar_mul(hs, sc, 0.5)
                        v.tensor_scalar_mul(nhs, sc, -0.5)
                        v.tensor_scalar(H["QS1"][:, par * D + gc * 128: par * D + (gc + 1) * 128],
                                        wsl, hs, None, ALU.is_gt)
                        v.tensor_scalar(H["QS2"][:, par * D + gc * 128: par * D + (gc + 1) * 128],
                                        wsl, nhs, None, ALU.is_lt)
                    S("vpn").inc(v.engine_nop())

        def emit_swq(lo, hi):
            if not QUANT_ON:
                return
            @block.scalar
            def _(s):
                s = _WS(s)
                for j in range(lo, hi):
                    hsrc, hnat, rc, ncol = qjobs[j]
                    ng = ncol // 128
                    par = j % 2
                    s.wait_ge(S("gptern").h, j + 1)
                    if j >= 2:
                        s.wait_ge(S("qwr").h, (j - 1) * 16)
                    scb = (j % 2) * 16
                    for gc in range(ng):
                        s.activation(H["QBF"][:, par * D + gc * 128: par * D + (gc + 1) * 128],
                                     H["QS1"][:, par * D + gc * 128: par * D + (gc + 1) * 128],
                                     AF.Copy, scale=H["QSC"][:, scb + gc:scb + gc + 1])
                    s.drain()
                    S("swq").inc(s.nop())

        emit_vq(0, 9)
        emit_swq(0, 8)

        nc.leave_named_scope("P3a", _sid_P3a, False)
        _sid_P4, _ = nc.enter_named_scope("P4", False)
        with (nc.psum_tensor("pkv", [128, KALL], F32) as pkv,
              nc.psum_tensor("ptva", [128, 128], BF16) as ptva,
              nc.psum_tensor("ptvb", [128, 128], BF16) as ptvb,
              nc.psum_tensor("ptvs0", [16, 128], BF16) as ptvs0,
              nc.psum_tensor("ptvs1", [16, 128], BF16) as ptvs1):
            r_vm = Rot(nc, "rC", [ptva, ptvb])
            r_vs = Rot(nc, "rD", [ptvs0, ptvs1])

            @block.sync
            def _(q):
                q.wait_ge(S("u1").h, 9)
                q.wait_ge(S("cast").h, 32)
                for half in range(2):
                    if half > 0:
                        q.wait_ge(S("vmm").h, 4)
                    for fc in range(FC):
                        S("wstk").dma(q.dma_start(
                            WST[:, fc * 512:(fc + 1) * 512],
                            wk_bf.ap()[half * 512:(half + 1) * 512, fc * 128:(fc + 1) * 128],
                            transpose=True))
                    q.wait_ge(S("kmm").h, 4 * (half + 1))
                    for fc in range(FC):
                        S("wstv").dma(q.dma_start(
                            WST[:, fc * 512:(fc + 1) * 512],
                            wv_bf.ap()[half * 512:(half + 1) * 512, fc * 128:(fc + 1) * 128],
                            transpose=True))

            @block.tensor
            def _(t):
                S("h1done").wait(t)
                for half in range(2):
                    t.wait_ge(S("wstk").h, (half + 1) * 16 * 16)
                    for gl in range(4):
                        g8 = half * 4 + gl
                        if g8 > 0:
                            t.wait_ge(S("kev").h, g8)
                        for fc in range(FC):
                            lhs = WST[:, fc * 512 + gl * 128: fc * 512 + (gl + 1) * 128]
                            st_ = (fc == 0)
                            t.matmul(pkv[:, 0:512], lhs, H1[:, fc * KALL: fc * KALL + 512],
                                     start=st_, stop=(fc == FC - 1))
                            t.matmul(pkv[:, 512:1024], lhs,
                                     H1[:, fc * KALL + 512: fc * KALL + 1024],
                                     start=st_, stop=(fc == FC - 1))
                            ins = t.matmul(pkv[:, 1024:1040], lhs,
                                           H1[:, fc * KALL + 1024: fc * KALL + 1040],
                                           start=st_, stop=(fc == FC - 1))
                        S("kmm").inc(ins)
                    t.wait_ge(S("wstv").h, (half + 1) * 16 * 16)
                    for gl in range(4):
                        g8 = half * 4 + gl
                        t.wait_ge(S("kev").h, 4 * (half + 1))
                        if g8 > 0:
                            t.wait_ge(S("vfm").h, g8)
                        for fc in range(FC):
                            lhs = WST[:, fc * 512 + gl * 128: fc * 512 + (gl + 1) * 128]
                            st_ = (fc == 0)
                            t.matmul(pkv[:, 0:512], lhs, H1[:, fc * KALL: fc * KALL + 512],
                                     start=st_, stop=(fc == FC - 1))
                            t.matmul(pkv[:, 512:1024], lhs,
                                     H1[:, fc * KALL + 512: fc * KALL + 1024],
                                     start=st_, stop=(fc == FC - 1))
                            ins = t.matmul(pkv[:, 1024:1040], lhs,
                                           H1[:, fc * KALL + 1024: fc * KALL + 1040],
                                           start=st_, stop=(fc == FC - 1))
                        S("vmm").inc(ins)
                        t.wait_ge(S("vfm").h, g8 + 1)
                        pa = r_vs.acquire(t)
                        r_vs.commit(t.transpose(pa[:], VFM[:, 0:16], IDB[:]))
                        t.wait_ge(S("vfmx").h, g8 + 1)
                        for k in range(TC):
                            pm = r_vm.acquire(t)
                            r_vm.commit(t.transpose(pm[:], VFM[:, 16 + k * 128:16 + (k + 1) * 128], IDB[:]))
                            if k < TC - 1:
                                ps = r_vs.acquire(t)
                                r_vs.commit(t.transpose(
                                    ps[:], H["VFMX"][:, k * 16:(k + 1) * 16], IDB[:]))

            @block.scalar
            def _(s):
                s = _WS(s)
                for half in range(2):
                    for gl in range(4):
                        g8 = half * 4 + gl
                        s.wait_ge(S("kmm").h, g8 + 1)
                        S("kev").inc(s.activation(KF[:, g8 * KALL:(g8 + 1) * KALL], pkv[:],
                                                  AF.Copy, scale=SCALE))
                    for gl in range(4):
                        g8 = half * 4 + gl
                        s.wait_ge(S("vmm").h, g8 + 1)
                        S("vfm").inc(s.activation(VFM[:], pkv[:], AF.Copy))

            emit_swq(8, 18)

            @block.vector
            def _(v):
                v = _WV(v)
                S("h1done").wait(v)
                v.memset(VSM[:], 0.0)
                for g8 in range(NKV):
                    v.wait_ge(S("vfm").h, g8 + 1)
                    for k in range(TC - 1):
                        v.tensor_copy(H["VFMX"][:, k * 16:k * 16 + 8], VFM[:, 0:8])
                        v.tensor_copy(H["VFMX"][:, k * 16 + 8:(k + 1) * 16],
                                      VFM[:, 16 + k * 128 + 120:16 + (k + 1) * 128])
                    S("vfmx").inc(v.engine_nop())
                    pa = r_vs.take(v)
                    r_vs.release(v.tensor_copy(VSM[0:16, g8 * TC * 128:(g8 * TC + 1) * 128], pa[:]))
                    for k in range(TC):
                        pm = r_vm.take(v)
                        r_vm.release(v.tensor_copy(
                            VT[:, (g8 * TC + k) * 128:(g8 * TC + k + 1) * 128], pm[:]))
                        if k < TC - 1:
                            ps = r_vs.take(v)
                            r_vs.release(v.tensor_copy(
                                VSM[0:16, (g8 * TC + k + 1) * 128:(g8 * TC + k + 2) * 128], ps[:]))
                    S("vev").inc(v.engine_nop())

            emit_vq(9, NQJ)

        _cut(3, block, ["vev"])

        nc.leave_named_scope("P4", _sid_P4, False)
        _sid_P5, _ = nc.enter_named_scope("P5", False)
        with (nc.psum_tensor("pq0", [128, 1024], F32) as pq0,
              nc.psum_tensor("pq1", [128, 1024], F32) as pq1):
            r_q = Rot(nc, "rE", [pq0, pq1])
            qmm0 = r_q.prod.n

            @block.sync
            def _(q):
                q.wait_ge(S("cast").h, 48)
                q.wait_ge(S("vmm").h, 8)
                for sw in range(8):
                    if sw > 0:
                        q.wait_ge(r_q.prod.h, qmm0 + 2 * sw)
                    for fc in range(FC):
                        S("wstq").dma(q.dma_start(
                            WST[:, fc * 256:(fc + 1) * 256],
                            wq_bf.ap()[sw * 256:(sw + 1) * 256, fc * 128:(fc + 1) * 128],
                            transpose=True))

            @block.tensor
            def _(t):
                t.wait_ge(S("vev").h, NKV)
                for sw in range(8):
                    t.wait_ge(S("wstq").h, (sw + 1) * 16 * 16)
                    for h2 in range(2):
                        h = sw * 2 + h2
                        pq = r_q.acquire(t)
                        for fc in range(FC):
                            lhs = WST[:, fc * 256 + h2 * 128: fc * 256 + (h2 + 1) * 128]
                            st_ = (fc == 0)
                            t.matmul(pq[:, 0:512], lhs, H1[:, fc * KALL + 16: fc * KALL + 528],
                                     start=st_, stop=(fc == FC - 1))
                            ins = t.matmul(pq[:, 512:1024], lhs,
                                           H1[:, fc * KALL + 528: fc * KALL + 1040],
                                           start=st_, stop=(fc == FC - 1))
                        r_q.commit(ins)

            @block.scalar
            def _(s):
                s = _WS(s)
                for h in range(NH):
                    pq = r_q.take(s)
                    ins = s.activation(QF[:, h * T:(h + 1) * T], pq[:], AF.Copy)
                    r_q.release(ins)
                S("qall").inc(s.nop())

            emit_swq(18, NQJ)

        _cut(4, block, ["qall"])

        nc.leave_named_scope("P5", _sid_P5, False)
        _sid_P3b, _ = nc.enter_named_scope("P3b", False)
        rjobs = [("gu", wg_nat, ocl, 0) for ocl in range(SLC)] + \
                [("gu", wu_nat, ocl, 1) for ocl in range(SLC)] + \
                [("d", wd_nat, oc16, 2) for oc16 in range(16)]
        qwr_need = [6 * 16, 12 * 16, 28 * 16]

        @block.sync
        def _(q):
            if not QUANT_ON:
                return
            cum = 0
            for j, (kind, hnat, idx, wi) in enumerate(rjobs):
                par = j % 2
                if j >= 2:
                    q.wait_ge(S("rord").h, (j - 1) * 16)
                q.wait_ge(S("qwr").h, qwr_need[wi])
                npc = 16 if kind == "gu" else SLC
                for pc in range(npc):
                    S("tpiece").dma(q.dma_start(
                        H["PIECE"][:, par * D + pc * 128: par * D + (pc + 1) * 128],
                        hnat.ap()[idx * 128:(idx + 1) * 128, pc * 128:(pc + 1) * 128],
                        transpose=True))
                cum += npc

        @block.scalar
        def _(s):
            s = _WS(s)
            if not QUANT_ON:
                return
            cum = 0
            for j, (kind, hnat, idx, wi) in enumerate(rjobs):
                par = j % 2
                npc = 16 if kind == "gu" else SLC
                cum += npc
                s.wait_ge(S("tpiece").h, cum * 16)
                if kind == "gu":
                    base = (G_OFF if wi == 0 else U_OFF) + idx * 2048
                    dst = ag_in.ap()[base:base + 2048, :].rearrange("(p a) j -> p (a j)", p=128)
                    src = H["PIECE"][:, par * D: par * D + 2048]
                else:
                    base = D_OFF + idx * DS
                    dst = ag_in.ap()[base:base + DS, :].rearrange("(p a) j -> p (a j)", p=128)
                    src = H["PIECE"][:, par * D: par * D + DS]
                S("rord").dma(s.dma_start(dst, src))

        @block.gpsimd
        def _(g):
            if not QUANT_ON:
                return
            S("rord").wait(g)
            S("agf").inc(g.collective_compute(
                "AllGather", ALU.bypass, RG, ins=[ag_in.ap()], outs=[ag_out.ap()]))

        nc.leave_named_scope("P3b", _sid_P3b, False)
        _sid_P6, _ = nc.enter_named_scope("P6", False)
        tiles = [(h, tt) for h in range(NH) for tt in range(TC)]
        NTI = len(tiles)

        with (nc.psum_tensor("psc0", [128, 144], F32) as psc0,
              nc.psum_tensor("psc1", [128, 144], F32) as psc1,
              nc.psum_tensor("ptab0", [16, 128], BF16) as ptab0,
              nc.psum_tensor("ptab1", [16, 128], BF16) as ptab1,
              nc.psum_tensor("ptm0", [128, 128], BF16) as ptm0,
              nc.psum_tensor("ptm1", [128, 128], BF16) as ptm1,
              nc.psum_tensor("pav0", [128, 128], F32) as pav0,
              nc.psum_tensor("pav1", [128, 128], F32) as pav1):
            r_s = Rot(nc, "rF", [psc0, psc1])
            r_ab = Rot(nc, "rA", [ptab0, ptab1])
            r_m = Rot(nc, "rB", [ptm0, ptm1])
            r_av = Rot(nc, "rC", [pav0, pav1])
            av0 = r_av.prod.n

            def emit_pt(t, i):
                par = i % 2
                t.wait_ge(S("pbf").h, i + 1)
                pab = r_ab.acquire(t)
                r_ab.commit(t.transpose(pab[:], H["PBF"][:, par * 144: par * 144 + 16], IDB[:]))
                pm = r_m.acquire(t)
                r_m.commit(t.transpose(pm[:], H["PBF"][:, par * 144 + 16: par * 144 + 144], IDB[:]))

            def emit_av(t, i):
                h, tt = tiles[i]
                g8 = h // 2
                par = i % 2
                t.wait_ge(S("ptev").h, i + 1)
                pav = r_av.acquire(t)
                t.matmul(pav[:], VT[:, (g8 * TC + tt) * 128:(g8 * TC + tt + 1) * 128],
                         H["PTMS"][:, par * 128:(par + 1) * 128], start=True, stop=False)
                ins = t.matmul(pav[:], VSM[:, (g8 * TC + tt) * 128:(g8 * TC + tt + 1) * 128],
                               H["PTAS"][:, par * 128:(par + 1) * 128], start=False, stop=True)
                r_av.commit(ins)

            @block.tensor
            def _(t):
                S("qall").wait(t)
                for i in range(NTI):
                    h, tt = tiles[i]
                    g8 = h // 2
                    ps = r_s.acquire(t)
                    t.matmul(ps[:, 8:144], QF[:, h * T + tt * 128: h * T + (tt + 1) * 128],
                             KF[:, g8 * KALL + 8 + 128 * tt: g8 * KALL + 144 + 128 * tt],
                             start=True, stop=True)
                    ins = t.matmul(ps[:, 0:8], QF[:, h * T + tt * 128: h * T + (tt + 1) * 128],
                                   KF[:, g8 * KALL: g8 * KALL + 8], start=True, stop=True)
                    r_s.commit(ins)
                    if i >= 2:
                        emit_pt(t, i - 2)
                    if i >= 3:
                        emit_av(t, i - 3)
                for i in range(NTI - 2, NTI):
                    emit_pt(t, i)
                for i in range(NTI - 3, NTI):
                    emit_av(t, i)

            def v_pre(v, i):
                h, tt = tiles[i]
                par = i % 2
                if i >= 2:
                    v.wait_ge(S("sexp").h, i - 1)
                ps = r_s.take(v)
                v.tensor_tensor(H["SSB"][:, par * 144:(par + 1) * 144], ps[:],
                                MASK[:, tt * 144:(tt + 1) * 144], ALU.add)
                r_s.release(
                    v.reduce_max(st(SB_NM, par), H["SSB"][:, par * 144:(par + 1) * 144],
                                 axis=AXX, negate=True))
                S("smax").inc(v.engine_nop())

            def v_post(v, i):
                par = i % 2
                v.wait_ge(S("sexp").h, i + 1)
                if i >= 2:
                    v.wait_ge(S("ptev").h, i - 1)
                v.reciprocal(st(SB_DEN, 2 + par), st(SB_DEN, par))
                S("pbf").inc(v.tensor_scalar(H["PBF"][:, par * 144:(par + 1) * 144],
                                             H["SSB"][:, (2 + par) * 144:(3 + par) * 144],
                                             st(SB_DEN, 2 + par), None, ALU.mult))

            def v_ptev(v, i):
                par = i % 2
                if i >= 2:
                    v.wait_ge(r_av.prod.h, av0 + i - 1)
                pab = r_ab.take(v)
                r_ab.release(v.tensor_copy(H["PTAS"][0:16, par * 128:(par + 1) * 128], pab[:]))
                pm = r_m.take(v)
                r_m.release(v.tensor_copy(H["PTMS"][:, par * 128:(par + 1) * 128], pm[:]))
                S("ptev").inc(v.engine_nop())

            def v_avev(v, i):
                h, tt = tiles[i]
                pav = r_av.take(v)
                r_av.release(v.tensor_copy(
                    AVF[:, h * T + tt * 128: h * T + (tt + 1) * 128], pav[:]))
                S("avev").inc(v.engine_nop())

            @block.vector
            def _(v):
                v = _WV(v)
                S("rord").wait(v)
                v.memset(H["PTAS"][:], 0.0)
                for i in range(NTI):
                    v_pre(v, i)
                    if i >= 1:
                        v_post(v, i - 1)
                    if i >= 2:
                        v_ptev(v, i - 2)
                    if i >= 3:
                        v_avev(v, i - 3)
                v_post(v, NTI - 1)
                for i in range(NTI - 2, NTI):
                    v_ptev(v, i)
                for i in range(NTI - 3, NTI):
                    v_avev(v, i)

            @block.scalar
            def _(s):
                s = _WS(s)
                for i in range(NTI):
                    par = i % 2
                    s.wait_ge(S("smax").h, i + 1)
                    if i >= 2:
                        s.wait_ge(S("pbf").h, i - 1)
                    S("sexp").inc(s.activation(H["SSB"][:, (2 + par) * 144:(3 + par) * 144],
                                               H["SSB"][:, par * 144:(par + 1) * 144], AF.Exp,
                                               bias=st(SB_NM, par), accum_out=st(SB_DEN, par)))

        _cut(5, block, ["avev"])

        nc.leave_named_scope("P6", _sid_P6, False)
        _sid_P7, _ = nc.enter_named_scope("P7", False)
        with (nc.psum_tensor("pxo0", [128, 512], F32) as pxo0,
              nc.psum_tensor("pxo1", [128, 512], F32) as pxo1,
              nc.psum_tensor("pga", [128, 128], F32) as pga,
              nc.psum_tensor("prow", [1, 128], F32) as prow):
            r_xo = Rot(nc, "rD", [pxo0, pxo1])
            womm0 = r_xo.prod.n

            @block.sync
            def _(q):
                S("avev").wait(q)
                q.wait_ge(S("cast").h, 64)
                for tc in range(TC):
                    S("xrl").dma(q.dma_start(X2[:, tc * D:(tc + 1) * D],
                                             din["x_main"].ap()[tc * 128:(tc + 1) * 128, :]))
                for q4 in range(4):
                    if q4 > 0:
                        q.wait_ge(r_xo.prod.h, womm0 + q4 * TC)
                    for fc in range(FC):
                        S("wsto").dma(q.dma_start(
                            WST[:, fc * 512:(fc + 1) * 512],
                            wo_bf.ap()[q4 * 512:(q4 + 1) * 512, fc * 128:(fc + 1) * 128],
                            transpose=True))

            @block.tensor
            def _(t):
                S("avev").wait(t)
                for fc in range(FC):
                    if fc > 0:
                        t.wait_ge(S("garow").h, fc)
                    t.transpose(prow[:], H["GAFM"][:, fc:fc + 1], IDF[:]).then_inc(S("gaT").h, 1)
                    S("gaT").n += 1
                    t.wait_ge(S("garowe").h, fc + 1)
                    ins = t.matmul(pga[:], ONES1[:], ROWB[:], start=True, stop=True)
                    S("gamm").inc(ins)

            @block.vector
            def _(v):
                v = _WV(v)
                for fc in range(FC):
                    v.wait_ge(S("gaT").h, fc + 1)
                    ins = v.tensor_copy(ROWB[:], prow[:])
                    S("garowe").inc(ins)
                    v.wait_ge(S("gamm").h, fc + 1)
                    ins = v.tensor_copy(H["GAFULL"][:, fc * 128:(fc + 1) * 128], pga[:])
                    S("garow").inc(ins)
                S("gafull").inc(v.engine_nop())

            @block.tensor
            def _(t):
                for q4 in range(4):
                    t.wait_ge(S("wsto").h, (q4 + 1) * 16 * 16)
                    for tc in range(TC):
                        pxo = r_xo.acquire(t)
                        for fc in range(FC):
                            ins = t.matmul(pxo[:], AVF[:, fc * T + tc * 128: fc * T + (tc + 1) * 128],
                                           WST[:, fc * 512:(fc + 1) * 512],
                                           start=(fc == 0), stop=(fc == FC - 1))
                        r_xo.commit(ins)

            @block.vector
            def _(v):
                v = _WV(v)
                S("gafull").wait(v)
                S("xrl").wait(v)
                for q4 in range(4):
                    for tc in range(TC):
                        pxo = r_xo.take(v)
                        dst = X2[:, tc * D + q4 * 512: tc * D + (q4 + 1) * 512]
                        v.tensor_tensor(H["TMPW"][:, 0:512], pxo[:],
                                        H["GAFULL"][:, q4 * 512:(q4 + 1) * 512], ALU.mult)
                        r_xo.release(v.tensor_add(dst, dst, H["TMPW"][:, 0:512]))
                S("x1done").inc(v.engine_nop())

        _cut(6, block, ["x1done"])

        nc.leave_named_scope("P7", _sid_P7, False)
        _sid_P8, _ = nc.enter_named_scope("P8", False)
        with (nc.psum_tensor("pbc", [128, 128], F32) as pbc,
              nc.psum_tensor("prow2", [1, 128], F32) as prow2,
              nc.psum_tensor("px0", [128, 128], BF16) as px0,
              nc.psum_tensor("px1", [128, 128], BF16) as px1,
              nc.psum_tensor("px2", [128, 128], BF16) as px2,
              nc.psum_tensor("px3", [128, 128], BF16) as px3):
            r_x2 = Rot(nc, "rE", [px0, px1, px2, px3])
            xq2t0 = r_x2.prod.n

            @block.tensor
            def _(t):
                S("x1done").wait(t)
                for k2, fm in enumerate([H["SBFM"], H["CBFM"]]):
                    for fc in range(FC):
                        i = k2 * FC + fc
                        if i > 0:
                            t.wait_ge(S("bcrow").h, i)
                        t.transpose(prow2[:], fm[:, fc:fc + 1], IDF[:]).then_inc(S("bcT").h, 1)
                        S("bcT").n += 1
                        t.wait_ge(S("bcrowe").h, i + 1)
                        ins = t.matmul(pbc[:], ONES1[:], ROWB[:], start=True, stop=True)
                        S("bcmm").inc(ins)

            @block.vector
            def _(v):
                v = _WV(v)
                S("x1done").wait(v)
                for k2, dstn in enumerate(["SBFULL", "CBFULL"]):
                    for fc in range(FC):
                        i = k2 * FC + fc
                        v.wait_ge(S("bcT").h, i + 1)
                        ins = v.tensor_copy(ROWB[:], prow2[:])
                        S("bcrowe").inc(ins)
                        v.wait_ge(S("bcmm").h, i + 1)
                        ins = v.tensor_copy(H[dstn][:, fc * 128:(fc + 1) * 128], pbc[:])
                        S("bcrow").inc(ins)
                S("bcfull").inc(v.engine_nop())

            @block.scalar
            def _(s):
                s = _WS(s)
                S("x1done").wait(s)
                for i in range(TC):
                    S("sq2").inc(s.activation(H["U2"][:], X2[:, i * D:(i + 1) * D], AF.Square,
                                              accum_out=st(SB_SSQ, i)))

            @block.tensor
            def _(t):
                for i in range(TC):
                    t.wait_ge(S("xq2s").h, i + 1)
                    for fc in range(FC):
                        px = r_x2.acquire(t)
                        ins = t.transpose(px[:], H["XQ2S"][:, fc * 128:(fc + 1) * 128], IDB[:])
                        r_x2.commit(ins)

            @block.scalar
            def _(s):
                s = _WS(s)
                for i in range(TC):
                    s.wait_ge(S("var2").h, i + 1)
                    S("sdev2").inc(s.activation(st(SB_VAR, i), st(SB_VAR, i), AF.Sqrt))

            @block.vector
            def _(v):
                v = _WV(v)
                S("bcfull").wait(v)

                def evac_chunk(i):
                    for fc in range(FC):
                        px = r_x2.take(v)
                        r_x2.release(v.tensor_copy(
                            H["XQ2"][:, fc * T + i * 128: fc * T + (i + 1) * 128], px[:]))

                for i in range(TC):
                    v.wait_ge(S("sq2").h, i + 1)
                    v.reduce_sum(st(SB_SUM, i), X2[:, i * D:(i + 1) * D], axis=AXX)
                    v.tensor_scalar_mul(st(SB_M, i), st(SB_SUM, i), 1.0 / D)
                    v.tensor_scalar_mul(st(SB_VAR, i), st(SB_SSQ, i), 1.0 / D)
                    v.tensor_mul(st(SB_SUM, i), st(SB_M, i), st(SB_M, i))
                    v.tensor_sub(st(SB_VAR, i), st(SB_VAR, i), st(SB_SUM, i))
                    S("var2").inc(v.tensor_scalar_add(st(SB_VAR, i), st(SB_VAR, i), 1e-5))
                    v.wait_ge(S("sdev2").h, i + 1)
                    v.reciprocal(st(SB_R, i), st(SB_VAR, i))
                    v.tensor_mul(st(SB_NMR, i), st(SB_M, i), st(SB_R, i))
                    v.tensor_scalar_mul(st(SB_NMR, i), st(SB_NMR, i), -1.0)
                    v.tensor_scalar(H["U2"][:], X2[:, i * D:(i + 1) * D],
                                    st(SB_R, i), st(SB_NMR, i), ALU.mult, ALU.add)
                    v.tensor_mul(H["U2"][:], H["U2"][:], H["CBFULL"][:])
                    v.tensor_add(H["U2"][:], H["U2"][:], H["SBFULL"][:])
                    v.reduce_max(st(SB_AMX, 0), H["U2"][:], axis=AXX, apply_absolute_value=True)
                    v.tensor_scalar_max(st(SB_AMX, 0), st(SB_AMX, 0), 1e-5)
                    v.reciprocal(st(SB_AMX, 1), st(SB_AMX, 0))
                    v.tensor_scalar_mul(st(SB_AMX, 1), st(SB_AMX, 1), 127.0)
                    v.tensor_scalar_mul(st(SB_AMX, 2), st(SB_AMX, 0), 1.0 / 127.0)
                    if i >= 1:
                        evac_chunk(i - 1)
                        v.wait_ge(r_x2.prod.h, xq2t0 + i * FC)
                    v.tensor_scalar(H["XQ2I"][:], H["U2"][:], st(SB_AMX, 1), None, ALU.mult)
                    S("xq2s").inc(v.tensor_scalar(H["XQ2S"][:], H["XQ2I"][:], st(SB_AMX, 2),
                                                  None, ALU.mult))
                evac_chunk(TC - 1)
                S("xq2done").inc(v.engine_nop())



        _cut(7, block, ["xq2done"])

        nc.leave_named_scope("P8", _sid_P8, False)
        _sid_P9, _ = nc.enter_named_scope("P9", False)
        with (nc.psum_tensor("pg0", [128, 1024], F32) as pg0,
              nc.psum_tensor("pg1", [128, 1024], F32) as pg1,
              nc.psum_tensor("pu0", [128, 1024], F32) as pu0,
              nc.psum_tensor("pu1", [128, 1024], F32) as pu1):
            r_g = Rot(nc, "rF", [pg0, pg1])
            r_u = Rot(nc, "rA", [pu0, pu1])
            gsil0 = r_g.free.n

            @block.sync
            def _(q):
                S("agf").wait(q)
                S("xq2done").wait(q)
                for oc in range(OCR):
                    par = oc % 3
                    c, ocl = oc // SLC, oc % SLC
                    if oc >= 3:
                        q.wait_ge(S("guev").h, oc - 2)
                    base = c * BLKROWS + G_OFF + ocl * 2048
                    S("bg").dma(q.dma_start(
                        H["BGT"][:, par * D:(par + 1) * D],
                        ag_out.ap()[base:base + 2048, :].rearrange("(p a) j -> p (a j)", p=128)))

            @block.gpsimd
            def _(g):
                S("agf").wait(g)
                S("xq2done").wait(g)
                for oc in range(OCR):
                    par = oc % 3
                    c, ocl = oc // SLC, oc % SLC
                    if oc >= 3:
                        g.wait_ge(S("guev").h, oc - 2)
                    base = c * BLKROWS + U_OFF + ocl * 2048
                    S("bu").dma(g.dma_start(
                        H["BUT"][:, par * D:(par + 1) * D],
                        ag_out.ap()[base:base + 2048, :].rearrange("(p a) j -> p (a j)", p=128)))

            @block.vector
            def _(v):
                v = _WV(v)
                S("xq2done").wait(v)
                v.memset(H["RMAX"][:], 0.0)
                v.memset(H["RMIN"][:], 0.0)
                S("rmax0").inc(v.engine_nop())

            @block.tensor
            def _(t):
                S("rmax0").wait(t)
                for oc in range(OCR):
                    par = oc % 3
                    t.wait_ge(S("bg").h, (oc + 1) * 16)
                    t.wait_ge(S("bu").h, (oc + 1) * 16)
                    pg = r_g.acquire(t)
                    for fc in range(FC):
                        lhs = H["BGT"][:, par * D + fc * 128: par * D + (fc + 1) * 128]
                        st_ = (fc == 0)
                        t.matmul(pg[:, 0:512], lhs, H["XQ2"][:, fc * T: fc * T + 512],
                                 start=st_, stop=(fc == FC - 1))
                        ins = t.matmul(pg[:, 512:1024], lhs,
                                       H["XQ2"][:, fc * T + 512: fc * T + 1024],
                                       start=st_, stop=(fc == FC - 1))
                    r_g.commit(ins)
                    pu = r_u.acquire(t)
                    for fc in range(FC):
                        lhs = H["BUT"][:, par * D + fc * 128: par * D + (fc + 1) * 128]
                        st_ = (fc == 0)
                        t.matmul(pu[:, 0:512], lhs, H["XQ2"][:, fc * T: fc * T + 512],
                                 start=st_, stop=(fc == FC - 1))
                        ins = t.matmul(pu[:, 512:1024], lhs,
                                       H["XQ2"][:, fc * T + 512: fc * T + 1024],
                                       start=st_, stop=(fc == FC - 1))
                    r_u.commit(ins)

            @block.scalar
            def _(s):
                s = _WS(s)

                def spill(oc_s):
                    par2 = oc_s % 2
                    if oc_s >= 2:
                        s.wait_ge(S("spill").h, (oc_s - 1) * 16)
                    s.activation(H["HBF"][:, par2 * 1024:(par2 + 1) * 1024],
                                 H["HIDS"][:, par2 * 1024:(par2 + 1) * 1024], AF.Copy)
                    s.drain()
                    S("spill").dma(s.dma_start(
                        hid_sp.ap()[oc_s * 128:(oc_s + 1) * 128, :],
                        H["HBF"][:, par2 * 1024:(par2 + 1) * 1024]))

                for oc in range(OCR):
                    pg = r_g.take(s)
                    ins = s.activation(H["GS"][:], pg[:], AF.Silu)
                    r_g.release(ins)
                    if oc >= 2:
                        s.wait_ge(S("guev").h, oc - 1)
                        spill(oc - 2)
                for oc in range(OCR - 2, OCR):
                    s.wait_ge(S("guev").h, oc + 1)
                    spill(oc)

            @block.vector
            def _(v):
                v = _WV(v)
                for oc in range(OCR):
                    par = oc % 2
                    v.wait_ge(r_g.free.h, gsil0 + oc + 1)
                    if oc >= 2:
                        v.wait_ge(S("spill").h, (oc - 1) * 16)
                    pu = r_u.take(v)
                    hb = H["HIDS"][:, par * 1024:(par + 1) * 1024]
                    r_u.release(v.tensor_tensor(hb, H["GS"][:], pu[:], ALU.mult))
                    v.tensor_tensor(H["RMAX"][:], H["RMAX"][:], hb, ALU.max)
                    v.tensor_tensor(H["RMIN"][:], H["RMIN"][:], hb, ALU.min)
                    v.drain()
                    S("guev").inc(v.engine_nop())

        _cut(8, block, ["guev", "spill"])

        nc.leave_named_scope("P9", _sid_P9, False)
        _sid_P10, _ = nc.enter_named_scope("P10", False)
        with (nc.psum_tensor("pr0", [128, 128], F32) as pr0,
              nc.psum_tensor("pr1", [128, 128], F32) as pr1,
              nc.psum_tensor("prow3", [1, 128], F32) as prow3,
              nc.psum_tensor("ps3b", [128, 128], F32) as ps3b):
            r_r = Rot(nc, "rB", [pr0, pr1])

            @block.vector
            def _(v):
                v = _WV(v)
                S("guev").wait(v)
                v.tensor_scalar_mul(H["RMIN"][:], H["RMIN"][:], -1.0)
                ins = v.tensor_max(H["RMAX"][:], H["RMAX"][:], H["RMIN"][:])
                S("rmfin").inc(ins)

            @block.tensor
            def _(t):
                S("rmfin").wait(t)
                for tc in range(TC):
                    pr = r_r.acquire(t)
                    r_r.commit(t.transpose(pr[:], H["RMAX"][:, tc * 128:(tc + 1) * 128], IDF[:]))

            @block.vector
            def _(v):
                v = _WV(v)
                S("spill").wait(v)
                for tc in range(TC):
                    pr = r_r.take(v)
                    v.reduce_max(st(SB_S3, tc), pr[:], axis=AXX)
                    r_r.release(v.tensor_scalar_max(st(SB_S3, tc), st(SB_S3, tc), 1e-5))
                    v.reciprocal(st(SB_S3, tc), st(SB_S3, tc))
                    v.tensor_scalar_mul(st(SB_S3, tc), st(SB_S3, tc), 127.0)
                    ins = v.reciprocal(st(SB_S3I, tc), st(SB_S3, tc))
                S("s3").inc(ins)

            @block.tensor
            def _(t):
                S("s3").wait(t)
                for tc in range(TC):
                    if tc > 0:
                        t.wait_ge(S("s3row").h, tc)
                    t.transpose(prow3[:], st(SB_S3, tc), IDF[:]).then_inc(S("s3T").h, 1)
                    S("s3T").n += 1
                    t.wait_ge(S("s3rowe").h, tc + 1)
                    ins = t.matmul(ps3b[:], ONES1[:], ROWB[:], start=True, stop=True)
                    S("s3mm").inc(ins)

            @block.vector
            def _(v):
                v = _WV(v)
                for tc in range(TC):
                    v.wait_ge(S("s3T").h, tc + 1)
                    ins = v.tensor_copy(ROWB[:], prow3[:])
                    S("s3rowe").inc(ins)
                    v.wait_ge(S("s3mm").h, tc + 1)
                    ins = v.tensor_copy(H["S3FULL"][:, tc * 128:(tc + 1) * 128], ps3b[:])
                    S("s3row").inc(ins)
                S("s3f").inc(v.engine_nop())

            @block.sync
            def _(q):
                S("spill").wait(q)
                for oc in range(OCR):
                    par = oc % 2
                    if oc >= 2:
                        q.wait_ge(S("xq3").h, oc - 1)
                    S("hrl").dma(q.dma_start(H["HIDR"][:, par * 1024:(par + 1) * 1024],
                                             hid_sp.ap()[oc * 128:(oc + 1) * 128, :]))

            @block.vector
            def _(v):
                v = _WV(v)
                S("s3f").wait(v)
                for oc in range(OCR):
                    par = oc % 2
                    v.wait_ge(S("hrl").h, (oc + 1) * 16)
                    v.tensor_tensor(H["XQ3I"][:], H["HIDR"][:, par * 1024:(par + 1) * 1024],
                                    H["S3FULL"][:], ALU.mult)
                    S("xq3").inc(v.tensor_copy(H["XQ3"][:, oc * 1024:(oc + 1) * 1024], H["XQ3I"][:]))

        _cut(9, block, ["xq3"])

        nc.leave_named_scope("P10", _sid_P10, False)
        _sid_P11, _ = nc.enter_named_scope("P11", False)
        with (nc.psum_tensor("pd0", [128, 1024], F32) as pd0,
              nc.psum_tensor("pd1", [128, 1024], F32) as pd1,
              nc.psum_tensor("py0", [128, 128], F32) as py0,
              nc.psum_tensor("py1", [128, 128], F32) as py1):
            r_d = Rot(nc, "rC", [pd0, pd1])
            r_y = Rot(nc, "rD", [py0, py1])
            dmm0 = r_d.prod.n
            yd0 = r_d.free.n
            yfree0 = r_y.free.n

            @block.sync
            def _(q):
                q.wait_ge(S("xq3").h, 2)
                for oc16 in range(16):
                    if oc16 >= 1:
                        q.wait_ge(r_d.prod.h, dmm0 + oc16)
                    src = ag_out.ap().rearrange("(c r) j -> c r j", c=8)[
                        0:4, D_OFF + oc16 * DS: D_OFF + (oc16 + 1) * DS, :]
                    src = src.rearrange("c (p l) j -> p c l j", p=128)
                    dst = H["BDT"][:, 0:4 * SLC * 128].rearrange(
                        "p (c l j) -> p c l j", c=4, l=SLC)
                    S("bd").dma(q.dma_start(dst, src))

            @block.gpsimd
            def _(g):
                g.wait_ge(S("xq3").h, 2)
                for oc16 in range(16):
                    if oc16 >= 1:
                        g.wait_ge(r_d.prod.h, dmm0 + oc16)
                    src = ag_out.ap().rearrange("(c r) j -> c r j", c=8)[
                        4:8, D_OFF + oc16 * DS: D_OFF + (oc16 + 1) * DS, :]
                    src = src.rearrange("c (p l) j -> p c l j", p=128)
                    dst = H["BDT"][:, 4 * SLC * 128:8 * SLC * 128].rearrange(
                        "p (c l j) -> p c l j", c=4, l=SLC)
                    S("bd").dma(g.dma_start(dst, src))

            @block.tensor
            def _(t):
                S("xq3").wait(t)
                for oc16 in range(16):
                    par = oc16 % 2
                    t.wait_ge(S("bd").h, (oc16 + 1) * 32)
                    pd = r_d.acquire(t)
                    for fc in range(OCR):
                        lhs = H["BDT"][:, fc * 128:(fc + 1) * 128]
                        st_ = (fc == 0)
                        t.matmul(pd[:, 0:512], lhs, H["XQ3"][:, fc * 1024: fc * 1024 + 512],
                                 start=st_, stop=(fc == OCR - 1))
                        ins = t.matmul(pd[:, 512:1024], lhs,
                                       H["XQ3"][:, fc * 1024 + 512: (fc + 1) * 1024],
                                       start=st_, stop=(fc == OCR - 1))
                    r_d.commit(ins)
                    t.wait_ge(r_d.free.h, yd0 + oc16 + 1)
                    ypar = oc16 % 2
                    for tc in range(TC):
                        py = r_y.acquire(t)
                        r_y.commit(t.transpose(
                            py[:], H["YD"][:, ypar * 1024 + tc * 128: ypar * 1024 + (tc + 1) * 128],
                            IDF[:]))

            @block.scalar
            def _(s):
                s = _WS(s)
                for oc16 in range(16):
                    ypar = oc16 % 2
                    pd = r_d.take(s)
                    if oc16 >= 2:
                        s.wait_ge(r_y.free.h, yfree0 + (oc16 - 1) * TC)
                    ins = s.activation(H["YD"][:, ypar * 1024:(ypar + 1) * 1024], pd[:],
                                       AF.Identity, scale=GBF[:, oc16:oc16 + 1])
                    r_d.release(ins)

            @block.vector
            def _(v):
                v = _WV(v)
                for oc16 in range(16):
                    for tc in range(TC):
                        py = r_y.take(v)
                        dst = X2[:, tc * D + oc16 * 128: tc * D + (oc16 + 1) * 128]
                        ins = v.scalar_tensor_tensor(dst, py[:], st(SB_S3I, tc), dst,
                                                     ALU.mult, ALU.add)
                        r_y.release(ins)
                v.drain()
                S("xfence").inc(v.engine_nop())

            @block.sync
            def _(q):
                q.wait_ge(r_y.free.h, yfree0 + 16 * TC)
                S("xfence").wait(q)
                for tc in range(TC):
                    S("stout").dma(q.dma_start(out_h.ap()[tc * 128:(tc + 1) * 128, :],
                                               X2[:, tc * D:(tc + 1) * D]))
                name2buf = {"h1": "H1", "kf": "KF", "qf": "QF", "avf": "AVF",
                            "xq2": "XQ2", "xq3": "XQ3", "s3full": "S3FULL",
                            "modsq": "MODSQ", "gafull": "GAFULL",
                            "stat": "STAT", "mods6": "MODS6", "hidr": "HIDR",
                            "yd": "YD", "xq3i": "XQ3I", "gs": "GS",
                            "rmax": "RMAX", "sbfull": "SBFULL",
                            "cbfull": "CBFULL", "xq2i": "XQ2I"}
                dram_map = {"wgnat": wg_nat, "wunat": wu_nat, "wdnat": wd_nat,
                            "agout": ag_out, "modout": mod_out, "hidsp": hid_sp}
                for tn, hh in tap_h.items():
                    if tn in name2buf:
                        S("stout").dma(q.dma_start(hh.ap(), H[name2buf[tn]][:]))
                    elif tn in dram_map:
                        S("stout").dma(q.dma_start(hh.ap(), dram_map[tn].ap()))
                S("stout").wait(q)

        nc.leave_named_scope("P11", _sid_P11, False)
    except _Cut:
        pass
    return nc



TAPS = ()
LASTRES = None
RUNKW = {}

def _make_mask(o):
    t = np.arange(8)[:, None, None]
    qi = np.arange(128)[None, :, None]
    col = np.arange(144)[None, None, :]
    i = o + t * 128 + qi
    wstart = o + t * 128 - 8
    jwin = wstart + (col - 8)
    jini = col
    init_ok = (jini < 4) & (jini <= i) & (jini < wstart) & (col < 8)
    win_ok = (jwin >= 0) & (jwin <= i) & ((jwin < 4) | (i - jwin < 8)) & (col >= 8)
    allowed = np.where(col < 8, init_ok, win_ok)
    m = np.where(allowed, np.float32(0.0), np.float32(NEG)).astype(np.float32)
    return np.ascontiguousarray(m.transpose(1, 0, 2).reshape(128, 8 * 144))


def kernel(L0_attn_x, L0_attn_Wq, L0_attn_Wk, L0_attn_Wv, L0_attn_Wo,
           L0_ffn_w_gate, L0_ffn_w_up, L0_ffn_w_down, L0_c, L0_wmod):
    x = np.ascontiguousarray(np.asarray(L0_attn_x, np.float32))
    wq = np.ascontiguousarray(np.asarray(L0_attn_Wq, np.float32))
    wk = np.ascontiguousarray(np.asarray(L0_attn_Wk, np.float32))
    wv = np.ascontiguousarray(np.asarray(L0_attn_Wv, np.float32))
    wo = np.ascontiguousarray(np.asarray(L0_attn_Wo, np.float32))
    wg = np.asarray(L0_ffn_w_gate, np.float32)
    wu = np.asarray(L0_ffn_w_up, np.float32)
    wd = np.asarray(L0_ffn_w_down, np.float32)
    c = np.ascontiguousarray(np.asarray(L0_c, np.float32))
    wmod = np.asarray(L0_wmod, np.float32)

    wg_p = np.zeros((8 * DS, D), np.float32)
    wg_p[:DFF] = wg
    wu_p = np.zeros((8 * DS, D), np.float32)
    wu_p[:DFF] = wu
    wd_p = np.zeros((D, 8 * DS), np.float32)
    wd_p[:, :DFF] = wd

    idf = np.eye(128, dtype=np.float32)
    idb = np.eye(128, dtype=np.float32).astype(ml_dtypes.bfloat16)
    mask0, mask1 = _make_mask(0), _make_mask(T)

    nc = build_program(taps=TAPS)
    in_maps = []
    for core in range(8):
        b, half = core // 2, core % 2
        o = half * T
        if half == 0:
            x_ext = np.ascontiguousarray(np.concatenate([x[b, 0:8], x[b, 8:16]], 0))
        else:
            x_ext = np.ascontiguousarray(np.concatenate([x[b, 0:8], x[b, o - 8:o]], 0))
        in_maps.append({
            "x_main": np.ascontiguousarray(x[b, o:o + T]),
            "x_ext": x_ext,
            "mask": mask0 if half == 0 else mask1,
            "c_all": c,
            "wmod_sl": np.ascontiguousarray(wmod[:, core * 1536:(core + 1) * 1536]),
            "wq": wq, "wk": wk, "wv": wv, "wo": wo,
            "wg_sl": np.ascontiguousarray(wg_p[core * DS:(core + 1) * DS]),
            "wu_sl": np.ascontiguousarray(wu_p[core * DS:(core + 1) * DS]),
            "wd_sl": np.ascontiguousarray(wd_p[:, core * DS:(core + 1) * DS]),
            "idf": idf, "idb": idb,
        })

    res = run_bass_kernel_spmd(nc, in_maps, core_ids=list(range(8)), **RUNKW)
    globals()["LASTRES"] = res
    out = np.empty((4, 2048, D), np.float32)
    for core in range(8):
        b, half = core // 2, core % 2
        out[b, half * T:(half + 1) * T] = res.results[core]["out"]
    return out
